# version 03
# baseline (speedup 1.0000x reference)
import contextlib
import numpy as np
import ml_dtypes
import concourse.bass as bass
import concourse.mybir as mybir
from concourse.bass_utils import run_bass_kernel_spmd
from concourse.alu_op_type import AluOpType

F32 = mybir.dt.float32
BF16 = mybir.dt.bfloat16
FP16 = mybir.dt.float16
FP8 = mybir.dt.float8e4
AF = mybir.ActivationFunctionType
MUL = AluOpType.mult
ADD = AluOpType.add
SUB = AluOpType.subtract

D = 1024
L = 2048
DIN = 2048
DS = 16
H = 4096
NH = 16
DH = 64
NFT = D // 128
NDG = DIN // 128
TCM = 512
NCM = L // TCM
EPS = 1e-5

DEBUG = False
STAB_A = True
STAB_B = True
STAB_RED = True


def build(debug=False):
    nc = bass.Bass("TRN2", target_bir_lowering=False)

    def din(name, shape, dt=F32):
        return nc.dram_tensor(name, list(shape), dt, kind="ExternalInput").ap()

    X = din("x", [L, D])
    IPW = din("ipw", [32, 128, 1024])
    OPW = din("opw", [8, 128, 2048])
    XPW = din("xpw", [128, NDG * 96])
    DTW = din("dtw", [64, DIN])
    CONVW = din("convw", [128, NDG * 4])
    CONVB = din("convb", [128, NDG])
    DTB = din("dtb", [128, NDG])
    ALOG = din("alog", [128, NDG * DS])
    DDD = din("dd", [128, NDG])
    RMSW = din("rmsw", [128, NFT])
    INDV = din("indv", [96, 32], BF16)
    W1T = din("w1t", [32, 128, 1024])
    B1 = din("b1", [128, 32])
    W2T = din("w2t", [8, 128, 4096])
    B2 = din("b2", [128, NFT])
    WQKT = din("wqkt", [16, 128, 1024])
    WVR = din("wvr", [8, 128, 1024])
    BQ = din("bq", [128, NFT])
    BK = din("bk", [128, NFT])
    BV = din("bv", [1, D])
    WOUTT = din("woutt", [8, 128, 1024])
    BOUT = din("bout", [128, NFT])
    L1WT = din("l1wt", [8, 128, 1024])
    IDF = din("idf", [128, 128], F32)
    IDB = din("idb", [128, 128], BF16)
    OUT = nc.dram_tensor("out", [L, D], F32, kind="ExternalOutput").ap()

    dbg = {}
    if debug:
        def dout(name, shape, dt=F32):
            dbg[name] = nc.dram_tensor(name, list(shape), dt, kind="ExternalOutput").ap()
        dout("d_xnb", [128, NFT * TCM])
        dout("d_y2", [128, NDG * TCM])
        dout("d_x1", [128, NFT * L])
        dout("d_x2", [128, NFT * L])
        dout("d_x3", [128, NFT * L])
        dout("d_q", [128, 8 * L])
        dout("d_vt", [128, 16 * 1040])
        dout("d_ao", [128, 8 * L])
        dout("d2_xnb", [128, NFT * TCM])
        dout("d2_stat", [96, TCM])
        dout("d2_xib", [128, NDG * (TCM + 3)])
        dout("d2_dbc", [96, TCM])
        dout("d2_y2", [128, NDG * TCM])
        dout("d2_of", [128, NFT * TCM])
        dout("d_ofa", [128, NFT * TCM])
        dout("d_den", [96, 1024])

    es = contextlib.ExitStack()

    def sb(name, shape, dt):
        return es.enter_context(nc.sbuf_tensor(name, list(shape), dt))

    def psum(name, shape, dt=F32):
        return es.enter_context(nc.psum_tensor(name, list(shape), dt))

    xfm = sb("xfm", [128, NFT * L], F32)
    consts = sb("consts", [128, 12], F32)
    onesf = sb("onesf", [1, 128], F32)
    onesb = sb("onesb", [1, 128], BF16)
    onescf = sb("onescf", [128, 1], F32)
    onescb = sb("onescb", [128, 1], BF16)
    onesch = sb("onesch", [128, 1], FP16)
    onesA = sb("onesA", [96, 128], F32)
    idf_sb = sb("idf_sb", [128, 128], F32)
    idb_sb = sb("idb_sb", [128, 128], BF16)
    statrA = sb("statrA", [96, TCM], F32)
    statrB = sb("statrB", [96, TCM], F32)
    statrC = sb("statrC", [96, TCM], F32)
    ipw_m = sb("ipw_m", [128, 2 * 1024], BF16)
    sq_sb = sb("sq_sb", [128, TCM], F32)
    xio = sb("xio", [128, D], F32)

    ps_mm = psum("ps_mm", [128, 2 * TCM])
    ps_o2 = psum("ps_o2", [128, 2 * TCM])
    ps_y = psum("ps_y", [128, TCM])
    ps_bc = psum("ps_bc", [128, TCM])
    ps_dt = psum("ps_dt", [128, TCM])
    ps_sx = psum("ps_sx", [128, TCM])
    ps_s1 = ps_bc[0:1, :]
    ps_s2 = ps_y[0:1, :]
    ps_dbcv = ps_sx[0:96, :]

    es_m = contextlib.ExitStack()

    def sbm(name, shape, dt):
        return es_m.enter_context(nc.sbuf_tensor(name, list(shape), dt))

    A_sb = sbm("A_sb", [128, NDG * DS], F32)
    convw_sb = sbm("convw_sb", [128, NDG * 4], F32)
    convb_sb = sbm("convb_sb", [128, NDG], F32)
    dtb_sb = sbm("dtb_sb", [128, NDG], F32)
    D_sb = sbm("D_sb", [128, NDG], F32)
    rmsw_sb = sbm("rmsw_sb", [128, NFT], F32)
    xpw_sb = sbm("xpw_sb", [128, NDG * 96], BF16)
    dtw_sb = sbm("dtw_sb", [64, DIN], BF16)
    xnbc = sbm("xnbc", [128, NFT * TCM], BF16)
    opw_m = sbm("opw_m", [128, 2 * 2048], BF16)
    xib = sbm("xib", [128, NDG * (TCM + 3)], BF16)
    xtail = sbm("xtail", [128, NDG * 3], F32)
    zsb = sbm("zsb", [128, NDG * TCM], BF16)
    Brep = sbm("BrepT", [128, DS * TCM], BF16)
    Crep = sbm("CrepT", [128, DS * TCM], BF16)
    y2b = sbm("y2b", [128, NDG * TCM], BF16)
    dbc_sb = sbm("dbc_sb", [96, TCM], BF16)
    delta_sb = sbm("delta_sb", [128, TCM], F32)
    u_sb = sbm("u_sb", [128, TCM], BF16)
    a_sb = sbm("a_sb", [128, 2 * TCM], F32)
    b_sb = sbm("b_sb", [128, 2 * TCM], BF16)
    h_sb = sbm("h_sb", [128, 2 * TCM], BF16)
    hc_sb = sbm("hc_sb", [128, 2 * TCM], BF16)
    h0_sb = sbm("h0_sb", [128, NDG * DS], F32)
    rrep_sb = sbm("rrep_sb", [128, TCM], F32)
    mrep_sb = sbm("mrep_sb", [128, TCM], F32)
    indv_sb = sbm("indv_sb", [96, 32], BF16)
    of16 = xnbc

    blk = es.enter_context(nc.Block())
    sems = {}
    C = {}
    for s in ["SD", "GD", "TE", "VE", "SE"]:
        sems[s] = es.enter_context(nc.semaphore(s))
        C[s] = 0

    def inc(instr, s, n=1):
        instr.then_inc(sems[s], n)
        C[s] += n
        return C[s]

    @blk.sync
    def _(e):
        e.dma_start(out=A_sb[:], in_=ALOG[:]).then_inc(sems["SD"], 16)
        e.dma_start(out=convw_sb[:], in_=CONVW[:]).then_inc(sems["SD"], 16)
        e.dma_start(out=convb_sb[:], in_=CONVB[:]).then_inc(sems["SD"], 16)
        e.dma_start(out=dtb_sb[:], in_=DTB[:]).then_inc(sems["SD"], 16)
        e.dma_start(out=D_sb[:], in_=DDD[:]).then_inc(sems["SD"], 16)
        e.dma_start(out=rmsw_sb[:], in_=RMSW[:]).then_inc(sems["SD"], 16)
        e.dma_start(out=idf_sb[:], in_=IDF[:]).then_inc(sems["SD"], 16)
        e.dma_start(out=idb_sb[:], in_=IDB[:]).then_inc(sems["SD"], 16)
        e.dma_start(out=indv_sb[:], in_=INDV[:]).then_inc(sems["SD"], 16)
    C["SD"] += 16 * 9
    sd_init = C["SD"]

    @blk.gpsimd
    def _(e):
        e.dma_start(out=xpw_sb[:], in_=XPW[:]).then_inc(sems["GD"], 16)
        e.dma_start(out=dtw_sb[:], in_=DTW[:]).then_inc(sems["GD"], 16)
    C["GD"] += 32
    gd_init = C["GD"]

    @blk.vector
    def _(e):
        for i, v in enumerate([1.0, -1.0, 0.125, 1.0 / 1024.0, EPS, -0.5, 0.0, 2.0,
                               -44.0, 2.0, 88.0, 0.5]):
            e.memset(consts[:, i:i + 1], v)
        e.memset(onesf[:], 1.0)
        e.memset(onesb[:], 1.0)
        e.memset(onescf[:], 1.0)
        e.memset(onescb[:], 1.0)
        e.memset(onesch[:], 1.0)
        e.memset(onesA[:], 1.0)
        e.memset(h0_sb[:], 0.0)
        e.memset(xtail[:], 0.0)
        inc(e.nop(), "VE")
    (C_ONE, C_NEG1, C_QS, C_I1024, C_EPS, C_NHALF, C_ZERO, C_TWO,
     C_M16, C_EIGHT, C_128, C_SPARE) = [consts[:, i:i + 1] for i in range(12)]

    @blk.scalar
    def _(e):
        e.wait_ge(sems["SD"], sd_init)
        inc(e.activation(A_sb[:], A_sb[:], AF.Exp), "SE")
    se_A = C["SE"]
    @blk.vector
    def _(e):
        e.wait_ge(sems["SE"], se_A)
        inc(e.tensor_scalar(A_sb[:], A_sb[:], C_NEG1, None, op0=MUL), "VE")
    ve_A = C["VE"]

    trv = [ps_mm[:, 0:128], ps_mm[:, TCM:TCM + 128],
           ps_o2[:, 0:128], ps_o2[:, TCM:TCM + 128],
           ps_y[:, 0:128], ps_bc[:, 0:128], ps_dt[:, 0:128]]
    trv_se = [0] * 7

    xio_v = [xio[:, 0:D], xio[:, 0:D]]
    Xt = X.rearrange("(n p) m -> n p m", p=128)
    OUTt = OUT.rearrange("(n p) m -> n p m", p=128)

    xio_te = [0, 0]
    for i in range(16):
        bi = i % 2
        @blk.sync
        def _(e, i=i, bi=bi):
            if xio_te[bi]:
                e.wait_ge(sems["TE"], xio_te[bi])
            e.dma_start(out=xio_v[bi][:], in_=Xt[i]).then_inc(sems["SD"], 16)
        C["SD"] += 16
        sd_t = C["SD"]
        for f in range(NFT):
            q = (i * NFT + f) % 7
            @blk.tensor
            def _(e, i=i, f=f, q=q, bi=bi, sd_t=sd_t):
                if f == 0:
                    e.wait_ge(sems["SD"], sd_t)
                    if i == 0:
                        e.wait_ge(sems["SD"], sd_init)
                        e.wait_ge(sems["VE"], ve_A)
                if trv_se[q]:
                    e.wait_ge(sems["SE"], trv_se[q])
                inc(e.transpose(trv[q][:], xio_v[bi][:, f * 128:(f + 1) * 128], idf_sb[:]), "TE")
            te_n = C["TE"]
            @blk.scalar
            def _(e, i=i, f=f, q=q, te_n=te_n):
                e.wait_ge(sems["TE"], te_n)
                inc(e.activation(xfm[:, f * L + i * 128:f * L + (i + 1) * 128], trv[q][:], AF.Identity), "SE")
            trv_se[q] = C["SE"]
        xio_te[0] = xio_te[1] = C["TE"]
    se_xtr = C["SE"]

    A_R, A_T0, A_BS = 0, 32, 64
    B_RS, B_T1, B_CS = 0, 32, 64
    C_M, C_V, C_RP = 0, 32, 64

    def rA(r, w):
        return statrA[r:r + 1, :w]

    def rB(r, w):
        return statrB[r:r + 1, :w]

    def rC(r, w):
        return statrC[r:r + 1, :w]

    def emit_ln_stats(src_tiles, width, ones_src, extra_rs=False):
        for f in range(NFT):
            @blk.scalar
            def _(e, f=f):
                e.wait_ge(sems["VE"], C["VE"])
                if f > 0:
                    e.wait_ge(sems["TE"], C["TE"])
                inc(e.activation(sq_sb[:, :width], src_tiles[f], AF.Square), "SE")
            se_f = C["SE"]
            @blk.tensor
            def _(e, f=f, se_f=se_f):
                e.wait_ge(sems["SE"], se_f)
                e.matmul(ps_s1[0:1, :width], ones_src, src_tiles[f], start=(f == 0), stop=(f == NFT - 1))
                inc(e.matmul(ps_s2[0:1, :width], onescf[:], sq_sb[:, :width],
                             start=(f == 0), stop=(f == NFT - 1)), "TE")
        te_stats = C["TE"]
        @blk.vector
        def _(e):
            e.wait_ge(sems["TE"], te_stats)
            e.tensor_scalar(rC(C_M, width), ps_s1[0:1, :width], C_I1024[0:1, :], None, op0=MUL)
            e.tensor_scalar(rA(A_T0, width), ps_s2[0:1, :width], C_I1024[0:1, :], None, op0=MUL)
            e.tensor_tensor(rB(B_T1, width), rC(C_M, width), rC(C_M, width), op=MUL)
            inc(e.tensor_tensor(rC(C_V, width), rA(A_T0, width), rB(B_T1, width), op=SUB), "VE")
        ve_v = C["VE"]
        @blk.scalar
        def _(e):
            e.wait_ge(sems["VE"], ve_v)
            e.activation(rA(A_T0, width), rC(C_V, width), AF.Ln, bias=C_EPS[0:1, :], scale=1.0)
            i1 = e.activation(rA(A_R, width), rA(A_T0, width), AF.Exp, scale=C_NHALF[0:1, :])
            if extra_rs:
                i1 = e.activation(rB(B_T1, width), rC(C_V, width), AF.Ln)
            inc(i1, "SE")
        se_r = C["SE"]
        if extra_rs:
            @blk.vector
            def _(e):
                e.wait_ge(sems["SE"], se_r)
                inc(e.tensor_tensor(rC(C_V, width), rB(B_T1, width), rA(A_T0, width), op=SUB), "VE")
            ve_q = C["VE"]
            @blk.scalar
            def _(e):
                e.wait_ge(sems["VE"], ve_q)
                e.activation(rB(B_T1, width), rC(C_V, width), AF.Exp)
                e.activation(rB(B_T1, width), rB(B_T1, width), AF.Ln, bias=C_EPS[0:1, :], scale=1.0)
                inc(e.activation(rB(B_RS, width), rB(B_T1, width), AF.Exp, scale=C_NHALF[0:1, :]), "SE")
        @blk.vector
        def _(e):
            e.wait_ge(sems["SE"], C["SE"])
            inc(e.nop(), "VE")
        return C["VE"]

    def emit_bcast(row_ap, width, dst, ve_ready, base=0):
        @blk.tensor
        def _(e, ve_ready=ve_ready):
            e.wait_ge(sems["VE"], ve_ready)
            for n in range((width + 511) // 512):
                sl = slice(n * 512, min(width, (n + 1) * 512))
                mm = e.matmul(dst[:, sl], onesA[base:base + 1, :], row_ap[:, sl], start=True, stop=True)
            inc(mm, "TE")
        return C["TE"]

    def emit_preln(t0, dst_fn):
        src_t = [xfm[:, f * L + t0: f * L + t0 + TCM] for f in range(NFT)]
        ve_mr = emit_ln_stats(src_t, TCM, onescf[:], extra_rs=False)
        emit_bcast(rC(C_M, TCM), TCM, ps_dt, ve_mr, base=C_M)
        te_b = emit_bcast(rA(A_R, TCM), TCM, ps_bc, ve_mr, base=A_R)
        for f in range(NFT):
            @blk.vector
            def _(e, f=f, te_b=te_b):
                if f == 0:
                    e.wait_ge(sems["TE"], te_b)
                    e.wait_ge(sems["SE"], C["SE"])
                e.tensor_tensor(sq_sb[:, :TCM], src_t[f], ps_dt[:, :TCM], op=SUB)
                i = e.tensor_tensor(dst_fn(f), sq_sb[:, :TCM], ps_bc[:, :TCM], op=MUL)
                if f == NFT - 1:
                    inc(i, "VE")
        return C["VE"]

    def emit_postln_resid(of_tiles, t0, ones_src):
        emit_ln_stats(of_tiles, TCM, ones_src, extra_rs=False)
        ve_mr2 = C["VE"]
        emit_bcast(rA(A_R, TCM), TCM, ps_bc, ve_mr2, base=A_R)
        te_b2 = emit_bcast(rC(C_M, TCM), TCM, ps_dt, ve_mr2, base=C_M)
        for f in range(NFT):
            @blk.vector
            def _(e, f=f, te_b2=te_b2):
                if f == 0:
                    e.wait_ge(sems["TE"], te_b2)
                    e.wait_ge(sems["SE"], C["SE"])
                e.tensor_tensor(sq_sb[:, :TCM], of_tiles[f], ps_dt[:, :TCM], op=SUB)
                e.tensor_tensor(sq_sb[:, :TCM], sq_sb[:, :TCM], ps_bc[:, :TCM], op=MUL)
                i = e.tensor_tensor(xfm[:, f * L + t0: f * L + t0 + TCM],
                                    xfm[:, f * L + t0: f * L + t0 + TCM], sq_sb[:, :TCM], op=ADD)
                if f == NFT - 1:
                    inc(i, "VE")
        return C["VE"]

    scan_done = [0, 0]
    y_done = [0, 0]
    for c in range(NCM):
        t0 = c * TCM
        src = [xfm[:, f * L + t0: f * L + t0 + TCM] for f in range(NFT)]

        emit_ln_stats(src, TCM, onescf[:], extra_rs=True)
        @blk.vector
        def _(e):
            e.wait_ge(sems["SE"], C["SE"])
            inc(e.tensor_tensor(rC(C_RP, TCM), rA(A_R, TCM), rB(B_RS, TCM), op=MUL), "VE")
        ve_rp = C["VE"]
        emit_bcast(rC(C_RP, TCM), TCM, ps_bc, ve_rp, base=C_RP)
        emit_bcast(rC(C_M, TCM), TCM, ps_dt, ve_rp, base=C_M)
        te_bc = emit_bcast(rA(A_R, TCM), TCM, ps_y, ve_rp, base=A_R)
        @blk.scalar
        def _(e, te_bc=te_bc):
            e.wait_ge(sems["TE"], te_bc)
            e.activation(rrep_sb[:], ps_y[:, :TCM], AF.Identity)
            inc(e.activation(mrep_sb[:], ps_dt[:, :TCM], AF.Identity), "SE")
        se_rm = C["SE"]
        for f in range(NFT):
            @blk.vector
            def _(e, f=f, te_bc=te_bc):
                if f == 0:
                    e.wait_ge(sems["TE"], te_bc)
                    e.wait_ge(sems["SE"], C["SE"])
                e.tensor_tensor(sq_sb[:, :TCM], src[f], ps_dt[:, :TCM], op=SUB)
                e.tensor_tensor(sq_sb[:, :TCM], sq_sb[:, :TCM], ps_bc[:, :TCM], op=MUL)
                i = e.tensor_scalar(xnbc[:, f * TCM:(f + 1) * TCM], sq_sb[:, :TCM],
                                    rmsw_sb[:, f:f + 1], None, op0=MUL)
                if f == NFT - 1:
                    inc(i, "VE")
        ve_xnb = C["VE"]
        if debug and c == 0:
            @blk.gpsimd
            def _(e):
                e.wait_ge(sems["VE"], ve_xnb)
                e.dma_start(out=dbg["d_xnb"][:], in_=xnbc[:]).then_inc(sems["GD"], 16)
            C["GD"] += 16
        if debug and c == 1:
            @blk.gpsimd
            def _(e):
                e.wait_ge(sems["VE"], ve_xnb)
                e.dma_start(out=dbg["d2_xnb"][:], in_=xnbc[:]).then_inc(sems["GD"], 16)
                e.dma_start(out=dbg["d2_stat"][:], in_=statrC[:]).then_inc(sems["GD"], 16)
            C["GD"] += 32

        ipw_v = [ipw_m[:, 0:1024], ipw_m[:, 1024:2048]]
        mm_done = [0, 0]
        evac_ve = [0, 0]
        evac_se = [0, 0]
        se_silu_prev = 0
        for m in range(32):
            buf = m % 2
            @blk.gpsimd
            def _(e, m=m, buf=buf):
                if m < 2:
                    e.wait_ge(sems["GD"], gd_init)
                if mm_done[buf]:
                    e.wait_ge(sems["TE"], mm_done[buf])
                e.dma_start(out=ipw_v[buf][:], in_=IPW[m]).then_inc(sems["GD"], 16)
            C["GD"] += 16
            gd_t = C["GD"]
            pshalf = ps_mm[:, buf * TCM:(buf + 1) * TCM]
            @blk.tensor
            def _(e, m=m, buf=buf, gd_t=gd_t, pshalf=pshalf, ve_xnb=ve_xnb):
                e.wait_ge(sems["GD"], gd_t)
                if m == 0:
                    e.wait_ge(sems["VE"], ve_xnb)
                if evac_se[buf]:
                    e.wait_ge(sems["SE"], evac_se[buf])
                if evac_ve[buf]:
                    e.wait_ge(sems["VE"], evac_ve[buf])
                for k in range(8):
                    mm = e.matmul(pshalf, ipw_v[buf][:, k * 128:(k + 1) * 128],
                                  xnbc[:, k * TCM:(k + 1) * TCM], start=(k == 0), stop=(k == 7))
                inc(mm, "TE")
            mm_done[buf] = C["TE"]
            if m < 16:
                dg = m
                @blk.vector
                def _(e, dg=dg, buf=buf, pshalf=pshalf, ssp=se_silu_prev):
                    e.wait_ge(sems["TE"], mm_done[buf])
                    if ssp:
                        e.wait_ge(sems["SE"], ssp)
                    w3 = convw_sb[:, dg * 4 + 3: dg * 4 + 4]
                    e.tensor_scalar(sq_sb[:, :TCM], pshalf, w3, None, op0=MUL)
                    for kk in range(3):
                        sh = 3 - kk
                        wk = convw_sb[:, dg * 4 + kk: dg * 4 + kk + 1]
                        e.scalar_tensor_tensor(sq_sb[:, 0:sh], xtail[:, dg * 3 + kk: dg * 3 + 3], wk,
                                               sq_sb[:, 0:sh], op0=MUL, op1=ADD)
                        e.scalar_tensor_tensor(sq_sb[:, sh:TCM], pshalf[:, 0:TCM - sh], wk,
                                               sq_sb[:, sh:TCM], op0=MUL, op1=ADD)
                    inc(e.tensor_copy(xtail[:, dg * 3: dg * 3 + 3], pshalf[:, TCM - 3:TCM]), "VE")
                evac_ve[buf] = C["VE"]
                @blk.scalar
                def _(e, dg=dg, buf=buf):
                    e.wait_ge(sems["VE"], evac_ve[buf])
                    inc(e.activation(xib[:, dg * (TCM + 3) + 3: (dg + 1) * (TCM + 3)], sq_sb[:, :TCM],
                                     AF.Silu, bias=convb_sb[:, dg:dg + 1], scale=1.0), "SE")
                evac_se[buf] = C["SE"]
                se_silu_prev = C["SE"]
            else:
                dg = m - 16
                @blk.scalar
                def _(e, dg=dg, buf=buf, pshalf=pshalf):
                    e.wait_ge(sems["TE"], mm_done[buf])
                    inc(e.activation(zsb[:, dg * TCM:(dg + 1) * TCM], pshalf, AF.Silu), "SE")
                evac_se[buf] = C["SE"]
                evac_ve[buf] = 0
        se_inproj = C["SE"]

        @blk.tensor
        def _(e):
            e.wait_ge(sems["SE"], se_inproj)
            for dg in range(NDG):
                mm = e.matmul(ps_dbcv[:, :], xpw_sb[:, dg * 96:(dg + 1) * 96],
                              xib[:, dg * (TCM + 3) + 3:(dg + 1) * (TCM + 3)],
                              start=(dg == 0), stop=(dg == NDG - 1))
            inc(mm, "TE")
        te_dbc = C["TE"]
        @blk.scalar
        def _(e, te_dbc=te_dbc):
            e.wait_ge(sems["TE"], te_dbc)
            inc(e.activation(dbc_sb[:], ps_dbcv[:, :], AF.Identity), "SE")
        se_dbc = C["SE"]

        se_stage = 0
        for s in range(DS):
            @blk.tensor
            def _(e, s=s, se_stage=se_stage):
                if s == 0:
                    e.wait_ge(sems["SE"], se_dbc)
                    e.wait_ge(sems["VE"], C["VE"])
                else:
                    e.wait_ge(sems["SE"], se_stage)
                e.matmul(ps_dt[0:1, :], indv_sb[:, s:s + 1], dbc_sb[:, :], start=True, stop=True)
                inc(e.matmul(ps_dt[32:33, :], indv_sb[:, 16 + s:17 + s], dbc_sb[:, :], start=True, stop=True), "TE")
            te_ex = C["TE"]
            @blk.scalar
            def _(e, s=s, te_ex=te_ex):
                e.wait_ge(sems["TE"], te_ex)
                e.activation(rA(A_BS, TCM), ps_dt[0:1, :], AF.Identity)
                inc(e.activation(rB(B_CS, TCM), ps_dt[32:33, :], AF.Identity), "SE")
            se_stage = C["SE"]
            @blk.tensor
            def _(e, s=s, se_stage=se_stage):
                e.wait_ge(sems["SE"], se_stage)
                e.matmul(ps_bc[:, :], onesA[A_BS:A_BS + 1, :], rA(A_BS, TCM), start=True, stop=True)
                inc(e.matmul(ps_y[:, :], onesA[B_CS:B_CS + 1, :], rB(B_CS, TCM), start=True, stop=True), "TE")
            te_b = C["TE"]
            @blk.scalar
            def _(e, s=s, te_b=te_b):
                e.wait_ge(sems["TE"], te_b)
                e.activation(Brep[:, s * TCM:(s + 1) * TCM], ps_bc[:, :], AF.Identity)
                inc(e.activation(Crep[:, s * TCM:(s + 1) * TCM], ps_y[:, :], AF.Identity), "SE")
        se_bcrep = C["SE"]

        ve_u_prev = 0
        for dg in range(NDG):
            @blk.tensor
            def _(e, dg=dg):
                if dg == 0:
                    e.wait_ge(sems["SE"], se_bcrep)
                else:
                    e.wait_ge(sems["SE"], C["SE"])
                inc(e.matmul(ps_dt[:, :], dtw_sb[:, dg * 128:(dg + 1) * 128], dbc_sb[0:64, :],
                             start=True, stop=True), "TE")
            te_dt = C["TE"]
            @blk.scalar
            def _(e, dg=dg, te_dt=te_dt, vup=ve_u_prev):
                e.wait_ge(sems["TE"], te_dt)
                if vup:
                    e.wait_ge(sems["VE"], vup)
                e.activation(delta_sb[:], ps_dt[:, :], AF.Exp, bias=dtb_sb[:, dg:dg + 1], scale=1.0)
                inc(e.activation(delta_sb[:], delta_sb[:], AF.Ln, bias=C_ONE, scale=1.0), "SE")
            se_delta = C["SE"]
            @blk.vector
            def _(e, dg=dg, se_delta=se_delta):
                e.wait_ge(sems["SE"], se_delta)
                inc(e.tensor_tensor(u_sb[:], delta_sb[:],
                                    xib[:, dg * (TCM + 3) + 3:(dg + 1) * (TCM + 3)], op=MUL), "VE")
            ve_u = C["VE"]
            ve_u_prev = ve_u

            for s in range(DS):
                sl = s % 2
                asl = a_sb[:, sl * TCM:(sl + 1) * TCM]
                bsl = b_sb[:, sl * TCM:(sl + 1) * TCM]
                hsl = h_sb[:, sl * TCM:(sl + 1) * TCM]
                hcsl = hc_sb[:, sl * TCM:(sl + 1) * TCM]
                @blk.scalar
                def _(e, dg=dg, s=s, sl=sl, asl=asl, sd=scan_done[sl]):
                    if sd:
                        e.wait_ge(sems["VE"], sd)
                    inc(e.activation(asl, delta_sb[:], AF.Exp,
                                     scale=A_sb[:, dg * DS + s: dg * DS + s + 1]), "SE")
                se_a = C["SE"]
                @blk.vector
                def _(e, dg=dg, s=s, sl=sl, se_a=se_a, asl=asl, bsl=bsl, hsl=hsl, hcsl=hcsl,
                      ve_u=ve_u, yd=y_done[sl]):
                    if s == 0:
                        e.wait_ge(sems["VE"], ve_u)
                    if yd:
                        e.wait_ge(sems["TE"], yd)
                    e.tensor_tensor(bsl, u_sb[:], Brep[:, s * TCM:(s + 1) * TCM], op=MUL)
                    e.wait_ge(sems["SE"], se_a)
                    e.tensor_tensor_scan(hsl, asl, bsl, h0_sb[:, dg * DS + s: dg * DS + s + 1],
                                         op0=MUL, op1=ADD)
                    e.tensor_copy(h0_sb[:, dg * DS + s: dg * DS + s + 1], hsl[:, TCM - 1:TCM])
                    inc(e.tensor_tensor(hcsl, hsl, Crep[:, s * TCM:(s + 1) * TCM], op=MUL), "VE")
                scan_done[sl] = C["VE"]
                @blk.tensor
                def _(e, s=s, sl=sl, hcsl=hcsl, vs=scan_done[sl]):
                    e.wait_ge(sems["VE"], vs)
                    inc(e.matmul(ps_y[:, :], idb_sb[:], hcsl, start=(s == 0), stop=(s == DS - 1)), "TE")
                y_done[sl] = C["TE"]
            te_y = C["TE"]
            @blk.vector
            def _(e, dg=dg, te_y=te_y):
                e.wait_ge(sems["TE"], te_y)
                e.scalar_tensor_tensor(sq_sb[:, :TCM], xib[:, dg * (TCM + 3) + 3:(dg + 1) * (TCM + 3)],
                                       D_sb[:, dg:dg + 1], ps_y[:, :], op0=MUL, op1=ADD)
                inc(e.tensor_tensor(y2b[:, dg * TCM:(dg + 1) * TCM], sq_sb[:, :TCM],
                                    zsb[:, dg * TCM:(dg + 1) * TCM], op=MUL), "VE")
        ve_y2 = C["VE"]
        if debug and c == 0:
            @blk.gpsimd
            def _(e):
                e.wait_ge(sems["VE"], ve_y2)
                e.dma_start(out=dbg["d_y2"][:], in_=y2b[:]).then_inc(sems["GD"], 16)
            C["GD"] += 16
        if debug and c == 1:
            @blk.gpsimd
            def _(e):
                e.wait_ge(sems["VE"], ve_y2)
                e.dma_start(out=dbg["d2_y2"][:], in_=y2b[:]).then_inc(sems["GD"], 16)
                e.dma_start(out=dbg["d2_xib"][:], in_=xib[:]).then_inc(sems["GD"], 16)
                e.dma_start(out=dbg["d2_dbc"][:], in_=dbc_sb[:]).then_inc(sems["GD"], 16)
            C["GD"] += 48

        opw_v = [opw_m[:, 0:2048], opw_m[:, 2048:4096]]
        opmm_done = [0, 0]
        of_done = [0, 0]
        ps_op = [ps_mm[:, 0:TCM], ps_mm[:, TCM:2 * TCM]]
        for m in range(NFT):
            buf = m % 2
            @blk.gpsimd
            def _(e, m=m, buf=buf):
                if opmm_done[buf]:
                    e.wait_ge(sems["TE"], opmm_done[buf])
                e.dma_start(out=opw_v[buf][:], in_=OPW[m]).then_inc(sems["GD"], 16)
            C["GD"] += 16
            gd_t = C["GD"]
            @blk.tensor
            def _(e, m=m, buf=buf, gd_t=gd_t, ve_y2=ve_y2):
                e.wait_ge(sems["GD"], gd_t)
                if m == 0:
                    e.wait_ge(sems["VE"], ve_y2)
                if of_done[buf]:
                    e.wait_ge(sems["VE"], of_done[buf])
                for k in range(16):
                    mm = e.matmul(ps_op[buf], opw_v[buf][:, k * 128:(k + 1) * 128],
                                  y2b[:, k * TCM:(k + 1) * TCM], start=(k == 0), stop=(k == 15))
                inc(mm, "TE")
            opmm_done[buf] = C["TE"]
            @blk.vector
            def _(e, m=m, buf=buf, c=c, se_rm=se_rm):
                e.wait_ge(sems["TE"], opmm_done[buf])
                if m == 0:
                    e.wait_ge(sems["SE"], se_rm)
                e.tensor_tensor(sq_sb[:, :TCM], xfm[:, m * L + t0: m * L + t0 + TCM], mrep_sb[:], op=SUB)
                e.tensor_tensor(sq_sb[:, :TCM], sq_sb[:, :TCM], rrep_sb[:], op=MUL)
                inc(e.tensor_tensor(of16[:, m * TCM:(m + 1) * TCM], sq_sb[:, :TCM], ps_op[buf], op=ADD), "VE")
            of_done[buf] = C["VE"]

        if debug and c == 1:
            @blk.gpsimd
            def _(e):
                e.wait_ge(sems["VE"], C["VE"])
                e.dma_start(out=dbg["d2_of"][:], in_=of16[:]).then_inc(sems["GD"], 16)
            C["GD"] += 16
        of_t = [of16[:, f * TCM:(f + 1) * TCM] for f in range(NFT)]
        emit_postln_resid(of_t, t0, onescb[:])
    ve_mamba = C["VE"]
    if debug:
        @blk.gpsimd
        def _(e):
            e.wait_ge(sems["VE"], ve_mamba)
            e.dma_start(out=dbg["d_x1"][:], in_=xfm[:]).then_inc(sems["GD"], 16)
        C["GD"] += 16

    es_m.close()

    es_f = contextlib.ExitStack()

    def sbf(name, shape, dt):
        return es_f.enter_context(nc.sbuf_tensor(name, list(shape), dt))

    xnb2 = sbf("xnb2", [128, NFT * TCM], BF16)
    offq = sbf("offq", [128, NFT * TCM], FP16)
    hq_sb = sbf("hq_sb", [128, 32 * TCM], FP8)
    wq_m = sbf("wq_m", [128, 2 * H], FP8)
    b1_sb = sbf("b1_sb", [128, 32], F32)
    b2q_sb = sbf("b2q_sb", [128, NFT], F32)
    b2h_sb = sbf("b2h_sb", [128, NFT], FP16)

    @blk.sync
    def _(e):
        e.wait_ge(sems["VE"], ve_mamba)
        e.dma_start(out=b1_sb[:], in_=B1[:]).then_inc(sems["SD"], 16)
        e.dma_start(out=b2q_sb[:], in_=B2[:]).then_inc(sems["SD"], 16)
    C["SD"] += 32
    sd_f = C["SD"]
    @blk.scalar
    def _(e):
        e.wait_ge(sems["SD"], sd_f)
        e.activation(b2h_sb[:], b2q_sb[:], AF.Identity)
        inc(e.activation(b2q_sb[:], b2h_sb[:], AF.Identity), "SE")
    se_b2q = C["SE"]

    for c in range(NCM):
        t0 = c * TCM
        ve_xn2 = emit_preln(t0, lambda f: xnb2[:, f * TCM:(f + 1) * TCM])
        mmf_done = [0, 0]
        epi_se = [0, 0]
        for m in range(32):
            buf = m % 2
            @blk.gpsimd
            def _(e, m=m, buf=buf, c=c):
                if c == 0 and m < 2:
                    e.wait_ge(sems["VE"], ve_mamba)
                if mmf_done[buf]:
                    e.wait_ge(sems["TE"], mmf_done[buf])
                e.dma_start(out=ipw_m[:, buf * 1024:(buf + 1) * 1024], in_=W1T[m]).then_inc(sems["GD"], 16)
            C["GD"] += 16
            gd_t = C["GD"]
            pshalf = ps_mm[:, buf * TCM:(buf + 1) * TCM]
            @blk.tensor
            def _(e, m=m, buf=buf, gd_t=gd_t, pshalf=pshalf, ve_xn2=ve_xn2):
                e.wait_ge(sems["GD"], gd_t)
                if m == 0:
                    e.wait_ge(sems["VE"], ve_xn2)
                if epi_se[buf]:
                    e.wait_ge(sems["SE"], epi_se[buf])
                for k in range(8):
                    mm = e.matmul(pshalf, ipw_m[:, buf * 1024 + k * 128: buf * 1024 + (k + 1) * 128],
                                  xnb2[:, k * TCM:(k + 1) * TCM], start=(k == 0), stop=(k == 7))
                inc(mm, "TE")
            mmf_done[buf] = C["TE"]
            @blk.scalar
            def _(e, m=m, buf=buf, pshalf=pshalf):
                e.wait_ge(sems["TE"], mmf_done[buf])
                if m == 0:
                    e.wait_ge(sems["SE"], se_b2q)
                inc(e.activation(hq_sb[:, m * TCM:(m + 1) * TCM], pshalf, AF.Relu,
                                 bias=b1_sb[:, m:m + 1], scale=1.0), "SE")
            epi_se[buf] = C["SE"]
        mm2_done = [0, 0]
        epi2_se = [0, 0]
        for m in range(NFT):
            buf = m % 2
            @blk.gpsimd
            def _(e, m=m, buf=buf):
                if mm2_done[buf]:
                    e.wait_ge(sems["TE"], mm2_done[buf])
                e.dma_start(out=wq_m[:, buf * H:(buf + 1) * H], in_=W2T[m]).then_inc(sems["GD"], 16)
            C["GD"] += 16
            gd_t = C["GD"]
            psh2 = ps_o2[:, buf * TCM:(buf + 1) * TCM]
            @blk.tensor
            def _(e, m=m, buf=buf, gd_t=gd_t, psh2=psh2):
                e.wait_ge(sems["GD"], gd_t)
                if m == 0:
                    e.wait_ge(sems["SE"], epi_se[0])
                    e.wait_ge(sems["SE"], epi_se[1])
                if epi2_se[buf]:
                    e.wait_ge(sems["SE"], epi2_se[buf])
                for k in range(32):
                    mm = e.matmul(psh2, wq_m[:, buf * H + k * 128: buf * H + (k + 1) * 128],
                                  hq_sb[:, k * TCM:(k + 1) * TCM], start=(k == 0), stop=(k == 31))
                inc(mm, "TE")
            mm2_done[buf] = C["TE"]
            @blk.scalar
            def _(e, m=m, buf=buf, psh2=psh2):
                e.wait_ge(sems["TE"], mm2_done[buf])
                inc(e.activation(offq[:, m * TCM:(m + 1) * TCM], psh2, AF.Identity,
                                 bias=b2q_sb[:, m:m + 1], scale=1.0), "SE")
            epi2_se[buf] = C["SE"]
        of_t = [offq[:, f * TCM:(f + 1) * TCM] for f in range(NFT)]
        emit_postln_resid(of_t, t0, onesch[:])
    ve_ffn0 = C["VE"]
    if debug:
        @blk.gpsimd
        def _(e):
            e.wait_ge(sems["VE"], ve_ffn0)
            e.dma_start(out=dbg["d_x2"][:], in_=xfm[:]).then_inc(sems["GD"], 16)
        C["GD"] += 16
    es_f.close()

    es_a = contextlib.ExitStack()

    def sba(name, shape, dt):
        return es_a.enter_context(nc.sbuf_tensor(name, list(shape), dt))

    VT = sba("VT", [128, 16 * 1040], BF16)
    negones = sba("negones", [1, 128], F32)
    bq_sb = sba("bq_sb", [128, NFT], F32)
    bk_sb = sba("bk_sb", [128, NFT], F32)
    bout_sb = sba("bout_sb", [128, NFT], F32)
    bvrow = sba("bvrow", [1, D], BF16)
    Qb = sba("Qb", [128, 8 * L], BF16)
    es_aK = contextlib.ExitStack()
    Kb = es_aK.enter_context(nc.sbuf_tensor("Kb", [128, 8 * L], BF16))
    es_a1 = contextlib.ExitStack()
    wv_sb = es_a1.enter_context(nc.sbuf_tensor("wv_sb", [128, 8 * 1024], BF16))
    xnbq = es_a1.enter_context(nc.sbuf_tensor("xnbq", [128, NFT * TCM], BF16))

    @blk.sync
    def _(e):
        e.wait_ge(sems["VE"], ve_ffn0)
        e.dma_start(out=bq_sb[:], in_=BQ[:]).then_inc(sems["SD"], 16)
        e.dma_start(out=bk_sb[:], in_=BK[:]).then_inc(sems["SD"], 16)
        e.dma_start(out=bout_sb[:], in_=BOUT[:]).then_inc(sems["SD"], 16)
    C["SD"] += 48
    sd_a = C["SD"]
    @blk.gpsimd
    def _(e):
        e.wait_ge(sems["VE"], ve_ffn0)
        e.dma_start(out=bvrow[:], in_=BV[:]).then_inc(sems["GD"], 16)
        for k in range(8):
            e.dma_start(out=wv_sb[:, k * 1024:(k + 1) * 1024], in_=WVR[k]).then_inc(sems["GD"], 16)
    C["GD"] += 144
    gd_wv = C["GD"]
    @blk.vector
    def _(e):
        e.wait_ge(sems["SD"], sd_a)
        e.memset(negones[:], -1.0)
        e.tensor_scalar(bq_sb[:], bq_sb[:], C_QS, None, op0=MUL)
        vt3 = VT[:].rearrange("p (t h c) -> p t h c", h=NH, c=65)
        inc(e.memset(vt3[:, :, :, 64:65], 1.0), "VE")
    ve_abias = C["VE"]

    se_vt_prev = 0
    mmq_done = [0, 0]
    for qc in range(NCM):
        t0 = qc * TCM
        ve_xq = emit_preln(t0, lambda f: xnbq[:, f * TCM:(f + 1) * TCM])
        for tt in range(4):
            t16 = qc * 4 + tt
            @blk.tensor
            def _(e, tt=tt, t16=t16, ve_xq=ve_xq, svp=se_vt_prev):
                if t16 == 0:
                    e.wait_ge(sems["GD"], gd_wv)
                e.wait_ge(sems["VE"], ve_xq)
                if svp:
                    e.wait_ge(sems["SE"], svp)
                for k in range(8):
                    for nh in range(2):
                        e.matmul(ps_mm[:, nh * TCM:(nh + 1) * TCM],
                                 xnbq[:, k * TCM + tt * 128: k * TCM + (tt + 1) * 128],
                                 wv_sb[:, k * 1024 + nh * TCM: k * 1024 + (nh + 1) * TCM],
                                 start=(k == 0), stop=False)
                for nh in range(2):
                    mm = e.matmul(ps_mm[:, nh * TCM:(nh + 1) * TCM], onesb[0:1, :],
                                  bvrow[0:1, nh * TCM:(nh + 1) * TCM], start=False, stop=True)
                inc(mm, "TE")
            te_v = C["TE"]
            @blk.scalar
            def _(e, t16=t16, te_v=te_v):
                e.wait_ge(sems["TE"], te_v)
                if t16 == 0:
                    e.wait_ge(sems["VE"], ve_abias)
                dst = VT[:, t16 * 1040:(t16 + 1) * 1040].rearrange("p (h c) -> p h c", c=65)[:, :, 0:64]
                src_ = ps_mm[:, 0:1024].rearrange("p (h c) -> p h c", c=64)
                inc(e.activation(dst, src_, AF.Identity), "SE")
            se_vt_prev = C["SE"]
        for m in range(16):
            buf = m % 2
            @blk.gpsimd
            def _(e, m=m, buf=buf, qc=qc, svp=se_vt_prev):
                if qc == 0 and m < 2:
                    e.wait_ge(sems["SE"], svp)
                if mmq_done[buf]:
                    e.wait_ge(sems["TE"], mmq_done[buf])
                e.dma_start(out=ipw_m[:, buf * 1024:(buf + 1) * 1024], in_=WQKT[m]).then_inc(sems["GD"], 16)
            C["GD"] += 16
            gd_t = C["GD"]
            pshalf = ps_mm[:, buf * TCM:(buf + 1) * TCM]
            @blk.tensor
            def _(e, m=m, buf=buf, gd_t=gd_t, pshalf=pshalf):
                e.wait_ge(sems["GD"], gd_t)
                e.wait_ge(sems["SE"], C["SE"])
                for k in range(8):
                    mm = e.matmul(pshalf, ipw_m[:, buf * 1024 + k * 128: buf * 1024 + (k + 1) * 128],
                                  xnbq[:, k * TCM:(k + 1) * TCM], start=(k == 0), stop=(k == 7))
                inc(mm, "TE")
            mmq_done[buf] = C["TE"]
            @blk.scalar
            def _(e, m=m, qc=qc, buf=buf, pshalf=pshalf):
                e.wait_ge(sems["TE"], mmq_done[buf])
                if m < 8:
                    inc(e.activation(Qb[:, m * L + qc * TCM: m * L + (qc + 1) * TCM], pshalf,
                                     AF.Identity, bias=bq_sb[:, m:m + 1], scale=C_QS), "SE")
                else:
                    inc(e.activation(Kb[:, (m - 8) * L + qc * TCM: (m - 8) * L + (qc + 1) * TCM], pshalf,
                                     AF.Identity, bias=bk_sb[:, m - 8:m - 7], scale=1.0), "SE")
    se_qk = C["SE"]
    es_a1.close()
    es_a2 = contextlib.ExitStack()
    arow = es_a2.enter_context(nc.sbuf_tensor("arow", [96, 1024], F32))
    rdrep = es_a2.enter_context(nc.sbuf_tensor("rdrep", [64, 1024], F32))
    expP = es_a2.enter_context(nc.sbuf_tensor("expP", [128, 2 * 1024], BF16))
    if debug:
        @blk.gpsimd
        def _(e):
            e.wait_ge(sems["SE"], se_qk)
            e.dma_start(out=dbg["d_q"][:], in_=Qb[:]).then_inc(sems["GD"], 16)
            e.dma_start(out=dbg["d_vt"][:], in_=VT[:]).then_inc(sems["GD"], 16)
        C["GD"] += 32

    se_exp = 0
    te_den = [0, 0]
    te_av = [0, 0]
    ve_anorm = 0
    for h in range(NH):
        tile = h // 2
        po = (h % 2) * 64
        for qh in range(2):
            for kc in range(16):
                sl = kc % 2
                @blk.tensor
                def _(e, h=h, qh=qh, kc=kc, tile=tile, po=po, se_exp=se_exp, va=ve_anorm):
                    if h == 0 and qh == 0 and kc == 0:
                        e.wait_ge(sems["SE"], se_qk)
                    if se_exp:
                        e.wait_ge(sems["SE"], se_exp)
                    if va:
                        e.wait_ge(sems["VE"], va)
                    for nh in range(2):
                        mm = e.matmul(ps_mm[:, nh * TCM:(nh + 1) * TCM],
                                      Kb[po:po + 64, tile * L + kc * 128: tile * L + (kc + 1) * 128],
                                      Qb[po:po + 64, tile * L + qh * 1024 + nh * TCM: tile * L + qh * 1024 + (nh + 1) * TCM],
                                      start=True, stop=True)
                    inc(mm, "TE")
                te_sc = C["TE"]
                @blk.scalar
                def _(e, kc=kc, sl=sl, te_sc=te_sc, td=te_den[sl]):
                    e.wait_ge(sems["TE"], te_sc)
                    if td:
                        e.wait_ge(sems["TE"], td)
                    inc(e.activation(expP[:, sl * 1024:(sl + 1) * 1024], ps_mm[:, 0:1024], AF.Exp,
                                     bias=C_M16, scale=C_SPARE), "SE")
                se_exp = C["SE"]
                @blk.tensor
                def _(e, kc=kc, sl=sl, se_exp=se_exp):
                    e.wait_ge(sems["SE"], se_exp)
                    e.matmul(ps_y[0:1, :], onescb[:], expP[:, sl * 1024: sl * 1024 + TCM],
                             start=(kc == 0), stop=(kc == 15))
                    inc(e.matmul(ps_bc[0:1, :], onescb[:], expP[:, sl * 1024 + TCM: (sl + 1) * 1024],
                                 start=(kc == 0), stop=(kc == 15)), "TE")
                te_den[sl] = C["TE"]
            te_denl = C["TE"]
            @blk.scalar
            def _(e, te_denl=te_denl):
                e.wait_ge(sems["TE"], te_denl)
                e.activation(arow[0:1, 0:TCM], ps_y[0:1, :], AF.Ln)
                e.activation(arow[0:1, TCM:1024], ps_bc[0:1, :], AF.Ln)
                inc(e.activation(arow[0:1, 0:1024], arow[0:1, 0:1024], AF.Identity,
                                 bias=C_128[0:1, :], scale=C_EIGHT[0:1, :]), "SE")
            se_c = C["SE"]
            for kc in range(16):
                sl = kc % 2
                @blk.tensor
                def _(e, h=h, qh=qh, kc=kc, tile=tile, po=po, se_exp=se_exp, se_c=se_c):
                    if kc == 0:
                        e.wait_ge(sems["SE"], se_c)
                    else:
                        e.wait_ge(sems["SE"], se_exp)
                    for nh in range(2):
                        e.matmul(ps_mm[:, nh * TCM:(nh + 1) * TCM],
                                 Kb[po:po + 64, tile * L + kc * 128: tile * L + (kc + 1) * 128],
                                 Qb[po:po + 64, tile * L + qh * 1024 + nh * TCM: tile * L + qh * 1024 + (nh + 1) * TCM],
                                 start=True, stop=False)
                    for nh in range(2):
                        mm = e.matmul(ps_mm[:, nh * TCM:(nh + 1) * TCM], negones[0:1, :],
                                      arow[0:1, nh * TCM:(nh + 1) * TCM], start=False, stop=True)
                    inc(mm, "TE")
                te_sc = C["TE"]
                @blk.scalar
                def _(e, kc=kc, sl=sl, te_sc=te_sc, ta=te_av[sl]):
                    e.wait_ge(sems["TE"], te_sc)
                    if ta:
                        e.wait_ge(sems["TE"], ta)
                    inc(e.activation(expP[:, sl * 1024:(sl + 1) * 1024], ps_mm[:, 0:1024], AF.Exp), "SE")
                se_exp = C["SE"]
                @blk.tensor
                def _(e, h=h, kc=kc, sl=sl, se_exp=se_exp):
                    e.wait_ge(sems["SE"], se_exp)
                    for nh in range(2):
                        mm = e.matmul(ps_o2[0:65, nh * TCM:(nh + 1) * TCM],
                                      VT[:, kc * 1040 + h * 65: kc * 1040 + (h + 1) * 65],
                                      expP[:, sl * 1024 + nh * TCM: sl * 1024 + (nh + 1) * TCM],
                                      start=(kc == 0), stop=(kc == 15))
                    inc(mm, "TE")
                te_av[sl] = C["TE"]
            te_avl = C["TE"]
            @blk.scalar
            def _(e, te_avl=te_avl):
                e.wait_ge(sems["TE"], te_avl)
                e.activation(arow[32:33, :], ps_o2[64:65, 0:1024], AF.Ln)
                inc(e.activation(arow[32:33, :], arow[32:33, :], AF.Exp, scale=C_NEG1[0:1, :]), "SE")
            se_rd = C["SE"]
            @blk.tensor
            def _(e, se_rd=se_rd):
                e.wait_ge(sems["SE"], se_rd)
                for nh in range(2):
                    mm = e.matmul(ps_mm[0:64, nh * TCM:(nh + 1) * TCM], onesA[32:33, 0:64],
                                  arow[32:33, nh * TCM:(nh + 1) * TCM], start=True, stop=True)
                inc(mm, "TE")
            te_rd = C["TE"]
            @blk.scalar
            def _(e, te_rd=te_rd):
                e.wait_ge(sems["TE"], te_rd)
                inc(e.activation(rdrep[:], ps_mm[0:64, 0:1024], AF.Identity), "SE")
            se_rdc = C["SE"]
            @blk.vector
            def _(e, h=h, qh=qh, tile=tile, po=po, se_rdc=se_rdc):
                e.wait_ge(sems["SE"], se_rdc)
                inc(e.tensor_tensor(Qb[po:po + 64, tile * L + qh * 1024: tile * L + (qh + 1) * 1024],
                                    ps_o2[0:64, 0:1024], rdrep[:], op=MUL), "VE")
            ve_anorm = C["VE"]
    ve_attn_done = C["VE"]
    if debug:
        @blk.gpsimd
        def _(e):
            e.wait_ge(sems["VE"], ve_attn_done)
            e.dma_start(out=dbg["d_ao"][:], in_=Qb[:]).then_inc(sems["GD"], 16)
            e.dma_start(out=dbg["d_den"][:], in_=arow[:]).then_inc(sems["GD"], 16)
        C["GD"] += 32

    es_a2.close()
    es_aK.close()
    wout_res = sba("wout_res", [128, 8 * 1024], BF16)
    ofa = sba("ofa", [128, NFT * TCM], F32)
    @blk.gpsimd
    def _(e):
        e.wait_ge(sems["VE"], ve_attn_done)
        for m in range(8):
            e.dma_start(out=wout_res[:, m * 1024:(m + 1) * 1024], in_=WOUTT[m]).then_inc(sems["GD"], 16)
    C["GD"] += 128
    gd_wo = C["GD"]
    for c in range(NCM):
        t0 = c * TCM
        for m in range(NFT):
            pshalf = ps_mm[:, (m % 2) * TCM:((m % 2) + 1) * TCM]
            @blk.tensor
            def _(e, m=m, c=c, pshalf=pshalf):
                if m == 0 and c == 0:
                    e.wait_ge(sems["GD"], gd_wo)
                    e.wait_ge(sems["VE"], ve_attn_done)
                e.wait_ge(sems["SE"], C["SE"])
                for k in range(8):
                    mm = e.matmul(pshalf, wout_res[:, m * 1024 + k * 128: m * 1024 + (k + 1) * 128],
                                  Qb[:, k * L + t0: k * L + t0 + TCM], start=(k == 0), stop=(k == 7))
                inc(mm, "TE")
            te_wo = C["TE"]
            @blk.scalar
            def _(e, m=m, te_wo=te_wo, pshalf=pshalf):
                e.wait_ge(sems["TE"], te_wo)
                inc(e.activation(ofa[:, m * TCM:(m + 1) * TCM], pshalf, AF.Identity,
                                 bias=bout_sb[:, m:m + 1], scale=1.0), "SE")
        of_t = [ofa[:, f * TCM:(f + 1) * TCM] for f in range(NFT)]
        if debug and c == 0:
            @blk.gpsimd
            def _(e):
                e.wait_ge(sems["SE"], C["SE"])
                e.dma_start(out=dbg["d_ofa"][:], in_=ofa[:]).then_inc(sems["GD"], 16)
            C["GD"] += 16
        emit_postln_resid(of_t, t0, onescf[:])
    ve_attn = C["VE"]
    if debug:
        @blk.gpsimd
        def _(e):
            e.wait_ge(sems["VE"], ve_attn)
            e.dma_start(out=dbg["d_x3"][:], in_=xfm[:]).then_inc(sems["GD"], 16)
        C["GD"] += 16
    es_a.close()

    es_l = contextlib.ExitStack()

    def sbl(name, shape, dt):
        return es_l.enter_context(nc.sbuf_tensor(name, list(shape), dt))

    xnbL = sbl("xnbL", [128, NFT * L], BF16)
    l1res = sbl("l1res", [128, 8 * 1024], BF16)
    ofl = sbl("ofl", [128, NFT * TCM], F32)

    @blk.gpsimd
    def _(e):
        e.wait_ge(sems["VE"], ve_attn)
        for m in range(8):
            e.dma_start(out=l1res[:, m * 1024:(m + 1) * 1024], in_=L1WT[m]).then_inc(sems["GD"], 16)
    C["GD"] += 128
    gd_l1 = C["GD"]
    @blk.scalar
    def _(e):
        e.wait_ge(sems["GD"], gd_l1)
        inc(e.activation(l1res[:], l1res[:], AF.Sign), "SE")
    se_sign = C["SE"]

    for c in range(NCM):
        emit_preln(c * TCM,
                   lambda f, c=c: xnbL[:, f * L + c * TCM: f * L + (c + 1) * TCM])
    ve_xnL = C["VE"]
    for c in range(NCM):
        t0 = c * TCM
        for m in range(NFT):
            pshalf = ps_mm[:, (m % 2) * TCM:((m % 2) + 1) * TCM]
            @blk.tensor
            def _(e, m=m, c=c, pshalf=pshalf):
                if m == 0 and c == 0:
                    e.wait_ge(sems["SE"], se_sign)
                    e.wait_ge(sems["VE"], ve_xnL)
                e.wait_ge(sems["SE"], C["SE"])
                for k in range(8):
                    mm = e.matmul(pshalf, l1res[:, m * 1024 + k * 128: m * 1024 + (k + 1) * 128],
                                  xnbL[:, k * L + t0: k * L + t0 + TCM], start=(k == 0), stop=(k == 7))
                inc(mm, "TE")
            te_l1 = C["TE"]
            @blk.scalar
            def _(e, m=m, te_l1=te_l1, pshalf=pshalf):
                e.wait_ge(sems["TE"], te_l1)
                inc(e.activation(ofl[:, m * TCM:(m + 1) * TCM], pshalf, AF.Identity), "SE")
        of_t = [ofl[:, f * TCM:(f + 1) * TCM] for f in range(NFT)]
        emit_postln_resid(of_t, t0, onescf[:])
    ve_final = C["VE"]
    es_l.close()

    xio_sd = [0, 0]
    for i in range(16):
        bi = i % 2
        for f in range(NFT):
            q = (i * NFT + f) % 7
            @blk.tensor
            def _(e, i=i, f=f, q=q):
                if i == 0 and f == 0:
                    e.wait_ge(sems["VE"], ve_final)
                if trv_se[q]:
                    e.wait_ge(sems["SE"], trv_se[q])
                inc(e.transpose(trv[q][:], xfm[:, f * L + i * 128:f * L + (i + 1) * 128], idf_sb[:]), "TE")
            te_n = C["TE"]
            @blk.scalar
            def _(e, i=i, f=f, q=q, bi=bi, te_n=te_n):
                e.wait_ge(sems["TE"], te_n)
                if f == 0 and xio_sd[bi]:
                    e.wait_ge(sems["SD"], xio_sd[bi])
                inc(e.activation(xio_v[bi][:, f * 128:(f + 1) * 128], trv[q][:], AF.Identity), "SE")
            trv_se[q] = C["SE"]
        se_t = C["SE"]
        @blk.sync
        def _(e, i=i, bi=bi, se_t=se_t):
            e.wait_ge(sems["SE"], se_t)
            e.dma_start(out=OUTt[i], in_=xio_v[bi][:]).then_inc(sems["SD"], 16)
        C["SD"] += 16
        xio_sd[0] = xio_sd[1] = C["SD"]
    @blk.sync
    def _(e):
        e.wait_ge(sems["SD"], C["SD"])

    es.close()
    return nc


_CACHE = {}


def _prep(inputs):
    f32 = np.float32

    def c(a):
        return np.ascontiguousarray(np.asarray(a), dtype=f32)

    def pmaj(a, n):
        s = np.asarray(a).shape[1]
        return c(np.asarray(a).reshape(n, 128, s).transpose(1, 0, 2).reshape(128, n * s))

    def vec_pm(a, n):
        return c(np.asarray(a).reshape(n, 128).T)

    ip = c(inputs["L0_attn_in_proj"])
    ipw = c(ip.reshape(8, 128, 32, 128).transpose(2, 1, 0, 3).reshape(32, 128, 1024))
    op = c(inputs["L0_attn_out_proj"])
    opw = c(op.reshape(16, 128, 8, 128).transpose(2, 1, 0, 3).reshape(8, 128, 2048))
    shared = {
        "ipw": ipw, "opw": opw,
        "xpw": pmaj(inputs["L0_attn_x_proj"], NDG),
        "dtw": c(inputs["L0_attn_dt_w"]),
        "convw": pmaj(inputs["L0_attn_conv_w"], NDG),
        "convb": vec_pm(inputs["L0_attn_conv_b"], NDG),
        "dtb": vec_pm(inputs["L0_attn_dt_b"], NDG),
        "alog": pmaj(inputs["L0_attn_A_log"], NDG),
        "dd": vec_pm(inputs["L0_attn_D"], NDG),
        "rmsw": vec_pm(inputs["L0_attn_rms_w"], NFT),
        "idf": np.eye(128, dtype=f32),
        "idb": np.eye(128, dtype=f32).astype(ml_dtypes.bfloat16),
        "indv": np.concatenate([np.eye(96, dtype=f32)[:, 64:80],
                                np.eye(96, dtype=f32)[:, 80:96]], axis=1).astype(ml_dtypes.bfloat16),
    }
    w1 = c(inputs["L0_ffn_w1"]).T
    shared["w1t"] = c(w1.reshape(8, 128, 32, 128).transpose(2, 1, 0, 3).reshape(32, 128, 1024))
    shared["b1"] = vec_pm(inputs["L0_ffn_b1"], 32)
    w2 = c(inputs["L0_ffn_w2"]).T
    shared["w2t"] = c(w2.reshape(32, 128, 8, 128).transpose(2, 1, 0, 3).reshape(8, 128, 4096))
    shared["b2"] = vec_pm(inputs["L0_ffn_b2"], NFT)
    wqkv = c(inputs["L1_attn_W_qkv"])
    wq_r = wqkv.reshape(NH, 3, DH, D).transpose(1, 0, 2, 3).reshape(3 * D, D)
    wqkT = c(wq_r[0:2 * D].T)
    shared["wqkt"] = c(wqkT.reshape(8, 128, 16, 128).transpose(2, 1, 0, 3).reshape(16, 128, 1024))
    shared["wvr"] = c(wq_r[2 * D:3 * D].T.reshape(8, 128, 1024))
    bqkv = c(inputs["L1_attn_b_qkv"]).reshape(NH, 3, DH).transpose(1, 0, 2).reshape(3, D)
    shared["bq"] = vec_pm(bqkv[0], NFT)
    shared["bk"] = vec_pm(bqkv[1], NFT)
    shared["bv"] = c(bqkv[2]).reshape(1, D)
    wout = c(inputs["L1_attn_W_out"]).T
    shared["woutt"] = c(wout.reshape(8, 128, 8, 128).transpose(2, 1, 0, 3).reshape(8, 128, 1024))
    shared["bout"] = vec_pm(inputs["L1_attn_b_out"], NFT)
    l1w = c(inputs["L1_ffn_weight"]).T
    shared["l1wt"] = c(l1w.reshape(8, 128, 8, 128).transpose(2, 1, 0, 3).reshape(8, 128, 1024))
    return shared


def kernel(**inputs):
    shared = _prep(inputs)
    x = np.ascontiguousarray(np.asarray(inputs["L0_attn_x"]), dtype=np.float32)
    if "nc" not in _CACHE:
        _CACHE["nc"] = build(debug=DEBUG)
    nc = _CACHE["nc"]
    in_maps = []
    for b in range(8):
        m = dict(shared)
        m["x"] = np.ascontiguousarray(x[b])
        in_maps.append(m)
    res = run_bass_kernel_spmd(nc, in_maps, core_ids=list(range(8)))
    out = np.stack([res.results[b]["out"] for b in range(8)], axis=0)
    return out


# version 07
# speedup vs baseline: 1.2130x; 1.2130x over previous
import contextlib
import numpy as np
import ml_dtypes
import concourse.bass as bass
import concourse.mybir as mybir
from concourse.bass_utils import run_bass_kernel_spmd
from concourse.alu_op_type import AluOpType

F32 = mybir.dt.float32
BF16 = mybir.dt.bfloat16
FP16 = mybir.dt.float16
FP8 = mybir.dt.float8e4
AF = mybir.ActivationFunctionType
MUL = AluOpType.mult
ADD = AluOpType.add
SUB = AluOpType.subtract

D = 1024
L = 2048
DIN = 2048
DS = 16
H = 4096
NH = 16
DH = 64
NFT = D // 128
NDG = DIN // 128
TCM = 512
NCM = L // TCM
EPS = 1e-5

DEBUG = False
STAB_A = True
STAB_B = True
STAB_RED = True
DEFER = False


def build(debug=False):
    nc = bass.Bass("TRN2", target_bir_lowering=False)

    def din(name, shape, dt=F32):
        return nc.dram_tensor(name, list(shape), dt, kind="ExternalInput").ap()

    X = din("x", [L, D])
    IPW = din("ipw", [32, 128, 1024])
    OPW = din("opw", [8, 128, 2048])
    XPW = din("xpw", [128, NDG * 96])
    DTW = din("dtw", [64, DIN])
    CONVW = din("convw", [128, NDG * 4])
    CONVB = din("convb", [128, NDG])
    DTB = din("dtb", [128, NDG])
    ALOG = din("alog", [128, NDG * DS])
    DDD = din("dd", [128, NDG])
    RMSW = din("rmsw", [128, NFT])
    INDV = din("indv", [96, 32], BF16)
    W1T = din("w1t", [32, 128, 1024])
    B1 = din("b1", [128, 32])
    W2T = din("w2t", [8, 128, 4096])
    B2 = din("b2", [128, NFT])
    WQKT = din("wqkt", [16, 128, 1024])
    WVR = din("wvr", [8, 128, 1024])
    BQ = din("bq", [128, NFT])
    BK = din("bk", [128, NFT])
    BV = din("bv", [1, D])
    WOUTT = din("woutt", [8, 128, 1024])
    BOUT = din("bout", [128, NFT])
    L1WT = din("l1wt", [8, 128, 1024])
    IDF = din("idf", [128, 128], F32)
    IDB = din("idb", [128, 128], BF16)
    OUT = nc.dram_tensor("out", [L, D], F32, kind="ExternalOutput").ap()

    dbg = {}
    if debug:
        def dout(name, shape, dt=F32):
            dbg[name] = nc.dram_tensor(name, list(shape), dt, kind="ExternalOutput").ap()
        dout("d_xnb", [128, NFT * TCM])
        dout("d_y2", [128, NDG * TCM])
        dout("d_x1", [128, NFT * L])
        dout("d_x2", [128, NFT * L])
        dout("d_x3", [128, NFT * L])
        dout("d_q", [128, 8 * L])
        dout("d_vt", [128, 16 * 1040])
        dout("d_ao", [128, 8 * L])
        dout("d2_xnb", [128, NFT * TCM])
        dout("d2_stat", [96, TCM])
        dout("d2_xib", [128, NDG * (TCM + 3)])
        dout("d2_dbc", [96, TCM])
        dout("d2_y2", [128, NDG * TCM])
        dout("d2_of", [128, NFT * TCM])
        dout("d_ofa", [128, NFT * TCM])
        dout("d_den", [96, 1024])

    es = contextlib.ExitStack()

    def sb(name, shape, dt):
        return es.enter_context(nc.sbuf_tensor(name, list(shape), dt))

    def psum(name, shape, dt=F32):
        return es.enter_context(nc.psum_tensor(name, list(shape), dt))

    xfm = sb("xfm", [128, NFT * L], F32)
    consts = sb("consts", [128, 12], F32)
    onesf = sb("onesf", [1, 128], F32)
    onesb = sb("onesb", [1, 128], BF16)
    onescf = sb("onescf", [128, 1], F32)
    onescb = sb("onescb", [128, 1], BF16)
    onesch = sb("onesch", [128, 1], FP16)
    onesA = sb("onesA", [96, 128], F32)
    idf_sb = sb("idf_sb", [128, 128], F32)
    idb_sb = sb("idb_sb", [128, 128], BF16)
    statrA = sb("statrA", [96, TCM], F32)
    statrB = sb("statrB", [96, TCM], F32)
    statrC = sb("statrC", [96, TCM], F32)
    ipw_m = sb("ipw_m", [128, 2 * 1024], BF16)
    sq_sb = sb("sq_sb", [128, TCM], F32)
    xio = sb("xio", [128, D], F32)

    ps_mm = psum("ps_mm", [128, 2 * TCM])
    ps_o2 = psum("ps_o2", [128, 2 * TCM])
    ps_y = psum("ps_y", [128, TCM])
    ps_bc = psum("ps_bc", [128, TCM])
    ps_dt = psum("ps_dt", [128, TCM])
    ps_sx = psum("ps_sx", [128, TCM])
    ps_s1 = ps_bc[0:1, :]
    ps_s2 = ps_y[0:1, :]
    ps_dbcv = ps_sx[0:96, :]

    es_m = contextlib.ExitStack()

    def sbm(name, shape, dt):
        return es_m.enter_context(nc.sbuf_tensor(name, list(shape), dt))

    A_sb = sbm("A_sb", [128, NDG * DS], F32)
    convw_sb = sbm("convw_sb", [128, NDG * 4], F32)
    convb_sb = sbm("convb_sb", [128, NDG], F32)
    dtb_sb = sbm("dtb_sb", [128, NDG], F32)
    D_sb = sbm("D_sb", [128, NDG], F32)
    rmsw_sb = sbm("rmsw_sb", [128, NFT], F32)
    xpw_sb = sbm("xpw_sb", [128, NDG * 96], BF16)
    dtw_sb = sbm("dtw_sb", [64, DIN], BF16)
    xnbc = sbm("xnbc", [128, NFT * TCM], BF16)
    opw_m = sbm("opw_m", [128, 2 * 2048], BF16)
    xib = sbm("xib", [128, NDG * (TCM + 3)], BF16)
    xtail = sbm("xtail", [128, NDG * 3], F32)
    zsb = sbm("zsb", [128, NDG * TCM], BF16)
    Brep = sbm("BrepT", [128, DS * TCM], BF16)
    Crep = sbm("CrepT", [128, DS * TCM], BF16)
    y2b = sbm("y2b", [128, NDG * TCM], BF16)
    dbc_sb = sbm("dbc_sb", [96, TCM], BF16)
    delta_sb = sbm("delta_sb", [128, TCM], F32)
    u_sb = sbm("u_sb", [128, TCM], BF16)
    a_sb = sbm("a_sb", [128, 2 * TCM], F32)
    b_sb = sbm("b_sb", [128, 2 * TCM], BF16)
    h_sb = sbm("h_sb", [128, 2 * TCM], BF16)
    hc_sb = sbm("hc_sb", [128, 2 * TCM], BF16)
    h0_sb = sbm("h0_sb", [128, NDG * DS], F32)
    rrep_sb = sbm("rrep_sb", [128, TCM], F32)
    mrep_sb = sbm("mrep_sb", [128, TCM], F32)
    indv_sb = sbm("indv_sb", [96, 32], BF16)
    of16 = xnbc

    blk_real = es.enter_context(nc.Block())
    sems = {}
    C = {}
    for s in ["SD", "GD", "TE", "VE", "SE"]:
        sems[s] = es.enter_context(nc.semaphore(s))
        C[s] = 0

    _replay = [False]

    def inc(instr, s, n=1):
        instr.then_inc(sems[s], n)
        if not _replay[0]:
            C[s] += n
        return C[s]

    class _Dummy:
        def then_inc(self, *a, **k):
            return self

    _DUM = _Dummy()

    class _Mock:
        def __getattr__(self, name):
            return lambda *a, **k: _DUM

    _MOCK = _Mock()
    _PEND = {k: [] for k in ("sync", "gpsimd", "tensor", "vector", "scalar")}

    class _EngDec:
        def __init__(self, key):
            self.key = key

        def __call__(self, fn):
            if DEFER is True or DEFER == self.key:
                _PEND[self.key].append(fn)
                fn(_MOCK)
            else:
                getattr(blk_real, self.key)(fn)

    class _BlkProxy:
        sync = _EngDec("sync")
        gpsimd = _EngDec("gpsimd")
        tensor = _EngDec("tensor")
        vector = _EngDec("vector")
        scalar = _EngDec("scalar")

    blk = _BlkProxy()

    @blk.sync
    def _(e):
        e.dma_start(out=A_sb[:], in_=ALOG[:]).then_inc(sems["SD"], 16)
        e.dma_start(out=convw_sb[:], in_=CONVW[:]).then_inc(sems["SD"], 16)
        e.dma_start(out=convb_sb[:], in_=CONVB[:]).then_inc(sems["SD"], 16)
        e.dma_start(out=dtb_sb[:], in_=DTB[:]).then_inc(sems["SD"], 16)
        e.dma_start(out=D_sb[:], in_=DDD[:]).then_inc(sems["SD"], 16)
        e.dma_start(out=rmsw_sb[:], in_=RMSW[:]).then_inc(sems["SD"], 16)
        e.dma_start(out=idf_sb[:], in_=IDF[:]).then_inc(sems["SD"], 16)
        e.dma_start(out=idb_sb[:], in_=IDB[:]).then_inc(sems["SD"], 16)
        e.dma_start(out=indv_sb[:], in_=INDV[:]).then_inc(sems["SD"], 16)
    C["SD"] += 16 * 9
    sd_init = C["SD"]

    @blk.gpsimd
    def _(e):
        e.dma_start(out=xpw_sb[:], in_=XPW[:]).then_inc(sems["GD"], 16)
        e.dma_start(out=dtw_sb[:], in_=DTW[:]).then_inc(sems["GD"], 16)
    C["GD"] += 32
    gd_init = C["GD"]

    @blk.vector
    def _(e):
        for i, v in enumerate([1.0, -1.0, 0.125, 1.0 / 1024.0, EPS, -0.5, 0.0, 2.0,
                               -44.0, 2.0, 88.0, 0.5]):
            e.memset(consts[:, i:i + 1], v)
        e.memset(onesf[:], 1.0)
        e.memset(onesb[:], 1.0)
        e.memset(onescf[:], 1.0)
        e.memset(onescb[:], 1.0)
        e.memset(onesch[:], 1.0)
        e.memset(onesA[:], 1.0)
        e.memset(h0_sb[:], 0.0)
        e.memset(xtail[:], 0.0)
        inc(e.nop(), "VE")
    (C_ONE, C_NEG1, C_QS, C_I1024, C_EPS, C_NHALF, C_ZERO, C_TWO,
     C_M16, C_EIGHT, C_128, C_SPARE) = [consts[:, i:i + 1] for i in range(12)]

    @blk.scalar
    def _(e):
        e.wait_ge(sems["SD"], sd_init)
        inc(e.activation(A_sb[:], A_sb[:], AF.Exp), "SE")
    se_A = C["SE"]
    @blk.vector
    def _(e):
        e.wait_ge(sems["SE"], se_A)
        inc(e.tensor_scalar(A_sb[:], A_sb[:], C_NEG1, None, op0=MUL), "VE")
    ve_A = C["VE"]

    trv = [ps_mm[:, 0:128], ps_mm[:, TCM:TCM + 128],
           ps_o2[:, 0:128], ps_o2[:, TCM:TCM + 128],
           ps_y[:, 0:128], ps_bc[:, 0:128], ps_dt[:, 0:128]]
    trv_se = [0] * 7

    xio_v = [xio[:, 0:D], xio[:, 0:D]]
    Xt = X.rearrange("(n p) m -> n p m", p=128)
    OUTt = OUT.rearrange("(n p) m -> n p m", p=128)

    xio_te = [0, 0]
    for i in range(16):
        bi = i % 2
        @blk.sync
        def _(e, i=i, bi=bi, _xio_te_bi=xio_te[bi]):
            if _xio_te_bi:
                e.wait_ge(sems["TE"], _xio_te_bi)
            e.dma_start(out=xio_v[bi][:], in_=Xt[i]).then_inc(sems["SD"], 16)
        C["SD"] += 16
        sd_t = C["SD"]
        for f in range(NFT):
            q = (i * NFT + f) % 7
            @blk.tensor
            def _(e, i=i, f=f, q=q, bi=bi, sd_t=sd_t, _trv_se_q=trv_se[q]):
                if f == 0:
                    e.wait_ge(sems["SD"], sd_t)
                    if i == 0:
                        e.wait_ge(sems["SD"], sd_init)
                        e.wait_ge(sems["VE"], ve_A)
                if _trv_se_q:
                    e.wait_ge(sems["SE"], _trv_se_q)
                inc(e.transpose(trv[q][:], xio_v[bi][:, f * 128:(f + 1) * 128], idf_sb[:]), "TE")
            te_n = C["TE"]
            @blk.scalar
            def _(e, i=i, f=f, q=q, te_n=te_n):
                e.wait_ge(sems["TE"], te_n)
                inc(e.activation(xfm[:, f * L + i * 128:f * L + (i + 1) * 128], trv[q][:], AF.Identity), "SE")
            trv_se[q] = C["SE"]
        xio_te[0] = xio_te[1] = C["TE"]
    se_xtr = C["SE"]

    A_R, A_T0, A_BS = 0, 32, 64
    B_RS, B_T1, B_CS = 0, 32, 64
    C_M, C_V, C_RP = 0, 32, 64

    def rA(r, w):
        return statrA[r:r + 1, :w]

    def rB(r, w):
        return statrB[r:r + 1, :w]

    def rC(r, w):
        return statrC[r:r + 1, :w]

    def emit_ln_stats(src_tiles, width, ones_src, extra_rs=False):
        for f in range(NFT):
            @blk.scalar
            def _(e, f=f, _cve=C["VE"], _cte=C["TE"]):
                e.wait_ge(sems["VE"], _cve)
                if f > 0:
                    e.wait_ge(sems["TE"], _cte)
                inc(e.activation(sq_sb[:, :width], src_tiles[f], AF.Square), "SE")
            se_f = C["SE"]
            @blk.tensor
            def _(e, f=f, se_f=se_f):
                e.wait_ge(sems["SE"], se_f)
                e.matmul(ps_s1[0:1, :width], ones_src, src_tiles[f], start=(f == 0), stop=(f == NFT - 1))
                inc(e.matmul(ps_s2[0:1, :width], onescf[:], sq_sb[:, :width],
                             start=(f == 0), stop=(f == NFT - 1)), "TE")
        te_stats = C["TE"]
        @blk.vector
        def _(e):
            e.wait_ge(sems["TE"], te_stats)
            e.tensor_scalar(rC(C_M, width), ps_s1[0:1, :width], C_I1024[0:1, :], None, op0=MUL)
            e.tensor_scalar(rA(A_T0, width), ps_s2[0:1, :width], C_I1024[0:1, :], None, op0=MUL)
            e.tensor_tensor(rB(B_T1, width), rC(C_M, width), rC(C_M, width), op=MUL)
            inc(e.tensor_tensor(rC(C_V, width), rA(A_T0, width), rB(B_T1, width), op=SUB), "VE")
        ve_v = C["VE"]
        @blk.scalar
        def _(e):
            e.wait_ge(sems["VE"], ve_v)
            e.activation(rA(A_T0, width), rC(C_V, width), AF.Ln, bias=C_EPS[0:1, :], scale=1.0)
            i1 = e.activation(rA(A_R, width), rA(A_T0, width), AF.Exp, scale=C_NHALF[0:1, :])
            if extra_rs:
                i1 = e.activation(rB(B_T1, width), rC(C_V, width), AF.Ln)
            inc(i1, "SE")
        se_r = C["SE"]
        if extra_rs:
            @blk.vector
            def _(e):
                e.wait_ge(sems["SE"], se_r)
                inc(e.tensor_tensor(rC(C_V, width), rB(B_T1, width), rA(A_T0, width), op=SUB), "VE")
            ve_q = C["VE"]
            @blk.scalar
            def _(e):
                e.wait_ge(sems["VE"], ve_q)
                e.activation(rB(B_T1, width), rC(C_V, width), AF.Exp)
                e.activation(rB(B_T1, width), rB(B_T1, width), AF.Ln, bias=C_EPS[0:1, :], scale=1.0)
                inc(e.activation(rB(B_RS, width), rB(B_T1, width), AF.Exp, scale=C_NHALF[0:1, :]), "SE")
        @blk.vector
        def _(e, _cse=C["SE"]):
            e.wait_ge(sems["SE"], _cse)
            inc(e.nop(), "VE")
        return C["VE"]

    def emit_bcast(row_ap, width, dst, ve_ready, base=0):
        @blk.tensor
        def _(e, ve_ready=ve_ready):
            e.wait_ge(sems["VE"], ve_ready)
            for n in range((width + 511) // 512):
                sl = slice(n * 512, min(width, (n + 1) * 512))
                mm = e.matmul(dst[:, sl], onesA[base:base + 1, :], row_ap[:, sl], start=True, stop=True)
            inc(mm, "TE")
        return C["TE"]

    def emit_preln(t0, dst_fn):
        src_t = [xfm[:, f * L + t0: f * L + t0 + TCM] for f in range(NFT)]
        ve_mr = emit_ln_stats(src_t, TCM, onescf[:], extra_rs=False)
        emit_bcast(rC(C_M, TCM), TCM, ps_dt, ve_mr, base=C_M)
        te_b = emit_bcast(rA(A_R, TCM), TCM, ps_bc, ve_mr, base=A_R)
        for f in range(NFT):
            @blk.vector
            def _(e, f=f, te_b=te_b, _cse=C["SE"]):
                if f == 0:
                    e.wait_ge(sems["TE"], te_b)
                    e.wait_ge(sems["SE"], _cse)
                e.tensor_tensor(sq_sb[:, :TCM], src_t[f], ps_dt[:, :TCM], op=SUB)
                i = e.tensor_tensor(dst_fn(f), sq_sb[:, :TCM], ps_bc[:, :TCM], op=MUL)
                if f == NFT - 1:
                    inc(i, "VE")
        return C["VE"]

    def emit_postln_resid(of_tiles, t0, ones_src):
        emit_ln_stats(of_tiles, TCM, ones_src, extra_rs=False)
        ve_mr2 = C["VE"]
        emit_bcast(rA(A_R, TCM), TCM, ps_bc, ve_mr2, base=A_R)
        te_b2 = emit_bcast(rC(C_M, TCM), TCM, ps_dt, ve_mr2, base=C_M)
        for f in range(NFT):
            @blk.vector
            def _(e, f=f, te_b2=te_b2, _cse=C["SE"]):
                if f == 0:
                    e.wait_ge(sems["TE"], te_b2)
                    e.wait_ge(sems["SE"], _cse)
                e.tensor_tensor(sq_sb[:, :TCM], of_tiles[f], ps_dt[:, :TCM], op=SUB)
                e.tensor_tensor(sq_sb[:, :TCM], sq_sb[:, :TCM], ps_bc[:, :TCM], op=MUL)
                i = e.tensor_tensor(xfm[:, f * L + t0: f * L + t0 + TCM],
                                    xfm[:, f * L + t0: f * L + t0 + TCM], sq_sb[:, :TCM], op=ADD)
                if f == NFT - 1:
                    inc(i, "VE")
        return C["VE"]

    scan_done = [0, 0]
    y_done = [0, 0]
    for c in range(NCM):
        t0 = c * TCM
        src = [xfm[:, f * L + t0: f * L + t0 + TCM] for f in range(NFT)]

        emit_ln_stats(src, TCM, onescf[:], extra_rs=True)
        @blk.vector
        def _(e, _cse=C["SE"]):
            e.wait_ge(sems["SE"], _cse)
            inc(e.tensor_tensor(rC(C_RP, TCM), rA(A_R, TCM), rB(B_RS, TCM), op=MUL), "VE")
        ve_rp = C["VE"]
        emit_bcast(rC(C_RP, TCM), TCM, ps_bc, ve_rp, base=C_RP)
        emit_bcast(rC(C_M, TCM), TCM, ps_dt, ve_rp, base=C_M)
        te_bc = emit_bcast(rA(A_R, TCM), TCM, ps_y, ve_rp, base=A_R)
        @blk.scalar
        def _(e, te_bc=te_bc):
            e.wait_ge(sems["TE"], te_bc)
            e.activation(rrep_sb[:], ps_y[:, :TCM], AF.Identity)
            inc(e.activation(mrep_sb[:], ps_dt[:, :TCM], AF.Identity), "SE")
        se_rm = C["SE"]
        for f in range(NFT):
            @blk.vector
            def _(e, f=f, te_bc=te_bc, _cse=C["SE"]):
                if f == 0:
                    e.wait_ge(sems["TE"], te_bc)
                    e.wait_ge(sems["SE"], _cse)
                e.tensor_tensor(sq_sb[:, :TCM], src[f], ps_dt[:, :TCM], op=SUB)
                e.tensor_tensor(sq_sb[:, :TCM], sq_sb[:, :TCM], ps_bc[:, :TCM], op=MUL)
                i = e.tensor_scalar(xnbc[:, f * TCM:(f + 1) * TCM], sq_sb[:, :TCM],
                                    rmsw_sb[:, f:f + 1], None, op0=MUL)
                if f == NFT - 1:
                    inc(i, "VE")
        ve_xnb = C["VE"]
        if debug and c == 0:
            @blk.gpsimd
            def _(e):
                e.wait_ge(sems["VE"], ve_xnb)
                e.dma_start(out=dbg["d_xnb"][:], in_=xnbc[:]).then_inc(sems["GD"], 16)
            C["GD"] += 16
        if debug and c == 1:
            @blk.gpsimd
            def _(e):
                e.wait_ge(sems["VE"], ve_xnb)
                e.dma_start(out=dbg["d2_xnb"][:], in_=xnbc[:]).then_inc(sems["GD"], 16)
                e.dma_start(out=dbg["d2_stat"][:], in_=statrC[:]).then_inc(sems["GD"], 16)
            C["GD"] += 32

        ipw_v = [ipw_m[:, 0:1024], ipw_m[:, 1024:2048]]
        mm_done = [0, 0]
        evac_ve = [0, 0]
        evac_se = [0, 0]
        se_silu_prev = 0
        for m in range(32):
            buf = m % 2
            @blk.gpsimd
            def _(e, m=m, buf=buf, _mm_done_buf=mm_done[buf]):
                if m < 2:
                    e.wait_ge(sems["GD"], gd_init)
                if _mm_done_buf:
                    e.wait_ge(sems["TE"], _mm_done_buf)
                e.dma_start(out=ipw_v[buf][:], in_=IPW[m]).then_inc(sems["GD"], 16)
            C["GD"] += 16
            gd_t = C["GD"]
            pshalf = ps_mm[:, buf * TCM:(buf + 1) * TCM]
            @blk.tensor
            def _(e, m=m, buf=buf, gd_t=gd_t, pshalf=pshalf, ve_xnb=ve_xnb, _evac_se_buf=evac_se[buf], _evac_ve_buf=evac_ve[buf]):
                e.wait_ge(sems["GD"], gd_t)
                if m == 0:
                    e.wait_ge(sems["VE"], ve_xnb)
                if _evac_se_buf:
                    e.wait_ge(sems["SE"], _evac_se_buf)
                if _evac_ve_buf:
                    e.wait_ge(sems["VE"], _evac_ve_buf)
                for k in range(8):
                    mm = e.matmul(pshalf, ipw_v[buf][:, k * 128:(k + 1) * 128],
                                  xnbc[:, k * TCM:(k + 1) * TCM], start=(k == 0), stop=(k == 7))
                inc(mm, "TE")
            mm_done[buf] = C["TE"]
            if m < 16:
                dg = m
                @blk.vector
                def _(e, dg=dg, buf=buf, pshalf=pshalf, ssp=se_silu_prev, _mm_done_buf=mm_done[buf]):
                    e.wait_ge(sems["TE"], _mm_done_buf)
                    if ssp:
                        e.wait_ge(sems["SE"], ssp)
                    w3 = convw_sb[:, dg * 4 + 3: dg * 4 + 4]
                    e.tensor_scalar(sq_sb[:, :TCM], pshalf, w3, None, op0=MUL)
                    for kk in range(3):
                        sh = 3 - kk
                        wk = convw_sb[:, dg * 4 + kk: dg * 4 + kk + 1]
                        e.scalar_tensor_tensor(sq_sb[:, 0:sh], xtail[:, dg * 3 + kk: dg * 3 + 3], wk,
                                               sq_sb[:, 0:sh], op0=MUL, op1=ADD)
                        e.scalar_tensor_tensor(sq_sb[:, sh:TCM], pshalf[:, 0:TCM - sh], wk,
                                               sq_sb[:, sh:TCM], op0=MUL, op1=ADD)
                    inc(e.tensor_copy(xtail[:, dg * 3: dg * 3 + 3], pshalf[:, TCM - 3:TCM]), "VE")
                evac_ve[buf] = C["VE"]
                @blk.scalar
                def _(e, dg=dg, buf=buf, _evac_ve_buf=evac_ve[buf]):
                    e.wait_ge(sems["VE"], _evac_ve_buf)
                    inc(e.activation(xib[:, dg * (TCM + 3) + 3: (dg + 1) * (TCM + 3)], sq_sb[:, :TCM],
                                     AF.Silu, bias=convb_sb[:, dg:dg + 1], scale=1.0), "SE")
                evac_se[buf] = C["SE"]
                se_silu_prev = C["SE"]
            else:
                dg = m - 16
                @blk.scalar
                def _(e, dg=dg, buf=buf, pshalf=pshalf, _mm_done_buf=mm_done[buf]):
                    e.wait_ge(sems["TE"], _mm_done_buf)
                    inc(e.activation(zsb[:, dg * TCM:(dg + 1) * TCM], pshalf, AF.Silu), "SE")
                evac_se[buf] = C["SE"]
                evac_ve[buf] = 0
        se_inproj = C["SE"]

        @blk.tensor
        def _(e):
            e.wait_ge(sems["SE"], se_inproj)
            for dg in range(NDG):
                mm = e.matmul(ps_dbcv[:, :], xpw_sb[:, dg * 96:(dg + 1) * 96],
                              xib[:, dg * (TCM + 3) + 3:(dg + 1) * (TCM + 3)],
                              start=(dg == 0), stop=(dg == NDG - 1))
            inc(mm, "TE")
        te_dbc = C["TE"]
        @blk.scalar
        def _(e, te_dbc=te_dbc):
            e.wait_ge(sems["TE"], te_dbc)
            inc(e.activation(dbc_sb[:], ps_dbcv[:, :], AF.Identity), "SE")
        se_dbc = C["SE"]

        se_stage = 0
        for s in range(DS):
            @blk.tensor
            def _(e, s=s, se_stage=se_stage, _cve=C["VE"]):
                if s == 0:
                    e.wait_ge(sems["SE"], se_dbc)
                    e.wait_ge(sems["VE"], _cve)
                else:
                    e.wait_ge(sems["SE"], se_stage)
                e.matmul(ps_dt[0:1, :], indv_sb[:, s:s + 1], dbc_sb[:, :], start=True, stop=True)
                inc(e.matmul(ps_dt[32:33, :], indv_sb[:, 16 + s:17 + s], dbc_sb[:, :], start=True, stop=True), "TE")
            te_ex = C["TE"]
            @blk.scalar
            def _(e, s=s, te_ex=te_ex):
                e.wait_ge(sems["TE"], te_ex)
                e.activation(rA(A_BS, TCM), ps_dt[0:1, :], AF.Identity)
                inc(e.activation(rB(B_CS, TCM), ps_dt[32:33, :], AF.Identity), "SE")
            se_stage = C["SE"]
            @blk.tensor
            def _(e, s=s, se_stage=se_stage):
                e.wait_ge(sems["SE"], se_stage)
                e.matmul(ps_bc[:, :], onesA[A_BS:A_BS + 1, :], rA(A_BS, TCM), start=True, stop=True)
                inc(e.matmul(ps_y[:, :], onesA[B_CS:B_CS + 1, :], rB(B_CS, TCM), start=True, stop=True), "TE")
            te_b = C["TE"]
            @blk.scalar
            def _(e, s=s, te_b=te_b):
                e.wait_ge(sems["TE"], te_b)
                e.activation(Brep[:, s * TCM:(s + 1) * TCM], ps_bc[:, :], AF.Identity)
                inc(e.activation(Crep[:, s * TCM:(s + 1) * TCM], ps_y[:, :], AF.Identity), "SE")
        se_bcrep = C["SE"]

        ve_u_prev = 0
        for dg in range(NDG):
            @blk.tensor
            def _(e, dg=dg, _cse=C["SE"]):
                if dg == 0:
                    e.wait_ge(sems["SE"], se_bcrep)
                else:
                    e.wait_ge(sems["SE"], _cse)
                inc(e.matmul(ps_dt[:, :], dtw_sb[:, dg * 128:(dg + 1) * 128], dbc_sb[0:64, :],
                             start=True, stop=True), "TE")
            te_dt = C["TE"]
            @blk.scalar
            def _(e, dg=dg, te_dt=te_dt, vup=ve_u_prev):
                e.wait_ge(sems["TE"], te_dt)
                if vup:
                    e.wait_ge(sems["VE"], vup)
                e.activation(delta_sb[:], ps_dt[:, :], AF.Exp, bias=dtb_sb[:, dg:dg + 1], scale=1.0)
                inc(e.activation(delta_sb[:], delta_sb[:], AF.Ln, bias=C_ONE, scale=1.0), "SE")
            se_delta = C["SE"]
            @blk.vector
            def _(e, dg=dg, se_delta=se_delta):
                e.wait_ge(sems["SE"], se_delta)
                inc(e.tensor_tensor(u_sb[:], delta_sb[:],
                                    xib[:, dg * (TCM + 3) + 3:(dg + 1) * (TCM + 3)], op=MUL), "VE")
            ve_u = C["VE"]
            ve_u_prev = ve_u

            for s in range(DS):
                sl = s % 2
                asl = a_sb[:, sl * TCM:(sl + 1) * TCM]
                bsl = b_sb[:, sl * TCM:(sl + 1) * TCM]
                hsl = h_sb[:, sl * TCM:(sl + 1) * TCM]
                hcsl = hc_sb[:, sl * TCM:(sl + 1) * TCM]
                @blk.scalar
                def _(e, dg=dg, s=s, sl=sl, asl=asl, sd=scan_done[sl]):
                    if sd:
                        e.wait_ge(sems["VE"], sd)
                    inc(e.activation(asl, delta_sb[:], AF.Exp,
                                     scale=A_sb[:, dg * DS + s: dg * DS + s + 1]), "SE")
                se_a = C["SE"]
                @blk.vector
                def _(e, dg=dg, s=s, sl=sl, se_a=se_a, asl=asl, bsl=bsl, hsl=hsl, hcsl=hcsl,
                      ve_u=ve_u, yd=y_done[sl]):
                    if s == 0:
                        e.wait_ge(sems["VE"], ve_u)
                    if yd:
                        e.wait_ge(sems["TE"], yd)
                    e.tensor_tensor(bsl, u_sb[:], Brep[:, s * TCM:(s + 1) * TCM], op=MUL)
                    e.wait_ge(sems["SE"], se_a)
                    e.tensor_tensor_scan(hsl, asl, bsl, h0_sb[:, dg * DS + s: dg * DS + s + 1],
                                         op0=MUL, op1=ADD)
                    e.tensor_copy(h0_sb[:, dg * DS + s: dg * DS + s + 1], hsl[:, TCM - 1:TCM])
                    inc(e.tensor_tensor(hcsl, hsl, Crep[:, s * TCM:(s + 1) * TCM], op=MUL), "VE")
                scan_done[sl] = C["VE"]
                @blk.tensor
                def _(e, s=s, sl=sl, hcsl=hcsl, vs=scan_done[sl]):
                    e.wait_ge(sems["VE"], vs)
                    inc(e.matmul(ps_y[:, :], idb_sb[:], hcsl, start=(s == 0), stop=(s == DS - 1)), "TE")
                y_done[sl] = C["TE"]
            te_y = C["TE"]
            @blk.vector
            def _(e, dg=dg, te_y=te_y):
                e.wait_ge(sems["TE"], te_y)
                e.scalar_tensor_tensor(sq_sb[:, :TCM], xib[:, dg * (TCM + 3) + 3:(dg + 1) * (TCM + 3)],
                                       D_sb[:, dg:dg + 1], ps_y[:, :], op0=MUL, op1=ADD)
                inc(e.tensor_tensor(y2b[:, dg * TCM:(dg + 1) * TCM], sq_sb[:, :TCM],
                                    zsb[:, dg * TCM:(dg + 1) * TCM], op=MUL), "VE")
        ve_y2 = C["VE"]
        if debug and c == 0:
            @blk.gpsimd
            def _(e):
                e.wait_ge(sems["VE"], ve_y2)
                e.dma_start(out=dbg["d_y2"][:], in_=y2b[:]).then_inc(sems["GD"], 16)
            C["GD"] += 16
        if debug and c == 1:
            @blk.gpsimd
            def _(e):
                e.wait_ge(sems["VE"], ve_y2)
                e.dma_start(out=dbg["d2_y2"][:], in_=y2b[:]).then_inc(sems["GD"], 16)
                e.dma_start(out=dbg["d2_xib"][:], in_=xib[:]).then_inc(sems["GD"], 16)
                e.dma_start(out=dbg["d2_dbc"][:], in_=dbc_sb[:]).then_inc(sems["GD"], 16)
            C["GD"] += 48

        opw_v = [opw_m[:, 0:2048], opw_m[:, 2048:4096]]
        opmm_done = [0, 0]
        of_done = [0, 0]
        ps_op = [ps_mm[:, 0:TCM], ps_mm[:, TCM:2 * TCM]]
        for m in range(NFT):
            buf = m % 2
            @blk.gpsimd
            def _(e, m=m, buf=buf, _opmm_done_buf=opmm_done[buf]):
                if _opmm_done_buf:
                    e.wait_ge(sems["TE"], _opmm_done_buf)
                e.dma_start(out=opw_v[buf][:], in_=OPW[m]).then_inc(sems["GD"], 16)
            C["GD"] += 16
            gd_t = C["GD"]
            @blk.tensor
            def _(e, m=m, buf=buf, gd_t=gd_t, ve_y2=ve_y2, _of_done_buf=of_done[buf]):
                e.wait_ge(sems["GD"], gd_t)
                if m == 0:
                    e.wait_ge(sems["VE"], ve_y2)
                if _of_done_buf:
                    e.wait_ge(sems["VE"], _of_done_buf)
                for k in range(16):
                    mm = e.matmul(ps_op[buf], opw_v[buf][:, k * 128:(k + 1) * 128],
                                  y2b[:, k * TCM:(k + 1) * TCM], start=(k == 0), stop=(k == 15))
                inc(mm, "TE")
            opmm_done[buf] = C["TE"]
            @blk.vector
            def _(e, m=m, buf=buf, c=c, se_rm=se_rm, _opmm_done_buf=opmm_done[buf]):
                e.wait_ge(sems["TE"], _opmm_done_buf)
                if m == 0:
                    e.wait_ge(sems["SE"], se_rm)
                e.tensor_tensor(sq_sb[:, :TCM], xfm[:, m * L + t0: m * L + t0 + TCM], mrep_sb[:], op=SUB)
                e.tensor_tensor(sq_sb[:, :TCM], sq_sb[:, :TCM], rrep_sb[:], op=MUL)
                inc(e.tensor_tensor(of16[:, m * TCM:(m + 1) * TCM], sq_sb[:, :TCM], ps_op[buf], op=ADD), "VE")
            of_done[buf] = C["VE"]

        if debug and c == 1:
            @blk.gpsimd
            def _(e, _cve=C["VE"]):
                e.wait_ge(sems["VE"], _cve)
                e.dma_start(out=dbg["d2_of"][:], in_=of16[:]).then_inc(sems["GD"], 16)
            C["GD"] += 16
        of_t = [of16[:, f * TCM:(f + 1) * TCM] for f in range(NFT)]
        emit_postln_resid(of_t, t0, onescb[:])
    ve_mamba = C["VE"]
    if debug:
        @blk.gpsimd
        def _(e):
            e.wait_ge(sems["VE"], ve_mamba)
            e.dma_start(out=dbg["d_x1"][:], in_=xfm[:]).then_inc(sems["GD"], 16)
        C["GD"] += 16

    es_m.close()

    es_f = contextlib.ExitStack()

    def sbf(name, shape, dt):
        return es_f.enter_context(nc.sbuf_tensor(name, list(shape), dt))

    xnb2 = sbf("xnb2", [128, NFT * TCM], BF16)
    offq = sbf("offq", [128, NFT * TCM], FP16)
    hq_sb = sbf("hq_sb", [128, 32 * TCM], FP8)
    wq_m = sbf("wq_m", [128, 2 * H], FP8)
    b1_sb = sbf("b1_sb", [128, 32], F32)
    b2q_sb = sbf("b2q_sb", [128, NFT], F32)
    b2h_sb = sbf("b2h_sb", [128, NFT], FP16)

    @blk.sync
    def _(e):
        e.wait_ge(sems["VE"], ve_mamba)
        e.dma_start(out=b1_sb[:], in_=B1[:]).then_inc(sems["SD"], 16)
        e.dma_start(out=b2q_sb[:], in_=B2[:]).then_inc(sems["SD"], 16)
    C["SD"] += 32
    sd_f = C["SD"]
    @blk.scalar
    def _(e):
        e.wait_ge(sems["SD"], sd_f)
        e.activation(b2h_sb[:], b2q_sb[:], AF.Identity)
        inc(e.activation(b2q_sb[:], b2h_sb[:], AF.Identity), "SE")
    se_b2q = C["SE"]

    for c in range(NCM):
        t0 = c * TCM
        ve_xn2 = emit_preln(t0, lambda f: xnb2[:, f * TCM:(f + 1) * TCM])
        mmf_done = [0, 0]
        epi_se = [0, 0]
        for m in range(32):
            buf = m % 2
            @blk.gpsimd
            def _(e, m=m, buf=buf, c=c, _mmf_done_buf=mmf_done[buf]):
                if c == 0 and m < 2:
                    e.wait_ge(sems["VE"], ve_mamba)
                if _mmf_done_buf:
                    e.wait_ge(sems["TE"], _mmf_done_buf)
                e.dma_start(out=ipw_m[:, buf * 1024:(buf + 1) * 1024], in_=W1T[m]).then_inc(sems["GD"], 16)
            C["GD"] += 16
            gd_t = C["GD"]
            pshalf = ps_mm[:, buf * TCM:(buf + 1) * TCM]
            @blk.tensor
            def _(e, m=m, buf=buf, gd_t=gd_t, pshalf=pshalf, ve_xn2=ve_xn2, _epi_se_buf=epi_se[buf]):
                e.wait_ge(sems["GD"], gd_t)
                if m == 0:
                    e.wait_ge(sems["VE"], ve_xn2)
                if _epi_se_buf:
                    e.wait_ge(sems["SE"], _epi_se_buf)
                for k in range(8):
                    mm = e.matmul(pshalf, ipw_m[:, buf * 1024 + k * 128: buf * 1024 + (k + 1) * 128],
                                  xnb2[:, k * TCM:(k + 1) * TCM], start=(k == 0), stop=(k == 7))
                inc(mm, "TE")
            mmf_done[buf] = C["TE"]
            @blk.scalar
            def _(e, m=m, buf=buf, pshalf=pshalf, _mmf_done_buf=mmf_done[buf]):
                e.wait_ge(sems["TE"], _mmf_done_buf)
                if m == 0:
                    e.wait_ge(sems["SE"], se_b2q)
                inc(e.activation(hq_sb[:, m * TCM:(m + 1) * TCM], pshalf, AF.Relu,
                                 bias=b1_sb[:, m:m + 1], scale=1.0), "SE")
            epi_se[buf] = C["SE"]
        mm2_done = [0, 0]
        epi2_se = [0, 0]
        for m in range(NFT):
            buf = m % 2
            @blk.gpsimd
            def _(e, m=m, buf=buf, _mm2_done_buf=mm2_done[buf]):
                if _mm2_done_buf:
                    e.wait_ge(sems["TE"], _mm2_done_buf)
                e.dma_start(out=wq_m[:, buf * H:(buf + 1) * H], in_=W2T[m]).then_inc(sems["GD"], 16)
            C["GD"] += 16
            gd_t = C["GD"]
            psh2 = ps_o2[:, buf * TCM:(buf + 1) * TCM]
            @blk.tensor
            def _(e, m=m, buf=buf, gd_t=gd_t, psh2=psh2, _epi_se_0=epi_se[0], _epi_se_1=epi_se[1], _epi2_se_buf=epi2_se[buf]):
                e.wait_ge(sems["GD"], gd_t)
                if m == 0:
                    e.wait_ge(sems["SE"], _epi_se_0)
                    e.wait_ge(sems["SE"], _epi_se_1)
                if _epi2_se_buf:
                    e.wait_ge(sems["SE"], _epi2_se_buf)
                for k in range(32):
                    mm = e.matmul(psh2, wq_m[:, buf * H + k * 128: buf * H + (k + 1) * 128],
                                  hq_sb[:, k * TCM:(k + 1) * TCM], start=(k == 0), stop=(k == 31))
                inc(mm, "TE")
            mm2_done[buf] = C["TE"]
            @blk.scalar
            def _(e, m=m, buf=buf, psh2=psh2, _mm2_done_buf=mm2_done[buf]):
                e.wait_ge(sems["TE"], _mm2_done_buf)
                inc(e.activation(offq[:, m * TCM:(m + 1) * TCM], psh2, AF.Identity,
                                 bias=b2q_sb[:, m:m + 1], scale=1.0), "SE")
            epi2_se[buf] = C["SE"]
        of_t = [offq[:, f * TCM:(f + 1) * TCM] for f in range(NFT)]
        emit_postln_resid(of_t, t0, onesch[:])
    ve_ffn0 = C["VE"]
    if debug:
        @blk.gpsimd
        def _(e):
            e.wait_ge(sems["VE"], ve_ffn0)
            e.dma_start(out=dbg["d_x2"][:], in_=xfm[:]).then_inc(sems["GD"], 16)
        C["GD"] += 16
    es_f.close()

    es_a = contextlib.ExitStack()

    def sba(name, shape, dt):
        return es_a.enter_context(nc.sbuf_tensor(name, list(shape), dt))

    VT = sba("VT", [128, 16 * 1040], BF16)
    negones = sba("negones", [1, 128], F32)
    bq_sb = sba("bq_sb", [128, NFT], F32)
    bk_sb = sba("bk_sb", [128, NFT], F32)
    bout_sb = sba("bout_sb", [128, NFT], F32)
    bvrow = sba("bvrow", [1, D], BF16)
    Qb = sba("Qb", [128, 8 * L], BF16)
    es_aK = contextlib.ExitStack()
    Kb = es_aK.enter_context(nc.sbuf_tensor("Kb", [128, 8 * L], BF16))
    es_a1 = contextlib.ExitStack()
    wv_sb = es_a1.enter_context(nc.sbuf_tensor("wv_sb", [128, 8 * 1024], BF16))
    xnbq = es_a1.enter_context(nc.sbuf_tensor("xnbq", [128, NFT * TCM], BF16))

    @blk.sync
    def _(e):
        e.wait_ge(sems["VE"], ve_ffn0)
        e.dma_start(out=bq_sb[:], in_=BQ[:]).then_inc(sems["SD"], 16)
        e.dma_start(out=bk_sb[:], in_=BK[:]).then_inc(sems["SD"], 16)
        e.dma_start(out=bout_sb[:], in_=BOUT[:]).then_inc(sems["SD"], 16)
    C["SD"] += 48
    sd_a = C["SD"]
    @blk.gpsimd
    def _(e):
        e.wait_ge(sems["VE"], ve_ffn0)
        e.dma_start(out=bvrow[:], in_=BV[:]).then_inc(sems["GD"], 16)
        for k in range(8):
            e.dma_start(out=wv_sb[:, k * 1024:(k + 1) * 1024], in_=WVR[k]).then_inc(sems["GD"], 16)
    C["GD"] += 144
    gd_wv = C["GD"]
    @blk.vector
    def _(e):
        e.wait_ge(sems["SD"], sd_a)
        e.memset(negones[:], -1.0)
        e.tensor_scalar(bq_sb[:], bq_sb[:], C_QS, None, op0=MUL)
        vt3 = VT[:].rearrange("p (t h c) -> p t h c", h=NH, c=65)
        inc(e.memset(vt3[:, :, :, 64:65], 1.0), "VE")
    ve_abias = C["VE"]

    se_vt_prev = 0
    mmq_done = [0, 0]
    for qc in range(NCM):
        t0 = qc * TCM
        ve_xq = emit_preln(t0, lambda f: xnbq[:, f * TCM:(f + 1) * TCM])
        for tt in range(4):
            t16 = qc * 4 + tt
            @blk.tensor
            def _(e, tt=tt, t16=t16, ve_xq=ve_xq, svp=se_vt_prev):
                if t16 == 0:
                    e.wait_ge(sems["GD"], gd_wv)
                e.wait_ge(sems["VE"], ve_xq)
                if svp:
                    e.wait_ge(sems["SE"], svp)
                for k in range(8):
                    for nh in range(2):
                        e.matmul(ps_mm[:, nh * TCM:(nh + 1) * TCM],
                                 xnbq[:, k * TCM + tt * 128: k * TCM + (tt + 1) * 128],
                                 wv_sb[:, k * 1024 + nh * TCM: k * 1024 + (nh + 1) * TCM],
                                 start=(k == 0), stop=False)
                for nh in range(2):
                    mm = e.matmul(ps_mm[:, nh * TCM:(nh + 1) * TCM], onesb[0:1, :],
                                  bvrow[0:1, nh * TCM:(nh + 1) * TCM], start=False, stop=True)
                inc(mm, "TE")
            te_v = C["TE"]
            @blk.scalar
            def _(e, t16=t16, te_v=te_v):
                e.wait_ge(sems["TE"], te_v)
                if t16 == 0:
                    e.wait_ge(sems["VE"], ve_abias)
                dst = VT[:, t16 * 1040:(t16 + 1) * 1040].rearrange("p (h c) -> p h c", c=65)[:, :, 0:64]
                src_ = ps_mm[:, 0:1024].rearrange("p (h c) -> p h c", c=64)
                inc(e.activation(dst, src_, AF.Identity), "SE")
            se_vt_prev = C["SE"]
        for m in range(16):
            buf = m % 2
            @blk.gpsimd
            def _(e, m=m, buf=buf, qc=qc, svp=se_vt_prev, _mmq_done_buf=mmq_done[buf]):
                if qc == 0 and m < 2:
                    e.wait_ge(sems["SE"], svp)
                if _mmq_done_buf:
                    e.wait_ge(sems["TE"], _mmq_done_buf)
                e.dma_start(out=ipw_m[:, buf * 1024:(buf + 1) * 1024], in_=WQKT[m]).then_inc(sems["GD"], 16)
            C["GD"] += 16
            gd_t = C["GD"]
            pshalf = ps_mm[:, buf * TCM:(buf + 1) * TCM]
            @blk.tensor
            def _(e, m=m, buf=buf, gd_t=gd_t, pshalf=pshalf, _cse=C["SE"]):
                e.wait_ge(sems["GD"], gd_t)
                e.wait_ge(sems["SE"], _cse)
                for k in range(8):
                    mm = e.matmul(pshalf, ipw_m[:, buf * 1024 + k * 128: buf * 1024 + (k + 1) * 128],
                                  xnbq[:, k * TCM:(k + 1) * TCM], start=(k == 0), stop=(k == 7))
                inc(mm, "TE")
            mmq_done[buf] = C["TE"]
            @blk.scalar
            def _(e, m=m, qc=qc, buf=buf, pshalf=pshalf, _mmq_done_buf=mmq_done[buf]):
                e.wait_ge(sems["TE"], _mmq_done_buf)
                if m < 8:
                    inc(e.activation(Qb[:, m * L + qc * TCM: m * L + (qc + 1) * TCM], pshalf,
                                     AF.Identity, bias=bq_sb[:, m:m + 1], scale=C_QS), "SE")
                else:
                    inc(e.activation(Kb[:, (m - 8) * L + qc * TCM: (m - 8) * L + (qc + 1) * TCM], pshalf,
                                     AF.Identity, bias=bk_sb[:, m - 8:m - 7], scale=1.0), "SE")
    se_qk = C["SE"]
    es_a1.close()
    es_a2 = contextlib.ExitStack()
    arow = es_a2.enter_context(nc.sbuf_tensor("arow", [96, 1024], F32))
    rdrep = es_a2.enter_context(nc.sbuf_tensor("rdrep", [64, TCM], F32))
    wrow = es_a2.enter_context(nc.sbuf_tensor("wrow", [1, TCM], BF16))
    wsb = es_a2.enter_context(nc.sbuf_tensor("wsb", [128, TCM], BF16))
    tsc = es_a2.enter_context(nc.sbuf_tensor("tsc", [128, TCM], BF16))
    expP = es_a2.enter_context(nc.sbuf_tensor("expP", [128, 16 * TCM], BF16))
    if debug:
        @blk.gpsimd
        def _(e):
            e.wait_ge(sems["SE"], se_qk)
            e.dma_start(out=dbg["d_q"][:], in_=Qb[:]).then_inc(sems["GD"], 16)
            e.dma_start(out=dbg["d_vt"][:], in_=VT[:]).then_inc(sems["GD"], 16)
        C["GD"] += 32

    se_exp_slot = [0, 0]
    ve_norm_par = [0, 0]
    ve_t_last = 0
    prev = None
    it_list = [(h, qv) for h in range(NH) for qv in range(4)]
    for it in range(65):
        cur = it_list[it] if it < 64 else None
        if cur is not None:
            h, qv = cur
            tile = h // 2
            po = (h % 2) * 64
            qoff = tile * L + qv * TCM
        for kc in range(16):
            sl = kc % 2
            if cur is not None:
                @blk.tensor
                def _(e, kc=kc, sl=sl, po=po, tile=tile, qoff=qoff,
                      ses=se_exp_slot[sl], first=(it == 0 and kc < 2)):
                    if first:
                        e.wait_ge(sems["SE"], se_qk)
                    if ses:
                        e.wait_ge(sems["SE"], ses)
                    inc(e.matmul(ps_mm[:, sl * TCM:(sl + 1) * TCM],
                                 Kb[po:po + 64, tile * L + kc * 128: tile * L + (kc + 1) * 128],
                                 Qb[po:po + 64, qoff:qoff + TCM],
                                 start=True, stop=True), "TE")
            if prev is not None:
                ph = prev["h"]
                osl = prev["osl"]
                @blk.tensor
                def _(e, kc=kc, ph=ph, osl=osl, vsq=prev["ve_sq"][kc],
                      vn=(ve_norm_par[osl] if kc == 0 else 0)):
                    e.wait_ge(sems["VE"], vsq)
                    if vn:
                        e.wait_ge(sems["VE"], vn)
                    inc(e.matmul(ps_o2[0:65, osl * TCM:(osl + 1) * TCM],
                                 VT[:, kc * 1040 + ph * 65: kc * 1040 + (ph + 1) * 65],
                                 expP[:, kc * TCM:(kc + 1) * TCM],
                                 start=(kc == 0), stop=(kc == 15)), "TE")
            if cur is not None:
                @blk.scalar
                def _(e, kc=kc, sl=sl, te_w=C["TE"]):
                    e.wait_ge(sems["TE"], te_w)
                    inc(e.activation(expP[:, kc * TCM:(kc + 1) * TCM],
                                     ps_mm[:, sl * TCM:(sl + 1) * TCM],
                                     AF.Exp, bias=C_M16, scale=C_SPARE), "SE")
                se_exp_slot[sl] = C["SE"]
                @blk.tensor
                def _(e, kc=kc, _se=C["SE"]):
                    e.wait_ge(sems["SE"], _se)
                    inc(e.matmul(ps_y[0:1, :], onescb[:], expP[:, kc * TCM:(kc + 1) * TCM],
                                 start=(kc == 0), stop=(kc == 15)), "TE")
        if prev is not None:
            @blk.scalar
            def _(e, osl=prev["osl"], _te=C["TE"]):
                e.wait_ge(sems["TE"], _te)
                e.activation(arow[64:65, 0:TCM], ps_o2[64:65, osl * TCM:(osl + 1) * TCM], AF.Ln)
                inc(e.activation(arow[64:65, 0:TCM], arow[64:65, 0:TCM], AF.Exp,
                                 scale=C_NEG1[0:1, :]), "SE")
            se_rd = C["SE"]
        if cur is not None:
            @blk.scalar
            def _(e, _te=C["TE"]):
                e.wait_ge(sems["TE"], _te)
                e.activation(arow[0:1, 0:TCM], ps_y[0:1, :], AF.Ln)
                inc(e.activation(wrow[0:1, :], arow[0:1, 0:TCM], AF.Exp,
                                 scale=C_NEG1[0:1, :]), "SE")
            se_w = C["SE"]
        if prev is not None:
            @blk.tensor
            def _(e, _se=se_rd):
                e.wait_ge(sems["SE"], _se)
                inc(e.matmul(ps_dt[0:64, :], onesA[64:65, 0:64], arow[64:65, 0:TCM],
                             start=True, stop=True), "TE")
            te_rdb = C["TE"]
        if cur is not None:
            @blk.tensor
            def _(e, _se=se_w):
                e.wait_ge(sems["SE"], _se)
                inc(e.matmul(ps_bc[:, :], onesb[0:1, :], wrow[0:1, :],
                             start=True, stop=True), "TE")
            te_wb = C["TE"]
        if prev is not None:
            @blk.vector
            def _(e, _te=te_rdb):
                e.wait_ge(sems["TE"], _te)
                inc(e.tensor_scalar(rdrep[:, :], ps_dt[0:64, :], C_ONE[0:64, :],
                                    None, op0=MUL), "VE")
        if cur is not None:
            @blk.scalar
            def _(e, _te=te_wb, _vt=ve_t_last):
                e.wait_ge(sems["TE"], _te)
                if _vt:
                    e.wait_ge(sems["VE"], _vt)
                inc(e.activation(wsb[:, :], ps_bc[:, :], AF.Identity), "SE")
            se_wc = C["SE"]
        if prev is not None:
            ph = prev["h"]
            osl = prev["osl"]
            ppo = (ph % 2) * 64
            pqoff = (ph // 2) * L + prev["qv"] * TCM
            @blk.vector
            def _(e, osl=osl, ppo=ppo, pqoff=pqoff):
                inc(e.tensor_tensor(Qb[ppo:ppo + 64, pqoff:pqoff + TCM],
                                    ps_o2[0:64, osl * TCM:(osl + 1) * TCM],
                                    rdrep[:, :], op=MUL), "VE")
            ve_norm_par[osl] = C["VE"]
        if cur is not None:
            ve_sq = [0] * 16
            for kc in range(16):
                @blk.vector
                def _(e, kc=kc, _se=(se_wc if kc == 0 else 0)):
                    if _se:
                        e.wait_ge(sems["SE"], _se)
                    e.tensor_tensor(tsc[:, :], expP[:, kc * TCM:(kc + 1) * TCM], wsb[:, :], op=MUL)
                    inc(e.tensor_tensor(expP[:, kc * TCM:(kc + 1) * TCM], tsc[:, :], tsc[:, :],
                                        op=MUL), "VE")
                ve_sq[kc] = C["VE"]
            ve_t_last = ve_sq[15]
            prev = {"h": h, "qv": qv, "osl": it % 2, "ve_sq": ve_sq}
    ve_attn_done = C["VE"]
    if debug:
        @blk.gpsimd
        def _(e):
            e.wait_ge(sems["VE"], ve_attn_done)
            e.dma_start(out=dbg["d_ao"][:], in_=Qb[:]).then_inc(sems["GD"], 16)
            e.dma_start(out=dbg["d_den"][:], in_=arow[:]).then_inc(sems["GD"], 16)
        C["GD"] += 32

    es_a2.close()
    es_aK.close()
    wout_res = sba("wout_res", [128, 8 * 1024], BF16)
    ofa = sba("ofa", [128, NFT * TCM], F32)
    @blk.gpsimd
    def _(e):
        e.wait_ge(sems["VE"], ve_attn_done)
        for m in range(8):
            e.dma_start(out=wout_res[:, m * 1024:(m + 1) * 1024], in_=WOUTT[m]).then_inc(sems["GD"], 16)
    C["GD"] += 128
    gd_wo = C["GD"]
    for c in range(NCM):
        t0 = c * TCM
        for m in range(NFT):
            pshalf = ps_mm[:, (m % 2) * TCM:((m % 2) + 1) * TCM]
            @blk.tensor
            def _(e, m=m, c=c, pshalf=pshalf, _cse=C["SE"]):
                if m == 0 and c == 0:
                    e.wait_ge(sems["GD"], gd_wo)
                    e.wait_ge(sems["VE"], ve_attn_done)
                e.wait_ge(sems["SE"], _cse)
                for k in range(8):
                    mm = e.matmul(pshalf, wout_res[:, m * 1024 + k * 128: m * 1024 + (k + 1) * 128],
                                  Qb[:, k * L + t0: k * L + t0 + TCM], start=(k == 0), stop=(k == 7))
                inc(mm, "TE")
            te_wo = C["TE"]
            @blk.scalar
            def _(e, m=m, te_wo=te_wo, pshalf=pshalf):
                e.wait_ge(sems["TE"], te_wo)
                inc(e.activation(ofa[:, m * TCM:(m + 1) * TCM], pshalf, AF.Identity,
                                 bias=bout_sb[:, m:m + 1], scale=1.0), "SE")
        of_t = [ofa[:, f * TCM:(f + 1) * TCM] for f in range(NFT)]
        if debug and c == 0:
            @blk.gpsimd
            def _(e, _cse=C["SE"]):
                e.wait_ge(sems["SE"], _cse)
                e.dma_start(out=dbg["d_ofa"][:], in_=ofa[:]).then_inc(sems["GD"], 16)
            C["GD"] += 16
        emit_postln_resid(of_t, t0, onescf[:])
    ve_attn = C["VE"]
    if debug:
        @blk.gpsimd
        def _(e):
            e.wait_ge(sems["VE"], ve_attn)
            e.dma_start(out=dbg["d_x3"][:], in_=xfm[:]).then_inc(sems["GD"], 16)
        C["GD"] += 16
    es_a.close()

    es_l = contextlib.ExitStack()

    def sbl(name, shape, dt):
        return es_l.enter_context(nc.sbuf_tensor(name, list(shape), dt))

    xnbL = sbl("xnbL", [128, NFT * L], BF16)
    l1res = sbl("l1res", [128, 8 * 1024], BF16)
    ofl = sbl("ofl", [128, NFT * TCM], F32)

    @blk.gpsimd
    def _(e):
        e.wait_ge(sems["VE"], ve_attn)
        for m in range(8):
            e.dma_start(out=l1res[:, m * 1024:(m + 1) * 1024], in_=L1WT[m]).then_inc(sems["GD"], 16)
    C["GD"] += 128
    gd_l1 = C["GD"]
    @blk.scalar
    def _(e):
        e.wait_ge(sems["GD"], gd_l1)
        inc(e.activation(l1res[:], l1res[:], AF.Sign), "SE")
    se_sign = C["SE"]

    for c in range(NCM):
        emit_preln(c * TCM,
                   lambda f, c=c: xnbL[:, f * L + c * TCM: f * L + (c + 1) * TCM])
    ve_xnL = C["VE"]
    for c in range(NCM):
        t0 = c * TCM
        for m in range(NFT):
            pshalf = ps_mm[:, (m % 2) * TCM:((m % 2) + 1) * TCM]
            @blk.tensor
            def _(e, m=m, c=c, pshalf=pshalf, _cse=C["SE"]):
                if m == 0 and c == 0:
                    e.wait_ge(sems["SE"], se_sign)
                    e.wait_ge(sems["VE"], ve_xnL)
                e.wait_ge(sems["SE"], _cse)
                for k in range(8):
                    mm = e.matmul(pshalf, l1res[:, m * 1024 + k * 128: m * 1024 + (k + 1) * 128],
                                  xnbL[:, k * L + t0: k * L + t0 + TCM], start=(k == 0), stop=(k == 7))
                inc(mm, "TE")
            te_l1 = C["TE"]
            @blk.scalar
            def _(e, m=m, te_l1=te_l1, pshalf=pshalf):
                e.wait_ge(sems["TE"], te_l1)
                inc(e.activation(ofl[:, m * TCM:(m + 1) * TCM], pshalf, AF.Identity), "SE")
        of_t = [ofl[:, f * TCM:(f + 1) * TCM] for f in range(NFT)]
        emit_postln_resid(of_t, t0, onescf[:])
    ve_final = C["VE"]
    es_l.close()

    xio_sd = [0, 0]
    for i in range(16):
        bi = i % 2
        for f in range(NFT):
            q = (i * NFT + f) % 7
            @blk.tensor
            def _(e, i=i, f=f, q=q, _trv_se_q=trv_se[q]):
                if i == 0 and f == 0:
                    e.wait_ge(sems["VE"], ve_final)
                if _trv_se_q:
                    e.wait_ge(sems["SE"], _trv_se_q)
                inc(e.transpose(trv[q][:], xfm[:, f * L + i * 128:f * L + (i + 1) * 128], idf_sb[:]), "TE")
            te_n = C["TE"]
            @blk.scalar
            def _(e, i=i, f=f, q=q, bi=bi, te_n=te_n, _xio_sd_bi=xio_sd[bi]):
                e.wait_ge(sems["TE"], te_n)
                if f == 0 and _xio_sd_bi:
                    e.wait_ge(sems["SD"], _xio_sd_bi)
                inc(e.activation(xio_v[bi][:, f * 128:(f + 1) * 128], trv[q][:], AF.Identity), "SE")
            trv_se[q] = C["SE"]
        se_t = C["SE"]
        @blk.sync
        def _(e, i=i, bi=bi, se_t=se_t):
            e.wait_ge(sems["SE"], se_t)
            e.dma_start(out=OUTt[i], in_=xio_v[bi][:]).then_inc(sems["SD"], 16)
        C["SD"] += 16
        xio_sd[0] = xio_sd[1] = C["SD"]
    @blk.sync
    def _(e, _csd=C["SD"]):
        e.wait_ge(sems["SD"], _csd)

    _replay[0] = True
    for _key in ("sync", "gpsimd", "tensor", "vector", "scalar"):
        _fns = _PEND[_key]
        for _i0 in range(0, len(_fns), 16):
            _chunk = _fns[_i0:_i0 + 16]
            getattr(blk_real, _key)(lambda e, _c=_chunk: [f(e) for f in _c])
    _replay[0] = False

    es.close()
    return nc


_CACHE = {}


def _prep(inputs):
    f32 = np.float32

    def c(a):
        return np.ascontiguousarray(np.asarray(a), dtype=f32)

    def pmaj(a, n):
        s = np.asarray(a).shape[1]
        return c(np.asarray(a).reshape(n, 128, s).transpose(1, 0, 2).reshape(128, n * s))

    def vec_pm(a, n):
        return c(np.asarray(a).reshape(n, 128).T)

    ip = c(inputs["L0_attn_in_proj"])
    ipw = c(ip.reshape(8, 128, 32, 128).transpose(2, 1, 0, 3).reshape(32, 128, 1024))
    op = c(inputs["L0_attn_out_proj"])
    opw = c(op.reshape(16, 128, 8, 128).transpose(2, 1, 0, 3).reshape(8, 128, 2048))
    shared = {
        "ipw": ipw, "opw": opw,
        "xpw": pmaj(inputs["L0_attn_x_proj"], NDG),
        "dtw": c(inputs["L0_attn_dt_w"]),
        "convw": pmaj(inputs["L0_attn_conv_w"], NDG),
        "convb": vec_pm(inputs["L0_attn_conv_b"], NDG),
        "dtb": vec_pm(inputs["L0_attn_dt_b"], NDG),
        "alog": pmaj(inputs["L0_attn_A_log"], NDG),
        "dd": vec_pm(inputs["L0_attn_D"], NDG),
        "rmsw": vec_pm(inputs["L0_attn_rms_w"], NFT),
        "idf": np.eye(128, dtype=f32),
        "idb": np.eye(128, dtype=f32).astype(ml_dtypes.bfloat16),
        "indv": np.concatenate([np.eye(96, dtype=f32)[:, 64:80],
                                np.eye(96, dtype=f32)[:, 80:96]], axis=1).astype(ml_dtypes.bfloat16),
    }
    w1 = c(inputs["L0_ffn_w1"]).T
    shared["w1t"] = c(w1.reshape(8, 128, 32, 128).transpose(2, 1, 0, 3).reshape(32, 128, 1024))
    shared["b1"] = vec_pm(inputs["L0_ffn_b1"], 32)
    w2 = c(inputs["L0_ffn_w2"]).T
    shared["w2t"] = c(w2.reshape(32, 128, 8, 128).transpose(2, 1, 0, 3).reshape(8, 128, 4096))
    shared["b2"] = vec_pm(inputs["L0_ffn_b2"], NFT)
    wqkv = c(inputs["L1_attn_W_qkv"])
    wq_r = wqkv.reshape(NH, 3, DH, D).transpose(1, 0, 2, 3).reshape(3 * D, D)
    wqkT = c(wq_r[0:2 * D].T)
    shared["wqkt"] = c(wqkT.reshape(8, 128, 16, 128).transpose(2, 1, 0, 3).reshape(16, 128, 1024))
    shared["wvr"] = c(wq_r[2 * D:3 * D].T.reshape(8, 128, 1024))
    bqkv = c(inputs["L1_attn_b_qkv"]).reshape(NH, 3, DH).transpose(1, 0, 2).reshape(3, D)
    shared["bq"] = vec_pm(bqkv[0], NFT)
    shared["bk"] = vec_pm(bqkv[1], NFT)
    shared["bv"] = c(bqkv[2]).reshape(1, D)
    wout = c(inputs["L1_attn_W_out"]).T
    shared["woutt"] = c(wout.reshape(8, 128, 8, 128).transpose(2, 1, 0, 3).reshape(8, 128, 1024))
    shared["bout"] = vec_pm(inputs["L1_attn_b_out"], NFT)
    l1w = c(inputs["L1_ffn_weight"]).T
    shared["l1wt"] = c(l1w.reshape(8, 128, 8, 128).transpose(2, 1, 0, 3).reshape(8, 128, 1024))
    return shared


def kernel(**inputs):
    shared = _prep(inputs)
    x = np.ascontiguousarray(np.asarray(inputs["L0_attn_x"]), dtype=np.float32)
    if "nc" not in _CACHE:
        _CACHE["nc"] = build(debug=DEBUG)
    nc = _CACHE["nc"]
    in_maps = []
    for b in range(8):
        m = dict(shared)
        m["x"] = np.ascontiguousarray(x[b])
        in_maps.append(m)
    res = run_bass_kernel_spmd(nc, in_maps, core_ids=list(range(8)))
    out = np.stack([res.results[b]["out"] for b in range(8)], axis=0)
    return out


# version 09
# speedup vs baseline: 1.3799x; 1.1375x over previous
import contextlib
import numpy as np
import ml_dtypes
import concourse.bass as bass
import concourse.mybir as mybir
from concourse.bass_utils import run_bass_kernel_spmd
from concourse.alu_op_type import AluOpType

F32 = mybir.dt.float32
BF16 = mybir.dt.bfloat16
FP16 = mybir.dt.float16
FP8 = mybir.dt.float8e4
AF = mybir.ActivationFunctionType
MUL = AluOpType.mult
ADD = AluOpType.add
SUB = AluOpType.subtract

D = 1024
L = 2048
DIN = 2048
DS = 16
H = 4096
NH = 16
DH = 64
NFT = D // 128
NDG = DIN // 128
TCM = 512
NCM = L // TCM
EPS = 1e-5

DEBUG = False
STAB_A = True
STAB_B = True
STAB_RED = True
DEFER = "scalar"


def build(debug=False):
    nc = bass.Bass("TRN2", target_bir_lowering=False)

    def din(name, shape, dt=F32):
        return nc.dram_tensor(name, list(shape), dt, kind="ExternalInput").ap()

    X = din("x", [L, D])
    IPW = din("ipw", [32, 128, 1024])
    OPW = din("opw", [8, 128, 2048])
    XPW = din("xpw", [128, NDG * 96])
    DTW = din("dtw", [64, DIN])
    CONVW = din("convw", [128, NDG * 4])
    CONVB = din("convb", [128, NDG])
    DTB = din("dtb", [128, NDG])
    ALOG = din("alog", [128, NDG * DS])
    DDD = din("dd", [128, NDG])
    RMSW = din("rmsw", [128, NFT])
    INDV = din("indv", [96, 32], BF16)
    W1T = din("w1t", [32, 128, 1024])
    B1 = din("b1", [128, 32])
    W2T = din("w2t", [8, 128, 4096])
    B2 = din("b2", [128, NFT])
    WQKT = din("wqkt", [16, 128, 1024])
    WVR = din("wvr", [8, 128, 1024])
    BQ = din("bq", [128, NFT])
    BK = din("bk", [128, NFT])
    BV = din("bv", [1, D])
    WOUTT = din("woutt", [8, 128, 1024])
    BOUT = din("bout", [128, NFT])
    L1WT = din("l1wt", [8, 128, 1024])
    IDF = din("idf", [128, 128], F32)
    IDB = din("idb", [128, 128], BF16)
    OUT = nc.dram_tensor("out", [L, D], F32, kind="ExternalOutput").ap()
    W1B = nc.dram_tensor("w1b", [32, 128, 1024], BF16, kind="Internal").ap()
    W2Q = nc.dram_tensor("w2q", [8, 128, 4096], FP8, kind="Internal").ap()
    WQKB = nc.dram_tensor("wqkb", [16, 128, 1024], BF16, kind="Internal").ap()
    WOUTB = nc.dram_tensor("woutb", [8, 128, 1024], BF16, kind="Internal").ap()
    L1B = nc.dram_tensor("l1b", [8, 128, 1024], BF16, kind="Internal").ap()

    dbg = {}
    if debug:
        def dout(name, shape, dt=F32):
            dbg[name] = nc.dram_tensor(name, list(shape), dt, kind="ExternalOutput").ap()
        dout("d_xnb", [128, NFT * TCM])
        dout("d_wq", [128, 8 * H])
        dout("d_hq", [128, 32 * TCM])
        dout("d_y2", [128, NDG * TCM])
        dout("d_x1", [128, NFT * L])
        dout("d_x2", [128, NFT * L])
        dout("d_x3", [128, NFT * L])
        dout("d_q", [128, 8 * L])
        dout("d_vt", [128, 16 * 1040])
        dout("d_ao", [128, 8 * L])
        dout("d2_xnb", [128, NFT * TCM])
        dout("d2_stat", [96, TCM])
        dout("d2_xib", [128, NDG * (TCM + 3)])
        dout("d2_dbc", [96, TCM])
        dout("d2_y2", [128, NDG * TCM])
        dout("d2_of", [128, NFT * TCM])
        dout("d_ofa", [128, NFT * TCM])
        dout("d_den", [96, 1024])

    es = contextlib.ExitStack()

    def sb(name, shape, dt):
        return es.enter_context(nc.sbuf_tensor(name, list(shape), dt))

    def psum(name, shape, dt=F32):
        return es.enter_context(nc.psum_tensor(name, list(shape), dt))

    xfm = sb("xfm", [128, NFT * L], F32)
    consts = sb("consts", [128, 12], F32)
    onesf = sb("onesf", [1, 128], F32)
    onesb = sb("onesb", [1, 128], BF16)
    onescf = sb("onescf", [128, 1], F32)
    onescb = sb("onescb", [128, 1], BF16)
    onesch = sb("onesch", [128, 1], FP16)
    onesA = sb("onesA", [96, 128], F32)
    idf_sb = sb("idf_sb", [128, 128], F32)
    idb_sb = sb("idb_sb", [128, 128], BF16)
    statrA = sb("statrA", [96, TCM], F32)
    statrB = sb("statrB", [96, TCM], F32)
    statrC = sb("statrC", [96, TCM], F32)
    ipw_m = sb("ipw_m", [128, 2 * 1024], BF16)
    sq_sb = sb("sq_sb", [128, TCM], F32)
    xio = sb("xio", [128, D], F32)

    ps_mm = psum("ps_mm", [128, 2 * TCM])
    ps_o2 = psum("ps_o2", [128, 2 * TCM])
    ps_y = psum("ps_y", [128, TCM])
    ps_bc = psum("ps_bc", [128, TCM])
    ps_dt = psum("ps_dt", [128, TCM])
    ps_sx = psum("ps_sx", [128, TCM])
    ps_s1 = ps_bc[0:1, :]
    ps_s2 = ps_y[0:1, :]
    ps_dbcv = ps_sx[0:96, :]

    es_m = contextlib.ExitStack()

    def sbm(name, shape, dt):
        return es_m.enter_context(nc.sbuf_tensor(name, list(shape), dt))

    A_sb = sbm("A_sb", [128, NDG * DS], F32)
    convw_sb = sbm("convw_sb", [128, NDG * 4], F32)
    convb_sb = sbm("convb_sb", [128, NDG], F32)
    dtb_sb = sbm("dtb_sb", [128, NDG], F32)
    D_sb = sbm("D_sb", [128, NDG], F32)
    rmsw_sb = sbm("rmsw_sb", [128, NFT], F32)
    xpw_sb = sbm("xpw_sb", [128, NDG * 96], BF16)
    dtw_sb = sbm("dtw_sb", [64, DIN], BF16)
    xnbc = sbm("xnbc", [128, NFT * TCM], BF16)
    opw_m = sbm("opw_m", [128, 2 * 2048], BF16)
    xib = sbm("xib", [128, NDG * (TCM + 3)], BF16)
    xtail = sbm("xtail", [128, NDG * 3], F32)
    zsb = sbm("zsb", [128, NDG * TCM], BF16)
    Brep = sbm("BrepT", [128, DS * TCM], BF16)
    Crep = sbm("CrepT", [128, DS * TCM], BF16)
    y2b = sbm("y2b", [128, NDG * TCM], BF16)
    dbc_sb = sbm("dbc_sb", [96, TCM], BF16)
    delta_sb = sbm("delta_sb", [128, TCM], F32)
    u_sb = sbm("u_sb", [128, TCM], BF16)
    a_sb = sbm("a_sb", [128, 2 * TCM], F32)
    b_sb = sbm("b_sb", [128, 2 * TCM], BF16)
    h_sb = sbm("h_sb", [128, 2 * TCM], BF16)
    hc_sb = sbm("hc_sb", [128, 2 * TCM], BF16)
    h0_sb = sbm("h0_sb", [128, NDG * DS], F32)
    rrep_sb = sbm("rrep_sb", [128, TCM], F32)
    mrep_sb = sbm("mrep_sb", [128, TCM], F32)
    indv_sb = sbm("indv_sb", [96, 32], BF16)
    of16 = xnbc

    blk_real = es.enter_context(nc.Block())
    sems = {}
    C = {}
    for s in ["SD", "GD", "TE", "VE", "SE"]:
        sems[s] = es.enter_context(nc.semaphore(s))
        C[s] = 0

    _replay = [False]

    def inc(instr, s, n=1):
        instr.then_inc(sems[s], n)
        if not _replay[0]:
            C[s] += n
        return C[s]

    class _Dummy:
        def then_inc(self, *a, **k):
            return self

    _DUM = _Dummy()

    class _Mock:
        def __getattr__(self, name):
            return lambda *a, **k: _DUM

    _MOCK = _Mock()
    _PEND = {k: [] for k in ("sync", "gpsimd", "tensor", "vector", "scalar")}

    class _EngDec:
        def __init__(self, key):
            self.key = key

        def __call__(self, fn):
            if DEFER is True or DEFER == self.key:
                _PEND[self.key].append(fn)
                fn(_MOCK)
            else:
                getattr(blk_real, self.key)(fn)

    class _BlkProxy:
        sync = _EngDec("sync")
        gpsimd = _EngDec("gpsimd")
        tensor = _EngDec("tensor")
        vector = _EngDec("vector")
        scalar = _EngDec("scalar")

    blk = _BlkProxy()

    @blk.sync
    def _(e):
        e.dma_start(out=A_sb[:], in_=ALOG[:]).then_inc(sems["SD"], 16)
        e.dma_start(out=convw_sb[:], in_=CONVW[:]).then_inc(sems["SD"], 16)
        e.dma_start(out=convb_sb[:], in_=CONVB[:]).then_inc(sems["SD"], 16)
        e.dma_start(out=dtb_sb[:], in_=DTB[:]).then_inc(sems["SD"], 16)
        e.dma_start(out=D_sb[:], in_=DDD[:]).then_inc(sems["SD"], 16)
        e.dma_start(out=rmsw_sb[:], in_=RMSW[:]).then_inc(sems["SD"], 16)
        e.dma_start(out=idf_sb[:], in_=IDF[:]).then_inc(sems["SD"], 16)
        e.dma_start(out=idb_sb[:], in_=IDB[:]).then_inc(sems["SD"], 16)
        e.dma_start(out=indv_sb[:], in_=INDV[:]).then_inc(sems["SD"], 16)
    C["SD"] += 16 * 9
    sd_init = C["SD"]

    @blk.gpsimd
    def _(e):
        e.dma_start(out=xpw_sb[:], in_=XPW[:]).then_inc(sems["GD"], 16)
        e.dma_start(out=dtw_sb[:], in_=DTW[:]).then_inc(sems["GD"], 16)
    C["GD"] += 32
    gd_init = C["GD"]

    @blk.vector
    def _(e):
        for i, v in enumerate([1.0, -1.0, 0.125, 1.0 / 1024.0, EPS, -0.5, 0.0, 2.0,
                               -44.0, 2.0, 88.0, 0.5]):
            e.memset(consts[:, i:i + 1], v)
        e.memset(onesf[:], 1.0)
        e.memset(onesb[:], 1.0)
        e.memset(onescf[:], 1.0)
        e.memset(onescb[:], 1.0)
        e.memset(onesch[:], 1.0)
        e.memset(onesA[:], 1.0)
        e.memset(statrA[:], 1.0)
        e.memset(statrB[:], 1.0)
        e.memset(statrC[:], 1.0)
        e.memset(h0_sb[:], 0.0)
        e.memset(xtail[:], 0.0)
        inc(e.nop(), "VE")
    (C_ONE, C_NEG1, C_QS, C_I1024, C_EPS, C_NHALF, C_ZERO, C_TWO,
     C_M16, C_EIGHT, C_128, C_SPARE) = [consts[:, i:i + 1] for i in range(12)]

    @blk.scalar
    def _(e):
        e.wait_ge(sems["SD"], sd_init)
        inc(e.activation(A_sb[:], A_sb[:], AF.Exp), "SE")
    se_A = C["SE"]
    @blk.vector
    def _(e):
        e.wait_ge(sems["SE"], se_A)
        inc(e.tensor_scalar(A_sb[:], A_sb[:], C_NEG1, None, op0=MUL), "VE")
    ve_A = C["VE"]

    trv = [ps_mm[:, 0:128], ps_mm[:, TCM:TCM + 128],
           ps_o2[:, 0:128], ps_o2[:, TCM:TCM + 128],
           ps_y[:, 0:128], ps_bc[:, 0:128], ps_dt[:, 0:128]]
    trv_se = [0] * 7

    xio_v = [xio[:, 0:D], xio[:, 0:D]]
    Xt = X.rearrange("(n p) m -> n p m", p=128)
    OUTt = OUT.rearrange("(n p) m -> n p m", p=128)

    xio_te = [0, 0]
    for i in range(16):
        bi = i % 2
        @blk.sync
        def _(e, i=i, bi=bi, _xio_te_bi=xio_te[bi]):
            if _xio_te_bi:
                e.wait_ge(sems["TE"], _xio_te_bi)
            e.dma_start(out=xio_v[bi][:], in_=Xt[i]).then_inc(sems["SD"], 16)
        C["SD"] += 16
        sd_t = C["SD"]
        for f in range(NFT):
            q = (i * NFT + f) % 7
            @blk.tensor
            def _(e, i=i, f=f, q=q, bi=bi, sd_t=sd_t, _trv_se_q=trv_se[q]):
                if f == 0:
                    e.wait_ge(sems["SD"], sd_t)
                    if i == 0:
                        e.wait_ge(sems["SD"], sd_init)
                        e.wait_ge(sems["VE"], ve_A)
                if _trv_se_q:
                    e.wait_ge(sems["SE"], _trv_se_q)
                inc(e.transpose(trv[q][:], xio_v[bi][:, f * 128:(f + 1) * 128], idf_sb[:]), "TE")
            te_n = C["TE"]
            @blk.scalar
            def _(e, i=i, f=f, q=q, te_n=te_n):
                e.wait_ge(sems["TE"], te_n)
                inc(e.activation(xfm[:, f * L + i * 128:f * L + (i + 1) * 128], trv[q][:], AF.Identity), "SE")
            trv_se[q] = C["SE"]
        xio_te[0] = xio_te[1] = C["TE"]
    se_xtr = C["SE"]

    A_R, A_T0, A_BS = 0, 32, 64
    B_RS, B_T1, B_CS = 0, 32, 64
    C_M, C_V, C_RP = 0, 32, 64

    def rA(r, w):
        return statrA[r:r + 1, :w]

    def rB(r, w):
        return statrB[r:r + 1, :w]

    def rC(r, w):
        return statrC[r:r + 1, :w]

    def emit_ln_stats(src_tiles, width, ones_src, extra_rs=False):
        for f in range(NFT):
            @blk.scalar
            def _(e, f=f, _cve=C["VE"], _cte=C["TE"]):
                e.wait_ge(sems["VE"], _cve)
                if f > 0:
                    e.wait_ge(sems["TE"], _cte)
                inc(e.activation(sq_sb[:, :width], src_tiles[f], AF.Square), "SE")
            se_f = C["SE"]
            @blk.tensor
            def _(e, f=f, se_f=se_f):
                e.wait_ge(sems["SE"], se_f)
                e.matmul(ps_s1[0:1, :width], ones_src, src_tiles[f], start=(f == 0), stop=(f == NFT - 1))
                inc(e.matmul(ps_s2[0:1, :width], onescf[:], sq_sb[:, :width],
                             start=(f == 0), stop=(f == NFT - 1)), "TE")
        te_stats = C["TE"]
        @blk.vector
        def _(e):
            e.wait_ge(sems["TE"], te_stats)
            e.tensor_scalar(rC(C_M, width), ps_s1[0:1, :width], C_I1024[0:1, :], None, op0=MUL)
            e.tensor_scalar(rA(A_T0, width), ps_s2[0:1, :width], C_I1024[0:1, :], None, op0=MUL)
            e.tensor_tensor(rB(B_T1, width), rC(C_M, width), rC(C_M, width), op=MUL)
            inc(e.tensor_tensor(rC(C_V, width), rA(A_T0, width), rB(B_T1, width), op=SUB), "VE")
        ve_v = C["VE"]
        @blk.scalar
        def _(e):
            e.wait_ge(sems["VE"], ve_v)
            e.activation(rA(A_T0, width), rC(C_V, width), AF.Ln, bias=C_EPS[0:1, :], scale=1.0)
            i1 = e.activation(rA(A_R, width), rA(A_T0, width), AF.Exp, scale=C_NHALF[0:1, :])
            if extra_rs:
                i1 = e.activation(rB(B_T1, width), rC(C_V, width), AF.Ln)
            inc(i1, "SE")
        se_r = C["SE"]
        if extra_rs:
            @blk.vector
            def _(e):
                e.wait_ge(sems["SE"], se_r)
                inc(e.tensor_tensor(rC(C_V, width), rB(B_T1, width), rA(A_T0, width), op=SUB), "VE")
            ve_q = C["VE"]
            @blk.scalar
            def _(e):
                e.wait_ge(sems["VE"], ve_q)
                e.activation(rB(B_T1, width), rC(C_V, width), AF.Exp)
                e.activation(rB(B_T1, width), rB(B_T1, width), AF.Ln, bias=C_EPS[0:1, :], scale=1.0)
                inc(e.activation(rB(B_RS, width), rB(B_T1, width), AF.Exp, scale=C_NHALF[0:1, :]), "SE")
        @blk.vector
        def _(e, _cse=C["SE"]):
            e.wait_ge(sems["SE"], _cse)
            inc(e.nop(), "VE")
        return C["VE"]

    def emit_bcast(row_ap, width, dst, ve_ready, base=0):
        @blk.tensor
        def _(e, ve_ready=ve_ready):
            e.wait_ge(sems["VE"], ve_ready)
            for n in range((width + 511) // 512):
                sl = slice(n * 512, min(width, (n + 1) * 512))
                mm = e.matmul(dst[:, sl], onesA[base:base + 1, :], row_ap[:, sl], start=True, stop=True)
            inc(mm, "TE")
        return C["TE"]

    def emit_preln(t0, dst_fn):
        src_t = [xfm[:, f * L + t0: f * L + t0 + TCM] for f in range(NFT)]
        ve_mr = emit_ln_stats(src_t, TCM, onescf[:], extra_rs=False)
        emit_bcast(rC(C_M, TCM), TCM, ps_dt, ve_mr, base=C_M)
        te_b = emit_bcast(rA(A_R, TCM), TCM, ps_bc, ve_mr, base=A_R)
        for f in range(NFT):
            @blk.vector
            def _(e, f=f, te_b=te_b, _cse=C["SE"]):
                if f == 0:
                    e.wait_ge(sems["TE"], te_b)
                    e.wait_ge(sems["SE"], _cse)
                e.tensor_tensor(sq_sb[:, :TCM], src_t[f], ps_dt[:, :TCM], op=SUB)
                i = e.tensor_tensor(dst_fn(f), sq_sb[:, :TCM], ps_bc[:, :TCM], op=MUL)
                if f == NFT - 1:
                    inc(i, "VE")
        return C["VE"]

    def emit_postln_resid(of_tiles, t0, ones_src):
        emit_ln_stats(of_tiles, TCM, ones_src, extra_rs=False)
        ve_mr2 = C["VE"]
        emit_bcast(rA(A_R, TCM), TCM, ps_bc, ve_mr2, base=A_R)
        te_b2 = emit_bcast(rC(C_M, TCM), TCM, ps_dt, ve_mr2, base=C_M)
        for f in range(NFT):
            @blk.vector
            def _(e, f=f, te_b2=te_b2, _cse=C["SE"]):
                if f == 0:
                    e.wait_ge(sems["TE"], te_b2)
                    e.wait_ge(sems["SE"], _cse)
                e.tensor_tensor(sq_sb[:, :TCM], of_tiles[f], ps_dt[:, :TCM], op=SUB)
                e.tensor_tensor(sq_sb[:, :TCM], sq_sb[:, :TCM], ps_bc[:, :TCM], op=MUL)
                i = e.tensor_tensor(xfm[:, f * L + t0: f * L + t0 + TCM],
                                    xfm[:, f * L + t0: f * L + t0 + TCM], sq_sb[:, :TCM], op=ADD)
                if f == NFT - 1:
                    inc(i, "VE")
        return C["VE"]

    scan_done = [0, 0]
    y_done = [0, 0]
    for c in range(NCM):
        t0 = c * TCM
        src = [xfm[:, f * L + t0: f * L + t0 + TCM] for f in range(NFT)]

        emit_ln_stats(src, TCM, onescf[:], extra_rs=True)
        @blk.vector
        def _(e, _cse=C["SE"]):
            e.wait_ge(sems["SE"], _cse)
            inc(e.tensor_tensor(rC(C_RP, TCM), rA(A_R, TCM), rB(B_RS, TCM), op=MUL), "VE")
        ve_rp = C["VE"]
        emit_bcast(rC(C_RP, TCM), TCM, ps_bc, ve_rp, base=C_RP)
        emit_bcast(rC(C_M, TCM), TCM, ps_dt, ve_rp, base=C_M)
        te_bc = emit_bcast(rA(A_R, TCM), TCM, ps_y, ve_rp, base=A_R)
        @blk.scalar
        def _(e, te_bc=te_bc):
            e.wait_ge(sems["TE"], te_bc)
            e.activation(rrep_sb[:], ps_y[:, :TCM], AF.Identity)
            inc(e.activation(mrep_sb[:], ps_dt[:, :TCM], AF.Identity), "SE")
        se_rm = C["SE"]
        for f in range(NFT):
            @blk.vector
            def _(e, f=f, te_bc=te_bc, _cse=C["SE"]):
                if f == 0:
                    e.wait_ge(sems["TE"], te_bc)
                    e.wait_ge(sems["SE"], _cse)
                e.tensor_tensor(sq_sb[:, :TCM], src[f], ps_dt[:, :TCM], op=SUB)
                e.tensor_tensor(sq_sb[:, :TCM], sq_sb[:, :TCM], ps_bc[:, :TCM], op=MUL)
                i = e.tensor_scalar(xnbc[:, f * TCM:(f + 1) * TCM], sq_sb[:, :TCM],
                                    rmsw_sb[:, f:f + 1], None, op0=MUL)
                if f == NFT - 1:
                    inc(i, "VE")
        ve_xnb = C["VE"]
        if debug and c == 0:
            @blk.gpsimd
            def _(e):
                e.wait_ge(sems["VE"], ve_xnb)
                e.dma_start(out=dbg["d_xnb"][:], in_=xnbc[:]).then_inc(sems["GD"], 16)
            C["GD"] += 16
        if debug and c == 1:
            @blk.gpsimd
            def _(e):
                e.wait_ge(sems["VE"], ve_xnb)
                e.dma_start(out=dbg["d2_xnb"][:], in_=xnbc[:]).then_inc(sems["GD"], 16)
                e.dma_start(out=dbg["d2_stat"][:], in_=statrC[:]).then_inc(sems["GD"], 16)
            C["GD"] += 32

        ipw_v = [ipw_m[:, 0:1024], ipw_m[:, 1024:2048]]
        mm_done = [0, 0]
        evac_ve = [0, 0]
        evac_se = [0, 0]
        se_silu_prev = 0
        for m in range(32):
            buf = m % 2
            @blk.gpsimd
            def _(e, m=m, buf=buf, _mm_done_buf=mm_done[buf]):
                if m < 2:
                    e.wait_ge(sems["GD"], gd_init)
                if _mm_done_buf:
                    e.wait_ge(sems["TE"], _mm_done_buf)
                e.dma_start(out=ipw_v[buf][:], in_=IPW[m]).then_inc(sems["GD"], 16)
            C["GD"] += 16
            gd_t = C["GD"]
            pshalf = ps_mm[:, buf * TCM:(buf + 1) * TCM]
            @blk.tensor
            def _(e, m=m, buf=buf, gd_t=gd_t, pshalf=pshalf, ve_xnb=ve_xnb, _evac_se_buf=evac_se[buf], _evac_ve_buf=evac_ve[buf]):
                e.wait_ge(sems["GD"], gd_t)
                if m == 0:
                    e.wait_ge(sems["VE"], ve_xnb)
                if _evac_se_buf:
                    e.wait_ge(sems["SE"], _evac_se_buf)
                if _evac_ve_buf:
                    e.wait_ge(sems["VE"], _evac_ve_buf)
                for k in range(8):
                    mm = e.matmul(pshalf, ipw_v[buf][:, k * 128:(k + 1) * 128],
                                  xnbc[:, k * TCM:(k + 1) * TCM], start=(k == 0), stop=(k == 7))
                inc(mm, "TE")
            mm_done[buf] = C["TE"]
            if m < 16:
                dg = m
                @blk.vector
                def _(e, dg=dg, buf=buf, pshalf=pshalf, ssp=se_silu_prev, _mm_done_buf=mm_done[buf]):
                    e.wait_ge(sems["TE"], _mm_done_buf)
                    if ssp:
                        e.wait_ge(sems["SE"], ssp)
                    w3 = convw_sb[:, dg * 4 + 3: dg * 4 + 4]
                    e.tensor_scalar(sq_sb[:, :TCM], pshalf, w3, None, op0=MUL)
                    for kk in range(3):
                        sh = 3 - kk
                        wk = convw_sb[:, dg * 4 + kk: dg * 4 + kk + 1]
                        e.scalar_tensor_tensor(sq_sb[:, 0:sh], xtail[:, dg * 3 + kk: dg * 3 + 3], wk,
                                               sq_sb[:, 0:sh], op0=MUL, op1=ADD)
                        e.scalar_tensor_tensor(sq_sb[:, sh:TCM], pshalf[:, 0:TCM - sh], wk,
                                               sq_sb[:, sh:TCM], op0=MUL, op1=ADD)
                    inc(e.tensor_copy(xtail[:, dg * 3: dg * 3 + 3], pshalf[:, TCM - 3:TCM]), "VE")
                evac_ve[buf] = C["VE"]
                @blk.scalar
                def _(e, dg=dg, buf=buf, _evac_ve_buf=evac_ve[buf]):
                    e.wait_ge(sems["VE"], _evac_ve_buf)
                    inc(e.activation(xib[:, dg * (TCM + 3) + 3: (dg + 1) * (TCM + 3)], sq_sb[:, :TCM],
                                     AF.Silu, bias=convb_sb[:, dg:dg + 1], scale=1.0), "SE")
                evac_se[buf] = C["SE"]
                se_silu_prev = C["SE"]
            else:
                dg = m - 16
                @blk.scalar
                def _(e, dg=dg, buf=buf, pshalf=pshalf, _mm_done_buf=mm_done[buf]):
                    e.wait_ge(sems["TE"], _mm_done_buf)
                    inc(e.activation(zsb[:, dg * TCM:(dg + 1) * TCM], pshalf, AF.Silu), "SE")
                evac_se[buf] = C["SE"]
                evac_ve[buf] = 0
        se_inproj = C["SE"]

        @blk.tensor
        def _(e):
            e.wait_ge(sems["SE"], se_inproj)
            for dg in range(NDG):
                mm = e.matmul(ps_dbcv[:, :], xpw_sb[:, dg * 96:(dg + 1) * 96],
                              xib[:, dg * (TCM + 3) + 3:(dg + 1) * (TCM + 3)],
                              start=(dg == 0), stop=(dg == NDG - 1))
            inc(mm, "TE")
        te_dbc = C["TE"]
        @blk.scalar
        def _(e, te_dbc=te_dbc):
            e.wait_ge(sems["TE"], te_dbc)
            inc(e.activation(dbc_sb[:], ps_dbcv[:, :], AF.Identity), "SE")
        se_dbc = C["SE"]

        se_stage = 0
        for s in range(DS):
            @blk.tensor
            def _(e, s=s, se_stage=se_stage, _cve=C["VE"]):
                if s == 0:
                    e.wait_ge(sems["SE"], se_dbc)
                    e.wait_ge(sems["VE"], _cve)
                else:
                    e.wait_ge(sems["SE"], se_stage)
                e.matmul(ps_dt[0:1, :], indv_sb[:, s:s + 1], dbc_sb[:, :], start=True, stop=True)
                inc(e.matmul(ps_dt[32:33, :], indv_sb[:, 16 + s:17 + s], dbc_sb[:, :], start=True, stop=True), "TE")
            te_ex = C["TE"]
            @blk.scalar
            def _(e, s=s, te_ex=te_ex):
                e.wait_ge(sems["TE"], te_ex)
                e.activation(rA(A_BS, TCM), ps_dt[0:1, :], AF.Identity)
                inc(e.activation(rB(B_CS, TCM), ps_dt[32:33, :], AF.Identity), "SE")
            se_stage = C["SE"]
            @blk.tensor
            def _(e, s=s, se_stage=se_stage):
                e.wait_ge(sems["SE"], se_stage)
                e.matmul(ps_bc[:, :], onesA[A_BS:A_BS + 1, :], rA(A_BS, TCM), start=True, stop=True)
                inc(e.matmul(ps_y[:, :], onesA[B_CS:B_CS + 1, :], rB(B_CS, TCM), start=True, stop=True), "TE")
            te_b = C["TE"]
            @blk.scalar
            def _(e, s=s, te_b=te_b):
                e.wait_ge(sems["TE"], te_b)
                e.activation(Brep[:, s * TCM:(s + 1) * TCM], ps_bc[:, :], AF.Identity)
                inc(e.activation(Crep[:, s * TCM:(s + 1) * TCM], ps_y[:, :], AF.Identity), "SE")
        se_bcrep = C["SE"]

        if c == 0:
            @blk.gpsimd
            def _(e):
                for m in range(8):
                    e.dma_start(out=W2Q[m], in_=W2T[m]).then_inc(sems["GD"], 16)
            C["GD"] += 128
        elif c == 1:
            @blk.gpsimd
            def _(e):
                for m in range(16):
                    e.dma_start(out=W1B[m], in_=W1T[m]).then_inc(sems["GD"], 16)
            C["GD"] += 256
        elif c == 2:
            @blk.gpsimd
            def _(e):
                for m in range(16):
                    e.dma_start(out=W1B[16 + m], in_=W1T[16 + m]).then_inc(sems["GD"], 16)
            C["GD"] += 256
        elif c == 3:
            @blk.gpsimd
            def _(e):
                for m in range(16):
                    e.dma_start(out=WQKB[m], in_=WQKT[m]).then_inc(sems["GD"], 16)
            C["GD"] += 256
        if c == 0:
            gd_w2q = C["GD"]
        elif c == 2:
            gd_w1b = C["GD"]
        elif c == 3:
            gd_wqkb = C["GD"]

        ve_u_prev = 0
        for dg in range(NDG):
            @blk.tensor
            def _(e, dg=dg, _cse=C["SE"]):
                if dg == 0:
                    e.wait_ge(sems["SE"], se_bcrep)
                else:
                    e.wait_ge(sems["SE"], _cse)
                inc(e.matmul(ps_dt[:, :], dtw_sb[:, dg * 128:(dg + 1) * 128], dbc_sb[0:64, :],
                             start=True, stop=True), "TE")
            te_dt = C["TE"]
            @blk.scalar
            def _(e, dg=dg, te_dt=te_dt, vup=ve_u_prev):
                e.wait_ge(sems["TE"], te_dt)
                if vup:
                    e.wait_ge(sems["VE"], vup)
                e.activation(delta_sb[:], ps_dt[:, :], AF.Exp, bias=dtb_sb[:, dg:dg + 1], scale=1.0)
                inc(e.activation(delta_sb[:], delta_sb[:], AF.Ln, bias=C_ONE, scale=1.0), "SE")
            se_delta = C["SE"]
            @blk.vector
            def _(e, dg=dg, se_delta=se_delta):
                e.wait_ge(sems["SE"], se_delta)
                inc(e.tensor_tensor(u_sb[:], delta_sb[:],
                                    xib[:, dg * (TCM + 3) + 3:(dg + 1) * (TCM + 3)], op=MUL), "VE")
            ve_u = C["VE"]
            ve_u_prev = ve_u

            for s in range(DS):
                sl = s % 2
                asl = a_sb[:, sl * TCM:(sl + 1) * TCM]
                bsl = b_sb[:, sl * TCM:(sl + 1) * TCM]
                hsl = h_sb[:, sl * TCM:(sl + 1) * TCM]
                hcsl = hc_sb[:, sl * TCM:(sl + 1) * TCM]
                @blk.scalar
                def _(e, dg=dg, s=s, sl=sl, asl=asl, sd=scan_done[sl]):
                    if sd:
                        e.wait_ge(sems["VE"], sd)
                    inc(e.activation(asl, delta_sb[:], AF.Exp,
                                     scale=A_sb[:, dg * DS + s: dg * DS + s + 1]), "SE")
                se_a = C["SE"]
                @blk.vector
                def _(e, dg=dg, s=s, sl=sl, se_a=se_a, asl=asl, bsl=bsl, hsl=hsl, hcsl=hcsl,
                      ve_u=ve_u, yd=y_done[sl]):
                    if s == 0:
                        e.wait_ge(sems["VE"], ve_u)
                    if yd:
                        e.wait_ge(sems["TE"], yd)
                    e.tensor_tensor(bsl, u_sb[:], Brep[:, s * TCM:(s + 1) * TCM], op=MUL)
                    e.wait_ge(sems["SE"], se_a)
                    e.tensor_tensor_scan(hsl, asl, bsl, h0_sb[:, dg * DS + s: dg * DS + s + 1],
                                         op0=MUL, op1=ADD)
                    e.tensor_copy(h0_sb[:, dg * DS + s: dg * DS + s + 1], hsl[:, TCM - 1:TCM])
                    inc(e.tensor_tensor(hcsl, hsl, Crep[:, s * TCM:(s + 1) * TCM], op=MUL), "VE")
                scan_done[sl] = C["VE"]
                @blk.tensor
                def _(e, s=s, sl=sl, hcsl=hcsl, vs=scan_done[sl]):
                    e.wait_ge(sems["VE"], vs)
                    inc(e.matmul(ps_y[:, :], idb_sb[:], hcsl, start=(s == 0), stop=(s == DS - 1)), "TE")
                y_done[sl] = C["TE"]
            te_y = C["TE"]
            @blk.vector
            def _(e, dg=dg, te_y=te_y):
                e.wait_ge(sems["TE"], te_y)
                e.scalar_tensor_tensor(sq_sb[:, :TCM], xib[:, dg * (TCM + 3) + 3:(dg + 1) * (TCM + 3)],
                                       D_sb[:, dg:dg + 1], ps_y[:, :], op0=MUL, op1=ADD)
                inc(e.tensor_tensor(y2b[:, dg * TCM:(dg + 1) * TCM], sq_sb[:, :TCM],
                                    zsb[:, dg * TCM:(dg + 1) * TCM], op=MUL), "VE")
        ve_y2 = C["VE"]
        if debug and c == 0:
            @blk.gpsimd
            def _(e):
                e.wait_ge(sems["VE"], ve_y2)
                e.dma_start(out=dbg["d_y2"][:], in_=y2b[:]).then_inc(sems["GD"], 16)
            C["GD"] += 16
        if debug and c == 1:
            @blk.gpsimd
            def _(e):
                e.wait_ge(sems["VE"], ve_y2)
                e.dma_start(out=dbg["d2_y2"][:], in_=y2b[:]).then_inc(sems["GD"], 16)
                e.dma_start(out=dbg["d2_xib"][:], in_=xib[:]).then_inc(sems["GD"], 16)
                e.dma_start(out=dbg["d2_dbc"][:], in_=dbc_sb[:]).then_inc(sems["GD"], 16)
            C["GD"] += 48

        opw_v = [opw_m[:, 0:2048], opw_m[:, 2048:4096]]
        opmm_done = [0, 0]
        of_done = [0, 0]
        ps_op = [ps_mm[:, 0:TCM], ps_mm[:, TCM:2 * TCM]]
        for m in range(NFT):
            buf = m % 2
            @blk.gpsimd
            def _(e, m=m, buf=buf, _opmm_done_buf=opmm_done[buf]):
                if _opmm_done_buf:
                    e.wait_ge(sems["TE"], _opmm_done_buf)
                e.dma_start(out=opw_v[buf][:], in_=OPW[m]).then_inc(sems["GD"], 16)
            C["GD"] += 16
            gd_t = C["GD"]
            @blk.tensor
            def _(e, m=m, buf=buf, gd_t=gd_t, ve_y2=ve_y2, _of_done_buf=of_done[buf]):
                e.wait_ge(sems["GD"], gd_t)
                if m == 0:
                    e.wait_ge(sems["VE"], ve_y2)
                if _of_done_buf:
                    e.wait_ge(sems["VE"], _of_done_buf)
                for k in range(16):
                    mm = e.matmul(ps_op[buf], opw_v[buf][:, k * 128:(k + 1) * 128],
                                  y2b[:, k * TCM:(k + 1) * TCM], start=(k == 0), stop=(k == 15))
                inc(mm, "TE")
            opmm_done[buf] = C["TE"]
            @blk.vector
            def _(e, m=m, buf=buf, c=c, se_rm=se_rm, _opmm_done_buf=opmm_done[buf]):
                e.wait_ge(sems["TE"], _opmm_done_buf)
                if m == 0:
                    e.wait_ge(sems["SE"], se_rm)
                e.tensor_tensor(sq_sb[:, :TCM], xfm[:, m * L + t0: m * L + t0 + TCM], mrep_sb[:], op=SUB)
                e.tensor_tensor(sq_sb[:, :TCM], sq_sb[:, :TCM], rrep_sb[:], op=MUL)
                inc(e.tensor_tensor(of16[:, m * TCM:(m + 1) * TCM], sq_sb[:, :TCM], ps_op[buf], op=ADD), "VE")
            of_done[buf] = C["VE"]

        if debug and c == 1:
            @blk.gpsimd
            def _(e, _cve=C["VE"]):
                e.wait_ge(sems["VE"], _cve)
                e.dma_start(out=dbg["d2_of"][:], in_=of16[:]).then_inc(sems["GD"], 16)
            C["GD"] += 16
        of_t = [of16[:, f * TCM:(f + 1) * TCM] for f in range(NFT)]
        emit_postln_resid(of_t, t0, onescb[:])
    ve_mamba = C["VE"]
    if debug:
        @blk.gpsimd
        def _(e):
            e.wait_ge(sems["VE"], ve_mamba)
            e.dma_start(out=dbg["d_x1"][:], in_=xfm[:]).then_inc(sems["GD"], 16)
        C["GD"] += 16

    es_m.close()

    es_f = contextlib.ExitStack()

    def sbf(name, shape, dt):
        return es_f.enter_context(nc.sbuf_tensor(name, list(shape), dt))

    xnb2 = sbf("xnb2", [128, NFT * TCM], BF16)
    offq = sbf("offq", [128, NFT * TCM], FP16)
    hq_sb = sbf("hq_sb", [128, 32 * TCM], FP8)
    wq_all = sbf("wq_all", [128, 8 * H], FP8)
    b1_sb = sbf("b1_sb", [128, 32], F32)
    b2q_sb = sbf("b2q_sb", [128, NFT], F32)
    b2h_sb = sbf("b2h_sb", [128, NFT], FP16)

    @blk.sync
    def _(e):
        e.wait_ge(sems["VE"], ve_mamba)
        e.wait_ge(sems["GD"], gd_w2q)
        e.dma_start(out=b1_sb[:], in_=B1[:]).then_inc(sems["SD"], 16)
        e.dma_start(out=b2q_sb[:], in_=B2[:]).then_inc(sems["SD"], 16)
        for m in range(8):
            e.dma_start(out=wq_all[:, m * H:(m + 1) * H], in_=W2Q[m]).then_inc(sems["SD"], 16)
    C["SD"] += 160
    sd_f = C["SD"]
    @blk.gpsimd
    def _(e):
        for m in range(8):
            e.dma_start(out=WOUTB[m], in_=WOUTT[m]).then_inc(sems["GD"], 16)
        for m in range(8):
            e.dma_start(out=L1B[m], in_=L1WT[m]).then_inc(sems["GD"], 16)
    C["GD"] += 256
    gd_woutb = C["GD"]
    @blk.scalar
    def _(e):
        e.wait_ge(sems["SD"], sd_f)
        e.activation(b2h_sb[:], b2q_sb[:], AF.Identity)
        inc(e.activation(b2q_sb[:], b2h_sb[:], AF.Identity), "SE")
    se_b2q = C["SE"]

    mmf_prev = [0, 0]
    for c in range(NCM):
        t0 = c * TCM
        ve_xn2 = emit_preln(t0, lambda f: xnb2[:, f * TCM:(f + 1) * TCM])
        mmf_done = [0, 0]
        epi_se = [0, 0]
        for m in range(32):
            buf = m % 2
            @blk.sync
            def _(e, m=m, buf=buf, c=c, _mmf_done_buf=(mmf_done[buf] or mmf_prev[buf])):
                if c == 0 and m < 2:
                    e.wait_ge(sems["VE"], ve_mamba)
                    e.wait_ge(sems["GD"], gd_w1b)
                if _mmf_done_buf:
                    e.wait_ge(sems["TE"], _mmf_done_buf)
                e.dma_start(out=ipw_m[:, buf * 1024:(buf + 1) * 1024], in_=W1B[m]).then_inc(sems["SD"], 16)
            C["SD"] += 16
            gd_t = C["SD"]
            pshalf = ps_mm[:, buf * TCM:(buf + 1) * TCM]
            @blk.tensor
            def _(e, m=m, buf=buf, gd_t=gd_t, pshalf=pshalf, ve_xn2=ve_xn2, _epi_se_buf=epi_se[buf]):
                e.wait_ge(sems["SD"], gd_t)
                if m == 0:
                    e.wait_ge(sems["VE"], ve_xn2)
                if _epi_se_buf:
                    e.wait_ge(sems["SE"], _epi_se_buf)
                for k in range(8):
                    mm = e.matmul(pshalf, ipw_m[:, buf * 1024 + k * 128: buf * 1024 + (k + 1) * 128],
                                  xnb2[:, k * TCM:(k + 1) * TCM], start=(k == 0), stop=(k == 7))
                inc(mm, "TE")
            mmf_done[buf] = C["TE"]
            @blk.scalar
            def _(e, m=m, buf=buf, pshalf=pshalf, _mmf_done_buf=mmf_done[buf]):
                e.wait_ge(sems["TE"], _mmf_done_buf)
                if m == 0:
                    e.wait_ge(sems["SE"], se_b2q)
                inc(e.activation(hq_sb[:, m * TCM:(m + 1) * TCM], pshalf, AF.Relu,
                                 bias=b1_sb[:, m:m + 1], scale=1.0), "SE")
            epi_se[buf] = C["SE"]
        mm2_done = [0, 0]
        epi2_se = [0, 0]
        for m in range(NFT):
            buf = m % 2
            psh2 = ps_o2[:, buf * TCM:(buf + 1) * TCM]
            @blk.tensor
            def _(e, m=m, buf=buf, psh2=psh2, _epi_se_0=epi_se[0], _epi_se_1=epi_se[1], _epi2_se_buf=epi2_se[buf], _sd_f=sd_f):
                if m == 0:
                    e.wait_ge(sems["SD"], _sd_f)
                    e.wait_ge(sems["SE"], _epi_se_0)
                    e.wait_ge(sems["SE"], _epi_se_1)
                if _epi2_se_buf:
                    e.wait_ge(sems["SE"], _epi2_se_buf)
                for k in range(32):
                    mm = e.matmul(psh2, wq_all[:, m * H + k * 128: m * H + (k + 1) * 128],
                                  hq_sb[:, k * TCM:(k + 1) * TCM], start=(k == 0), stop=(k == 31))
                inc(mm, "TE")
            mm2_done[buf] = C["TE"]
            @blk.scalar
            def _(e, m=m, buf=buf, psh2=psh2, _mm2_done_buf=mm2_done[buf]):
                e.wait_ge(sems["TE"], _mm2_done_buf)
                inc(e.activation(offq[:, m * TCM:(m + 1) * TCM], psh2, AF.Identity,
                                 bias=b2q_sb[:, m:m + 1], scale=1.0), "SE")
            epi2_se[buf] = C["SE"]
        of_t = [offq[:, f * TCM:(f + 1) * TCM] for f in range(NFT)]
        emit_postln_resid(of_t, t0, onesch[:])
        mmf_prev = list(mmf_done)
    ve_ffn0 = C["VE"]
    if debug:
        @blk.gpsimd
        def _(e):
            e.wait_ge(sems["VE"], ve_ffn0)
            e.dma_start(out=dbg["d_wq"][:], in_=wq_all[:]).then_inc(sems["GD"], 16)
            e.dma_start(out=dbg["d_hq"][:], in_=hq_sb[:]).then_inc(sems["GD"], 16)
            e.dma_start(out=dbg["d_x2"][:], in_=xfm[:]).then_inc(sems["GD"], 16)
        C["GD"] += 16
    es_f.close()

    es_a = contextlib.ExitStack()

    def sba(name, shape, dt):
        return es_a.enter_context(nc.sbuf_tensor(name, list(shape), dt))

    VT = sba("VT", [128, 16 * 1040], BF16)
    negones = sba("negones", [1, 128], F32)
    bq_sb = sba("bq_sb", [128, NFT], F32)
    bk_sb = sba("bk_sb", [128, NFT], F32)
    bout_sb = sba("bout_sb", [128, NFT], F32)
    bvrow = sba("bvrow", [1, D], BF16)
    Qb = sba("Qb", [128, 8 * L], BF16)
    es_aK = contextlib.ExitStack()
    Kb = es_aK.enter_context(nc.sbuf_tensor("Kb", [128, 8 * L], BF16))
    es_a1 = contextlib.ExitStack()
    wv_sb = es_a1.enter_context(nc.sbuf_tensor("wv_sb", [128, 8 * 1024], BF16))
    xnbq = es_a1.enter_context(nc.sbuf_tensor("xnbq", [128, NFT * TCM], BF16))

    @blk.sync
    def _(e):
        e.wait_ge(sems["VE"], ve_ffn0)
        e.dma_start(out=bq_sb[:], in_=BQ[:]).then_inc(sems["SD"], 16)
        e.dma_start(out=bk_sb[:], in_=BK[:]).then_inc(sems["SD"], 16)
        e.dma_start(out=bout_sb[:], in_=BOUT[:]).then_inc(sems["SD"], 16)
    C["SD"] += 48
    sd_a = C["SD"]
    @blk.gpsimd
    def _(e):
        e.wait_ge(sems["VE"], ve_ffn0)
        e.dma_start(out=bvrow[:], in_=BV[:]).then_inc(sems["GD"], 16)
        for k in range(8):
            e.dma_start(out=wv_sb[:, k * 1024:(k + 1) * 1024], in_=WVR[k]).then_inc(sems["GD"], 16)
    C["GD"] += 144
    gd_wv = C["GD"]
    @blk.vector
    def _(e):
        e.wait_ge(sems["SD"], sd_a)
        e.memset(negones[:], -1.0)
        e.tensor_scalar(bq_sb[:], bq_sb[:], C_QS, None, op0=MUL)
        vt3 = VT[:].rearrange("p (t h c) -> p t h c", h=NH, c=65)
        inc(e.memset(vt3[:, :, :, 64:65], 1.0), "VE")
    ve_abias = C["VE"]

    se_vt_prev = 0
    mmq_done = [0, 0]
    for qc in range(NCM):
        t0 = qc * TCM
        ve_xq = emit_preln(t0, lambda f: xnbq[:, f * TCM:(f + 1) * TCM])
        for tt in range(4):
            t16 = qc * 4 + tt
            @blk.tensor
            def _(e, tt=tt, t16=t16, ve_xq=ve_xq, svp=se_vt_prev):
                if t16 == 0:
                    e.wait_ge(sems["GD"], gd_wv)
                e.wait_ge(sems["VE"], ve_xq)
                if svp:
                    e.wait_ge(sems["SE"], svp)
                for k in range(8):
                    for nh in range(2):
                        e.matmul(ps_mm[:, nh * TCM:(nh + 1) * TCM],
                                 xnbq[:, k * TCM + tt * 128: k * TCM + (tt + 1) * 128],
                                 wv_sb[:, k * 1024 + nh * TCM: k * 1024 + (nh + 1) * TCM],
                                 start=(k == 0), stop=False)
                for nh in range(2):
                    mm = e.matmul(ps_mm[:, nh * TCM:(nh + 1) * TCM], onesb[0:1, :],
                                  bvrow[0:1, nh * TCM:(nh + 1) * TCM], start=False, stop=True)
                inc(mm, "TE")
            te_v = C["TE"]
            @blk.scalar
            def _(e, t16=t16, te_v=te_v):
                e.wait_ge(sems["TE"], te_v)
                if t16 == 0:
                    e.wait_ge(sems["VE"], ve_abias)
                dst = VT[:, t16 * 1040:(t16 + 1) * 1040].rearrange("p (h c) -> p h c", c=65)[:, :, 0:64]
                src_ = ps_mm[:, 0:1024].rearrange("p (h c) -> p h c", c=64)
                inc(e.activation(dst, src_, AF.Identity), "SE")
            se_vt_prev = C["SE"]
        for m in range(16):
            buf = m % 2
            @blk.sync
            def _(e, m=m, buf=buf, qc=qc, svp=se_vt_prev, _mmq_done_buf=mmq_done[buf]):
                if qc == 0 and m < 2:
                    e.wait_ge(sems["SE"], svp)
                    e.wait_ge(sems["GD"], gd_wqkb)
                if _mmq_done_buf:
                    e.wait_ge(sems["TE"], _mmq_done_buf)
                e.dma_start(out=ipw_m[:, buf * 1024:(buf + 1) * 1024], in_=WQKB[m]).then_inc(sems["SD"], 16)
            C["SD"] += 16
            gd_t = C["SD"]
            pshalf = ps_mm[:, buf * TCM:(buf + 1) * TCM]
            @blk.tensor
            def _(e, m=m, buf=buf, gd_t=gd_t, pshalf=pshalf, _cse=C["SE"]):
                e.wait_ge(sems["SD"], gd_t)
                e.wait_ge(sems["SE"], _cse)
                for k in range(8):
                    mm = e.matmul(pshalf, ipw_m[:, buf * 1024 + k * 128: buf * 1024 + (k + 1) * 128],
                                  xnbq[:, k * TCM:(k + 1) * TCM], start=(k == 0), stop=(k == 7))
                inc(mm, "TE")
            mmq_done[buf] = C["TE"]
            @blk.scalar
            def _(e, m=m, qc=qc, buf=buf, pshalf=pshalf, _mmq_done_buf=mmq_done[buf]):
                e.wait_ge(sems["TE"], _mmq_done_buf)
                if m < 8:
                    inc(e.activation(Qb[:, m * L + qc * TCM: m * L + (qc + 1) * TCM], pshalf,
                                     AF.Identity, bias=bq_sb[:, m:m + 1], scale=C_QS), "SE")
                else:
                    inc(e.activation(Kb[:, (m - 8) * L + qc * TCM: (m - 8) * L + (qc + 1) * TCM], pshalf,
                                     AF.Identity, bias=bk_sb[:, m - 8:m - 7], scale=1.0), "SE")
    se_qk = C["SE"]
    es_a1.close()
    es_a2 = contextlib.ExitStack()
    arow = es_a2.enter_context(nc.sbuf_tensor("arow", [96, 1024], F32))
    rdrep = es_a2.enter_context(nc.sbuf_tensor("rdrep", [64, TCM], F32))
    wrow = es_a2.enter_context(nc.sbuf_tensor("wrow", [1, TCM], BF16))
    wsb = es_a2.enter_context(nc.sbuf_tensor("wsb", [128, TCM], BF16))
    tsc = es_a2.enter_context(nc.sbuf_tensor("tsc", [128, TCM], BF16))
    expP = es_a2.enter_context(nc.sbuf_tensor("expP", [128, 16 * TCM], BF16))
    if debug:
        @blk.gpsimd
        def _(e):
            e.wait_ge(sems["SE"], se_qk)
            e.dma_start(out=dbg["d_q"][:], in_=Qb[:]).then_inc(sems["GD"], 16)
            e.dma_start(out=dbg["d_vt"][:], in_=VT[:]).then_inc(sems["GD"], 16)
        C["GD"] += 32

    @blk.vector
    def _(e):
        e.wait_ge(sems["SE"], se_qk)
        e.memset(arow[0:1, 0:TCM], 1.0)
        inc(e.memset(arow[64:65, 0:TCM], 1.0), "VE")

    se_exp_slot = [0, 0]
    ve_norm_par = [0, 0]
    ve_t_last = 0
    ve_sd1 = 0
    prev = None
    it_list = [(h, qv) for h in range(NH) for qv in range(4)]
    for it in range(65):
        cur = it_list[it] if it < 64 else None
        if cur is not None:
            h, qv = cur
            tile = h // 2
            po = (h % 2) * 64
            qoff = tile * L + qv * TCM
        for kc in range(16):
            sl = kc % 2
            if cur is not None:
                @blk.tensor
                def _(e, kc=kc, sl=sl, po=po, tile=tile, qoff=qoff,
                      ses=se_exp_slot[sl], first=(it == 0 and kc < 2)):
                    if first:
                        e.wait_ge(sems["SE"], se_qk)
                    if ses:
                        e.wait_ge(sems["SE"], ses)
                    inc(e.matmul(ps_mm[:, sl * TCM:(sl + 1) * TCM],
                                 Kb[po:po + 64, tile * L + kc * 128: tile * L + (kc + 1) * 128],
                                 Qb[po:po + 64, qoff:qoff + TCM],
                                 start=True, stop=True), "TE")
            if prev is not None:
                ph = prev["h"]
                osl = prev["osl"]
                @blk.tensor
                def _(e, kc=kc, ph=ph, osl=osl, vsq=prev["ve_sq"][kc],
                      vn=(ve_norm_par[osl] if kc == 0 else 0)):
                    e.wait_ge(sems["VE"], vsq)
                    if vn:
                        e.wait_ge(sems["VE"], vn)
                    inc(e.matmul(ps_o2[0:65, osl * TCM:(osl + 1) * TCM],
                                 VT[:, kc * 1040 + ph * 65: kc * 1040 + (ph + 1) * 65],
                                 expP[:, kc * TCM:(kc + 1) * TCM],
                                 start=(kc == 0), stop=(kc == 15)), "TE")
            if cur is not None:
                @blk.scalar
                def _(e, kc=kc, sl=sl, te_w=C["TE"]):
                    e.wait_ge(sems["TE"], te_w)
                    inc(e.activation(expP[:, kc * TCM:(kc + 1) * TCM],
                                     ps_mm[:, sl * TCM:(sl + 1) * TCM],
                                     AF.Exp, bias=C_M16, scale=C_SPARE), "SE")
                se_exp_slot[sl] = C["SE"]
                @blk.tensor
                def _(e, kc=kc, _se=C["SE"], _vs=(ve_sd1 if kc == 0 else 0)):
                    e.wait_ge(sems["SE"], _se)
                    if _vs:
                        e.wait_ge(sems["VE"], _vs)
                    inc(e.matmul(ps_y[0:1, :], onescb[:], expP[:, kc * TCM:(kc + 1) * TCM],
                                 start=(kc == 0), stop=(kc == 15)), "TE")
        if prev is not None:
            @blk.vector
            def _(e, osl=prev["osl"], _te=C["TE"]):
                e.wait_ge(sems["TE"], _te)
                inc(e.tensor_scalar(arow[64:65, 0:TCM], ps_o2[64:65, osl * TCM:(osl + 1) * TCM],
                                    C_ONE[64:65, :], None, op0=MUL), "VE")
            ve_sd2 = C["VE"]
            @blk.scalar
            def _(e, _ve=ve_sd2):
                e.wait_ge(sems["VE"], _ve)
                e.activation(arow[64:65, TCM:2 * TCM], arow[64:65, 0:TCM], AF.Ln)
                inc(e.activation(arow[64:65, TCM:2 * TCM], arow[64:65, TCM:2 * TCM], AF.Exp,
                                 scale=C_NEG1[0:1, :]), "SE")
            se_rd = C["SE"]
        if cur is not None:
            @blk.vector
            def _(e, _te=C["TE"]):
                e.wait_ge(sems["TE"], _te)
                inc(e.tensor_scalar(arow[0:1, 0:TCM], ps_y[0:1, :],
                                    C_ONE[0:1, :], None, op0=MUL), "VE")
            ve_sd1 = C["VE"]
            @blk.scalar
            def _(e, _ve=ve_sd1):
                e.wait_ge(sems["VE"], _ve)
                e.activation(arow[0:1, TCM:2 * TCM], arow[0:1, 0:TCM], AF.Ln)
                inc(e.activation(wrow[0:1, :], arow[0:1, TCM:2 * TCM], AF.Exp,
                                 scale=C_NEG1[0:1, :]), "SE")
            se_w = C["SE"]
        if prev is not None:
            @blk.tensor
            def _(e, _se=se_rd):
                e.wait_ge(sems["SE"], _se)
                inc(e.matmul(ps_dt[0:64, :], onesA[64:65, 0:64], arow[64:65, TCM:2 * TCM],
                             start=True, stop=True), "TE")
            te_rdb = C["TE"]
        if cur is not None:
            @blk.tensor
            def _(e, _se=se_w):
                e.wait_ge(sems["SE"], _se)
                inc(e.matmul(ps_bc[:, :], onesb[0:1, :], wrow[0:1, :],
                             start=True, stop=True), "TE")
            te_wb = C["TE"]
        if prev is not None:
            @blk.vector
            def _(e, _te=te_rdb):
                e.wait_ge(sems["TE"], _te)
                inc(e.tensor_scalar(rdrep[:, :], ps_dt[0:64, :], C_ONE[0:64, :],
                                    None, op0=MUL), "VE")
        if cur is not None:
            @blk.scalar
            def _(e, _te=te_wb, _vt=ve_t_last):
                e.wait_ge(sems["TE"], _te)
                if _vt:
                    e.wait_ge(sems["VE"], _vt)
                inc(e.activation(wsb[:, :], ps_bc[:, :], AF.Identity), "SE")
            se_wc = C["SE"]
        if prev is not None:
            ph = prev["h"]
            osl = prev["osl"]
            ppo = (ph % 2) * 64
            pqoff = (ph // 2) * L + prev["qv"] * TCM
            @blk.vector
            def _(e, osl=osl, ppo=ppo, pqoff=pqoff):
                inc(e.tensor_tensor(Qb[ppo:ppo + 64, pqoff:pqoff + TCM],
                                    ps_o2[0:64, osl * TCM:(osl + 1) * TCM],
                                    rdrep[:, :], op=MUL), "VE")
            ve_norm_par[osl] = C["VE"]
        if cur is not None:
            ve_sq = [0] * 16
            for kc in range(16):
                @blk.vector
                def _(e, kc=kc, _se=(se_wc if kc == 0 else 0)):
                    if _se:
                        e.wait_ge(sems["SE"], _se)
                    e.tensor_tensor(tsc[:, :], expP[:, kc * TCM:(kc + 1) * TCM], wsb[:, :], op=MUL)
                    inc(e.tensor_tensor(expP[:, kc * TCM:(kc + 1) * TCM], tsc[:, :], tsc[:, :],
                                        op=MUL), "VE")
                ve_sq[kc] = C["VE"]
            ve_t_last = ve_sq[15]
            prev = {"h": h, "qv": qv, "osl": it % 2, "ve_sq": ve_sq}
    ve_attn_done = C["VE"]
    if debug:
        @blk.gpsimd
        def _(e):
            e.wait_ge(sems["VE"], ve_attn_done)
            e.dma_start(out=dbg["d_ao"][:], in_=Qb[:]).then_inc(sems["GD"], 16)
            e.dma_start(out=dbg["d_den"][:], in_=arow[:]).then_inc(sems["GD"], 16)
        C["GD"] += 32

    es_a2.close()
    es_aK.close()
    wout_res = sba("wout_res", [128, 8 * 1024], BF16)
    ofa = sba("ofa", [128, NFT * TCM], F32)
    @blk.sync
    def _(e):
        e.wait_ge(sems["VE"], ve_attn_done)
        e.wait_ge(sems["GD"], gd_woutb)
        for m in range(8):
            e.dma_start(out=wout_res[:, m * 1024:(m + 1) * 1024], in_=WOUTB[m]).then_inc(sems["SD"], 16)
    C["SD"] += 128
    gd_wo = C["SD"]
    for c in range(NCM):
        t0 = c * TCM
        for m in range(NFT):
            pshalf = ps_mm[:, (m % 2) * TCM:((m % 2) + 1) * TCM]
            @blk.tensor
            def _(e, m=m, c=c, pshalf=pshalf, _cse=C["SE"]):
                if m == 0 and c == 0:
                    e.wait_ge(sems["SD"], gd_wo)
                    e.wait_ge(sems["VE"], ve_attn_done)
                e.wait_ge(sems["SE"], _cse)
                for k in range(8):
                    mm = e.matmul(pshalf, wout_res[:, m * 1024 + k * 128: m * 1024 + (k + 1) * 128],
                                  Qb[:, k * L + t0: k * L + t0 + TCM], start=(k == 0), stop=(k == 7))
                inc(mm, "TE")
            te_wo = C["TE"]
            @blk.scalar
            def _(e, m=m, te_wo=te_wo, pshalf=pshalf):
                e.wait_ge(sems["TE"], te_wo)
                inc(e.activation(ofa[:, m * TCM:(m + 1) * TCM], pshalf, AF.Identity,
                                 bias=bout_sb[:, m:m + 1], scale=1.0), "SE")
        of_t = [ofa[:, f * TCM:(f + 1) * TCM] for f in range(NFT)]
        if debug and c == 0:
            @blk.gpsimd
            def _(e, _cse=C["SE"]):
                e.wait_ge(sems["SE"], _cse)
                e.dma_start(out=dbg["d_ofa"][:], in_=ofa[:]).then_inc(sems["GD"], 16)
            C["GD"] += 16
        emit_postln_resid(of_t, t0, onescf[:])
    ve_attn = C["VE"]
    if debug:
        @blk.gpsimd
        def _(e):
            e.wait_ge(sems["VE"], ve_attn)
            e.dma_start(out=dbg["d_x3"][:], in_=xfm[:]).then_inc(sems["GD"], 16)
        C["GD"] += 16
    es_a.close()

    es_l = contextlib.ExitStack()

    def sbl(name, shape, dt):
        return es_l.enter_context(nc.sbuf_tensor(name, list(shape), dt))

    xnbL = sbl("xnbL", [128, NFT * L], BF16)
    l1res = sbl("l1res", [128, 8 * 1024], BF16)
    ofl = sbl("ofl", [128, NFT * TCM], F32)

    @blk.sync
    def _(e):
        e.wait_ge(sems["VE"], ve_attn)
        e.wait_ge(sems["GD"], gd_woutb)
        for m in range(8):
            e.dma_start(out=l1res[:, m * 1024:(m + 1) * 1024], in_=L1B[m]).then_inc(sems["SD"], 16)
    C["SD"] += 128
    gd_l1 = C["SD"]
    @blk.scalar
    def _(e):
        e.wait_ge(sems["SD"], gd_l1)
        inc(e.activation(l1res[:], l1res[:], AF.Sign), "SE")
    se_sign = C["SE"]

    for c in range(NCM):
        emit_preln(c * TCM,
                   lambda f, c=c: xnbL[:, f * L + c * TCM: f * L + (c + 1) * TCM])
    ve_xnL = C["VE"]
    for c in range(NCM):
        t0 = c * TCM
        for m in range(NFT):
            pshalf = ps_mm[:, (m % 2) * TCM:((m % 2) + 1) * TCM]
            @blk.tensor
            def _(e, m=m, c=c, pshalf=pshalf, _cse=C["SE"]):
                if m == 0 and c == 0:
                    e.wait_ge(sems["SE"], se_sign)
                    e.wait_ge(sems["VE"], ve_xnL)
                e.wait_ge(sems["SE"], _cse)
                for k in range(8):
                    mm = e.matmul(pshalf, l1res[:, m * 1024 + k * 128: m * 1024 + (k + 1) * 128],
                                  xnbL[:, k * L + t0: k * L + t0 + TCM], start=(k == 0), stop=(k == 7))
                inc(mm, "TE")
            te_l1 = C["TE"]
            @blk.scalar
            def _(e, m=m, te_l1=te_l1, pshalf=pshalf):
                e.wait_ge(sems["TE"], te_l1)
                inc(e.activation(ofl[:, m * TCM:(m + 1) * TCM], pshalf, AF.Identity), "SE")
        of_t = [ofl[:, f * TCM:(f + 1) * TCM] for f in range(NFT)]
        emit_postln_resid(of_t, t0, onescf[:])
    ve_final = C["VE"]
    es_l.close()

    xio_sd = [0, 0]
    for i in range(16):
        bi = i % 2
        for f in range(NFT):
            q = (i * NFT + f) % 7
            @blk.tensor
            def _(e, i=i, f=f, q=q, _trv_se_q=trv_se[q]):
                if i == 0 and f == 0:
                    e.wait_ge(sems["VE"], ve_final)
                if _trv_se_q:
                    e.wait_ge(sems["SE"], _trv_se_q)
                inc(e.transpose(trv[q][:], xfm[:, f * L + i * 128:f * L + (i + 1) * 128], idf_sb[:]), "TE")
            te_n = C["TE"]
            @blk.scalar
            def _(e, i=i, f=f, q=q, bi=bi, te_n=te_n, _xio_sd_bi=xio_sd[bi]):
                e.wait_ge(sems["TE"], te_n)
                if f == 0 and _xio_sd_bi:
                    e.wait_ge(sems["SD"], _xio_sd_bi)
                inc(e.activation(xio_v[bi][:, f * 128:(f + 1) * 128], trv[q][:], AF.Identity), "SE")
            trv_se[q] = C["SE"]
        se_t = C["SE"]
        @blk.sync
        def _(e, i=i, bi=bi, se_t=se_t):
            e.wait_ge(sems["SE"], se_t)
            e.dma_start(out=OUTt[i], in_=xio_v[bi][:]).then_inc(sems["SD"], 16)
        C["SD"] += 16
        xio_sd[0] = xio_sd[1] = C["SD"]
    @blk.sync
    def _(e, _csd=C["SD"]):
        e.wait_ge(sems["SD"], _csd)

    _replay[0] = True
    for _key in ("sync", "gpsimd", "tensor", "vector", "scalar"):
        _fns = _PEND[_key]
        for _i0 in range(0, len(_fns), 16):
            _chunk = _fns[_i0:_i0 + 16]
            getattr(blk_real, _key)(lambda e, _c=_chunk: [f(e) for f in _c])
    _replay[0] = False

    es.close()
    return nc


_CACHE = {}


def _prep(inputs):
    f32 = np.float32

    def c(a):
        return np.ascontiguousarray(np.asarray(a), dtype=f32)

    def pmaj(a, n):
        s = np.asarray(a).shape[1]
        return c(np.asarray(a).reshape(n, 128, s).transpose(1, 0, 2).reshape(128, n * s))

    def vec_pm(a, n):
        return c(np.asarray(a).reshape(n, 128).T)

    ip = c(inputs["L0_attn_in_proj"])
    ipw = c(ip.reshape(8, 128, 32, 128).transpose(2, 1, 0, 3).reshape(32, 128, 1024))
    op = c(inputs["L0_attn_out_proj"])
    opw = c(op.reshape(16, 128, 8, 128).transpose(2, 1, 0, 3).reshape(8, 128, 2048))
    shared = {
        "ipw": ipw, "opw": opw,
        "xpw": pmaj(inputs["L0_attn_x_proj"], NDG),
        "dtw": c(inputs["L0_attn_dt_w"]),
        "convw": pmaj(inputs["L0_attn_conv_w"], NDG),
        "convb": vec_pm(inputs["L0_attn_conv_b"], NDG),
        "dtb": vec_pm(inputs["L0_attn_dt_b"], NDG),
        "alog": pmaj(inputs["L0_attn_A_log"], NDG),
        "dd": vec_pm(inputs["L0_attn_D"], NDG),
        "rmsw": vec_pm(inputs["L0_attn_rms_w"], NFT),
        "idf": np.eye(128, dtype=f32),
        "idb": np.eye(128, dtype=f32).astype(ml_dtypes.bfloat16),
        "indv": np.concatenate([np.eye(96, dtype=f32)[:, 64:80],
                                np.eye(96, dtype=f32)[:, 80:96]], axis=1).astype(ml_dtypes.bfloat16),
    }
    w1 = c(inputs["L0_ffn_w1"]).T
    shared["w1t"] = c(w1.reshape(8, 128, 32, 128).transpose(2, 1, 0, 3).reshape(32, 128, 1024))
    shared["b1"] = vec_pm(inputs["L0_ffn_b1"], 32)
    w2 = c(inputs["L0_ffn_w2"]).T
    shared["w2t"] = c(w2.reshape(32, 128, 8, 128).transpose(2, 1, 0, 3).reshape(8, 128, 4096))
    shared["b2"] = vec_pm(inputs["L0_ffn_b2"], NFT)
    wqkv = c(inputs["L1_attn_W_qkv"])
    wq_r = wqkv.reshape(NH, 3, DH, D).transpose(1, 0, 2, 3).reshape(3 * D, D)
    wqkT = c(wq_r[0:2 * D].T)
    shared["wqkt"] = c(wqkT.reshape(8, 128, 16, 128).transpose(2, 1, 0, 3).reshape(16, 128, 1024))
    shared["wvr"] = c(wq_r[2 * D:3 * D].T.reshape(8, 128, 1024))
    bqkv = c(inputs["L1_attn_b_qkv"]).reshape(NH, 3, DH).transpose(1, 0, 2).reshape(3, D)
    shared["bq"] = vec_pm(bqkv[0], NFT)
    shared["bk"] = vec_pm(bqkv[1], NFT)
    shared["bv"] = c(bqkv[2]).reshape(1, D)
    wout = c(inputs["L1_attn_W_out"]).T
    shared["woutt"] = c(wout.reshape(8, 128, 8, 128).transpose(2, 1, 0, 3).reshape(8, 128, 1024))
    shared["bout"] = vec_pm(inputs["L1_attn_b_out"], NFT)
    l1w = c(inputs["L1_ffn_weight"]).T
    shared["l1wt"] = c(l1w.reshape(8, 128, 8, 128).transpose(2, 1, 0, 3).reshape(8, 128, 1024))
    return shared


def kernel(**inputs):
    shared = _prep(inputs)
    x = np.ascontiguousarray(np.asarray(inputs["L0_attn_x"]), dtype=np.float32)
    if "nc" not in _CACHE:
        _CACHE["nc"] = build(debug=DEBUG)
    nc = _CACHE["nc"]
    in_maps = []
    for b in range(8):
        m = dict(shared)
        m["x"] = np.ascontiguousarray(x[b])
        in_maps.append(m)
    res = run_bass_kernel_spmd(nc, in_maps, core_ids=list(range(8)))
    out = np.stack([res.results[b]["out"] for b in range(8)], axis=0)
    return out


# version 10
# speedup vs baseline: 1.6483x; 1.1945x over previous
import contextlib
import numpy as np
import ml_dtypes
import concourse.bass as bass
import concourse.mybir as mybir
from concourse.bass_utils import run_bass_kernel_spmd
from concourse.alu_op_type import AluOpType

F32 = mybir.dt.float32
BF16 = mybir.dt.bfloat16
FP16 = mybir.dt.float16
FP8 = mybir.dt.float8e4
AF = mybir.ActivationFunctionType
MUL = AluOpType.mult
ADD = AluOpType.add
SUB = AluOpType.subtract

D = 1024
L = 2048
DIN = 2048
DS = 16
H = 4096
NH = 16
DH = 64
NFT = D // 128
NDG = DIN // 128
TCM = 512
NCM = L // TCM
EPS = 1e-5

DEBUG = False
STAB_A = True
STAB_B = True
STAB_RED = True
DEFER = "scalar"


def build(debug=False):
    nc = bass.Bass("TRN2", target_bir_lowering=False)

    def din(name, shape, dt=F32):
        return nc.dram_tensor(name, list(shape), dt, kind="ExternalInput").ap()

    X = din("x", [L, D])
    IPW = din("ipw", [32, 128, 1024])
    OPW = din("opw", [8, 128, 2048])
    XPW = din("xpw", [128, NDG * 96])
    DTW = din("dtw", [64, DIN])
    CONVW = din("convw", [128, NDG * 4])
    CONVB = din("convb", [128, NDG])
    DTB = din("dtb", [128, NDG])
    ALOG = din("alog", [128, NDG * DS])
    DDD = din("dd", [128, NDG])
    RMSW = din("rmsw", [128, NFT])
    INDV = din("indv", [96, 32], BF16)
    W1T = din("w1t", [32, 128, 1024])
    B1 = din("b1", [128, 32])
    W2T = din("w2t", [8, 128, 4096])
    B2 = din("b2", [128, NFT])
    WQKT = din("wqkt", [16, 128, 1024])
    WVR = din("wvr", [8, 128, 1024])
    BQ = din("bq", [128, NFT])
    BK = din("bk", [128, NFT])
    BV = din("bv", [1, D])
    WOUTT = din("woutt", [8, 128, 1024])
    BOUT = din("bout", [128, NFT])
    L1WT = din("l1wt", [8, 128, 1024])
    IDF = din("idf", [128, 128], F32)
    IDB = din("idb", [128, 128], BF16)
    OUT = nc.dram_tensor("out", [L, D], F32, kind="ExternalOutput").ap()
    W1B = nc.dram_tensor("w1b", [32, 128, 1024], BF16, kind="Internal").ap()
    IPWB = nc.dram_tensor("ipwb", [32, 128, 1024], BF16, kind="Internal").ap()
    OPWB = nc.dram_tensor("opwb", [8, 128, 2048], BF16, kind="Internal").ap()
    W2Q = nc.dram_tensor("w2q", [8, 128, 4096], FP8, kind="Internal").ap()
    WQKB = nc.dram_tensor("wqkb", [16, 128, 1024], BF16, kind="Internal").ap()
    WOUTB = nc.dram_tensor("woutb", [8, 128, 1024], BF16, kind="Internal").ap()
    L1B = nc.dram_tensor("l1b", [8, 128, 1024], BF16, kind="Internal").ap()

    dbg = {}
    if debug:
        def dout(name, shape, dt=F32):
            dbg[name] = nc.dram_tensor(name, list(shape), dt, kind="ExternalOutput").ap()
        dout("d_xnb", [128, NFT * TCM])
        dout("d_wq", [128, 8 * H])
        dout("d_hq", [128, 32 * TCM])
        dout("d_y2", [128, NDG * TCM])
        dout("d_x1", [128, NFT * L])
        dout("d_x2", [128, NFT * L])
        dout("d_x3", [128, NFT * L])
        dout("d_q", [128, 8 * L])
        dout("d_vt", [128, 16 * 1040])
        dout("d_ao", [128, 8 * L])
        dout("d2_xnb", [128, NFT * TCM])
        dout("d2_stat", [96, TCM])
        dout("d2_xib", [128, NDG * (TCM + 3)])
        dout("d2_dbc", [96, TCM])
        dout("d2_y2", [128, NDG * TCM])
        dout("d2_of", [128, NFT * TCM])
        dout("d_ofa", [128, NFT * TCM])
        dout("d_den", [96, 1024])

    es = contextlib.ExitStack()

    def sb(name, shape, dt):
        return es.enter_context(nc.sbuf_tensor(name, list(shape), dt))

    def psum(name, shape, dt=F32):
        return es.enter_context(nc.psum_tensor(name, list(shape), dt))

    xfm = sb("xfm", [128, NFT * L], F32)
    consts = sb("consts", [128, 12], F32)
    onesf = sb("onesf", [1, 128], F32)
    onesb = sb("onesb", [1, 128], BF16)
    onescf = sb("onescf", [128, 1], F32)
    onescb = sb("onescb", [128, 1], BF16)
    onesch = sb("onesch", [128, 1], FP16)
    onesA = sb("onesA", [96, 128], F32)
    idf_sb = sb("idf_sb", [128, 128], F32)
    idb_sb = sb("idb_sb", [128, 128], BF16)
    statrA = sb("statrA", [96, TCM], F32)
    statrB = sb("statrB", [96, TCM], F32)
    statrC = sb("statrC", [96, TCM], F32)
    ipw_m = sb("ipw_m", [128, 2 * 1024], BF16)
    sq_sb = sb("sq_sb", [128, TCM], F32)
    xio = sb("xio", [128, D], F32)

    ps_mm = psum("ps_mm", [128, 2 * TCM])
    ps_o2 = psum("ps_o2", [128, 2 * TCM])
    ps_y = psum("ps_y", [128, TCM])
    ps_bc = psum("ps_bc", [128, TCM])
    ps_dt = psum("ps_dt", [128, TCM])
    ps_sx = psum("ps_sx", [128, TCM])
    ps_s1 = ps_bc[0:1, :]
    ps_s2 = ps_y[0:1, :]
    ps_dbcv = ps_sx[0:96, :]

    es_m = contextlib.ExitStack()

    def sbm(name, shape, dt):
        return es_m.enter_context(nc.sbuf_tensor(name, list(shape), dt))

    A_sb = sbm("A_sb", [128, NDG * DS], F32)
    convw_sb = sbm("convw_sb", [128, NDG * 4], F32)
    convb_sb = sbm("convb_sb", [128, NDG], F32)
    dtb_sb = sbm("dtb_sb", [128, NDG], F32)
    D_sb = sbm("D_sb", [128, NDG], F32)
    rmsw_sb = sbm("rmsw_sb", [128, NFT], F32)
    xpw_sb = sbm("xpw_sb", [128, NDG * 96], BF16)
    dtw_sb = sbm("dtw_sb", [64, DIN], BF16)
    xnbc = sbm("xnbc", [128, NFT * TCM], BF16)
    opw_m = sbm("opw_m", [128, 2 * 2048], BF16)
    xib = sbm("xib", [128, NDG * (TCM + 3)], BF16)
    xtail = sbm("xtail", [128, NDG * 3], F32)
    zsb = sbm("zsb", [128, NDG * TCM], BF16)
    Brep = sbm("BrepT", [128, DS * TCM], BF16)
    Crep = sbm("CrepT", [128, DS * TCM], BF16)
    y2b = sbm("y2b", [128, NDG * TCM], BF16)
    dbc_sb = sbm("dbc_sb", [96, TCM], BF16)
    delta_sb = sbm("delta_sb", [128, TCM], F32)
    u_sb = sbm("u_sb", [128, TCM], BF16)
    a_sb = sbm("a_sb", [128, 2 * TCM], F32)
    b_sb = sbm("b_sb", [128, 2 * TCM], BF16)
    h_sb = sbm("h_sb", [128, 2 * TCM], BF16)
    hc_sb = sbm("hc_sb", [128, 2 * TCM], BF16)
    h0_sb = sbm("h0_sb", [128, NDG * DS], F32)
    rrep_sb = sbm("rrep_sb", [128, TCM], F32)
    mrep_sb = sbm("mrep_sb", [128, TCM], F32)
    indv_sb = sbm("indv_sb", [96, 32], BF16)
    of16 = xnbc

    blk_real = es.enter_context(nc.Block())
    sems = {}
    C = {}
    for s in ["SD", "GD", "TE", "VE", "SE"]:
        sems[s] = es.enter_context(nc.semaphore(s))
        C[s] = 0

    _replay = [False]

    def inc(instr, s, n=1):
        instr.then_inc(sems[s], n)
        if not _replay[0]:
            C[s] += n
        return C[s]

    class _Dummy:
        def then_inc(self, *a, **k):
            return self

    _DUM = _Dummy()

    class _Mock:
        def __getattr__(self, name):
            return lambda *a, **k: _DUM

    _MOCK = _Mock()
    _PEND = {k: [] for k in ("sync", "gpsimd", "tensor", "vector", "scalar")}

    class _EngDec:
        def __init__(self, key):
            self.key = key

        def __call__(self, fn):
            if DEFER is True or DEFER == self.key:
                _PEND[self.key].append(fn)
                fn(_MOCK)
            else:
                getattr(blk_real, self.key)(fn)

    class _BlkProxy:
        sync = _EngDec("sync")
        gpsimd = _EngDec("gpsimd")
        tensor = _EngDec("tensor")
        vector = _EngDec("vector")
        scalar = _EngDec("scalar")

    blk = _BlkProxy()

    @blk.sync
    def _(e):
        e.dma_start(out=A_sb[:], in_=ALOG[:]).then_inc(sems["SD"], 16)
        e.dma_start(out=convw_sb[:], in_=CONVW[:]).then_inc(sems["SD"], 16)
        e.dma_start(out=convb_sb[:], in_=CONVB[:]).then_inc(sems["SD"], 16)
        e.dma_start(out=dtb_sb[:], in_=DTB[:]).then_inc(sems["SD"], 16)
        e.dma_start(out=D_sb[:], in_=DDD[:]).then_inc(sems["SD"], 16)
        e.dma_start(out=rmsw_sb[:], in_=RMSW[:]).then_inc(sems["SD"], 16)
        e.dma_start(out=idf_sb[:], in_=IDF[:]).then_inc(sems["SD"], 16)
        e.dma_start(out=idb_sb[:], in_=IDB[:]).then_inc(sems["SD"], 16)
        e.dma_start(out=indv_sb[:], in_=INDV[:]).then_inc(sems["SD"], 16)
    C["SD"] += 16 * 9
    sd_init = C["SD"]

    @blk.gpsimd
    def _(e):
        e.dma_start(out=xpw_sb[:], in_=XPW[:]).then_inc(sems["GD"], 16)
        e.dma_start(out=dtw_sb[:], in_=DTW[:]).then_inc(sems["GD"], 16)
    C["GD"] += 32
    gd_init = C["GD"]

    @blk.vector
    def _(e):
        for i, v in enumerate([1.0, -1.0, 0.125, 1.0 / 1024.0, EPS, -0.5, 0.0, 2.0,
                               -44.0, 2.0, 88.0, 0.5]):
            e.memset(consts[:, i:i + 1], v)
        e.memset(onesf[:], 1.0)
        e.memset(onesb[:], 1.0)
        e.memset(onescf[:], 1.0)
        e.memset(onescb[:], 1.0)
        e.memset(onesch[:], 1.0)
        e.memset(onesA[:], 1.0)
        e.memset(statrA[:], 1.0)
        e.memset(statrB[:], 1.0)
        e.memset(statrC[:], 1.0)
        e.memset(h0_sb[:], 0.0)
        e.memset(xtail[:], 0.0)
        inc(e.nop(), "VE")
    (C_ONE, C_NEG1, C_QS, C_I1024, C_EPS, C_NHALF, C_ZERO, C_TWO,
     C_M16, C_EIGHT, C_128, C_SPARE) = [consts[:, i:i + 1] for i in range(12)]

    @blk.scalar
    def _(e):
        e.wait_ge(sems["SD"], sd_init)
        inc(e.activation(A_sb[:], A_sb[:], AF.Exp), "SE")
    se_A = C["SE"]
    @blk.vector
    def _(e):
        e.wait_ge(sems["SE"], se_A)
        inc(e.tensor_scalar(A_sb[:], A_sb[:], C_NEG1, None, op0=MUL), "VE")
    ve_A = C["VE"]

    trv = [ps_mm[:, 0:128], ps_mm[:, TCM:TCM + 128],
           ps_o2[:, 0:128], ps_o2[:, TCM:TCM + 128],
           ps_y[:, 0:128], ps_bc[:, 0:128], ps_dt[:, 0:128]]
    trv_se = [0] * 7

    xio_v = [xio[:, 0:D], xio[:, 0:D]]
    Xt = X.rearrange("(n p) m -> n p m", p=128)
    OUTt = OUT.rearrange("(n p) m -> n p m", p=128)

    xio_te = [0, 0]
    for i in range(16):
        bi = i % 2
        @blk.sync
        def _(e, i=i, bi=bi, _xio_te_bi=xio_te[bi]):
            if _xio_te_bi:
                e.wait_ge(sems["TE"], _xio_te_bi)
            e.dma_start(out=xio_v[bi][:], in_=Xt[i]).then_inc(sems["SD"], 16)
        C["SD"] += 16
        sd_t = C["SD"]
        for f in range(NFT):
            q = (i * NFT + f) % 7
            @blk.tensor
            def _(e, i=i, f=f, q=q, bi=bi, sd_t=sd_t, _trv_se_q=trv_se[q]):
                if f == 0:
                    e.wait_ge(sems["SD"], sd_t)
                    if i == 0:
                        e.wait_ge(sems["SD"], sd_init)
                        e.wait_ge(sems["VE"], ve_A)
                if _trv_se_q:
                    e.wait_ge(sems["SE"], _trv_se_q)
                inc(e.transpose(trv[q][:], xio_v[bi][:, f * 128:(f + 1) * 128], idf_sb[:]), "TE")
            te_n = C["TE"]
            @blk.scalar
            def _(e, i=i, f=f, q=q, te_n=te_n):
                e.wait_ge(sems["TE"], te_n)
                inc(e.activation(xfm[:, f * L + i * 128:f * L + (i + 1) * 128], trv[q][:], AF.Identity), "SE")
            trv_se[q] = C["SE"]
        xio_te[0] = xio_te[1] = C["TE"]
    se_xtr = C["SE"]

    A_R, A_T0, A_BS = 0, 32, 64
    B_RS, B_T1, B_CS = 0, 32, 64
    C_M, C_V, C_RP = 0, 32, 64

    def rA(r, w):
        return statrA[r:r + 1, :w]

    def rB(r, w):
        return statrB[r:r + 1, :w]

    def rC(r, w):
        return statrC[r:r + 1, :w]

    def emit_ln_stats(src_tiles, width, ones_src, extra_rs=False):
        for f in range(NFT):
            @blk.scalar
            def _(e, f=f, _cve=C["VE"], _cte=C["TE"]):
                e.wait_ge(sems["VE"], _cve)
                if f > 0:
                    e.wait_ge(sems["TE"], _cte)
                inc(e.activation(sq_sb[:, :width], src_tiles[f], AF.Square), "SE")
            se_f = C["SE"]
            @blk.tensor
            def _(e, f=f, se_f=se_f):
                e.wait_ge(sems["SE"], se_f)
                e.matmul(ps_s1[0:1, :width], ones_src, src_tiles[f], start=(f == 0), stop=(f == NFT - 1))
                inc(e.matmul(ps_s2[0:1, :width], onescf[:], sq_sb[:, :width],
                             start=(f == 0), stop=(f == NFT - 1)), "TE")
        te_stats = C["TE"]
        @blk.vector
        def _(e):
            e.wait_ge(sems["TE"], te_stats)
            e.tensor_scalar(rC(C_M, width), ps_s1[0:1, :width], C_I1024[0:1, :], None, op0=MUL)
            e.tensor_scalar(rA(A_T0, width), ps_s2[0:1, :width], C_I1024[0:1, :], None, op0=MUL)
            e.tensor_tensor(rB(B_T1, width), rC(C_M, width), rC(C_M, width), op=MUL)
            inc(e.tensor_tensor(rC(C_V, width), rA(A_T0, width), rB(B_T1, width), op=SUB), "VE")
        ve_v = C["VE"]
        @blk.scalar
        def _(e):
            e.wait_ge(sems["VE"], ve_v)
            e.activation(rA(A_T0, width), rC(C_V, width), AF.Ln, bias=C_EPS[0:1, :], scale=1.0)
            i1 = e.activation(rA(A_R, width), rA(A_T0, width), AF.Exp, scale=C_NHALF[0:1, :])
            if extra_rs:
                i1 = e.activation(rB(B_T1, width), rC(C_V, width), AF.Ln)
            inc(i1, "SE")
        se_r = C["SE"]
        if extra_rs:
            @blk.vector
            def _(e):
                e.wait_ge(sems["SE"], se_r)
                inc(e.tensor_tensor(rC(C_V, width), rB(B_T1, width), rA(A_T0, width), op=SUB), "VE")
            ve_q = C["VE"]
            @blk.scalar
            def _(e):
                e.wait_ge(sems["VE"], ve_q)
                e.activation(rB(B_T1, width), rC(C_V, width), AF.Exp)
                e.activation(rB(B_T1, width), rB(B_T1, width), AF.Ln, bias=C_EPS[0:1, :], scale=1.0)
                inc(e.activation(rB(B_RS, width), rB(B_T1, width), AF.Exp, scale=C_NHALF[0:1, :]), "SE")
        @blk.vector
        def _(e, _cse=C["SE"]):
            e.wait_ge(sems["SE"], _cse)
            inc(e.nop(), "VE")
        return C["VE"]

    def emit_bcast(row_ap, width, dst, ve_ready, base=0):
        @blk.tensor
        def _(e, ve_ready=ve_ready):
            e.wait_ge(sems["VE"], ve_ready)
            for n in range((width + 511) // 512):
                sl = slice(n * 512, min(width, (n + 1) * 512))
                mm = e.matmul(dst[:, sl], onesA[base:base + 1, :], row_ap[:, sl], start=True, stop=True)
            inc(mm, "TE")
        return C["TE"]

    def emit_preln(t0, dst_fn):
        src_t = [xfm[:, f * L + t0: f * L + t0 + TCM] for f in range(NFT)]
        ve_mr = emit_ln_stats(src_t, TCM, onescf[:], extra_rs=False)
        emit_bcast(rC(C_M, TCM), TCM, ps_dt, ve_mr, base=C_M)
        te_b = emit_bcast(rA(A_R, TCM), TCM, ps_bc, ve_mr, base=A_R)
        for f in range(NFT):
            @blk.vector
            def _(e, f=f, te_b=te_b, _cse=C["SE"]):
                if f == 0:
                    e.wait_ge(sems["TE"], te_b)
                    e.wait_ge(sems["SE"], _cse)
                e.tensor_tensor(sq_sb[:, :TCM], src_t[f], ps_dt[:, :TCM], op=SUB)
                i = e.tensor_tensor(dst_fn(f), sq_sb[:, :TCM], ps_bc[:, :TCM], op=MUL)
                if f == NFT - 1:
                    inc(i, "VE")
        return C["VE"]

    def emit_postln_resid(of_tiles, t0, ones_src):
        emit_ln_stats(of_tiles, TCM, ones_src, extra_rs=False)
        ve_mr2 = C["VE"]
        emit_bcast(rA(A_R, TCM), TCM, ps_bc, ve_mr2, base=A_R)
        te_b2 = emit_bcast(rC(C_M, TCM), TCM, ps_dt, ve_mr2, base=C_M)
        for f in range(NFT):
            @blk.vector
            def _(e, f=f, te_b2=te_b2, _cse=C["SE"]):
                if f == 0:
                    e.wait_ge(sems["TE"], te_b2)
                    e.wait_ge(sems["SE"], _cse)
                e.tensor_tensor(sq_sb[:, :TCM], of_tiles[f], ps_dt[:, :TCM], op=SUB)
                e.tensor_tensor(sq_sb[:, :TCM], sq_sb[:, :TCM], ps_bc[:, :TCM], op=MUL)
                i = e.tensor_tensor(xfm[:, f * L + t0: f * L + t0 + TCM],
                                    xfm[:, f * L + t0: f * L + t0 + TCM], sq_sb[:, :TCM], op=ADD)
                if f == NFT - 1:
                    inc(i, "VE")
        return C["VE"]

    scan_done = [0, 0]
    y_done = [0, 0]
    mm_prev = [0, 0]
    op_prev = [0, 0]
    for c in range(NCM):
        t0 = c * TCM
        src = [xfm[:, f * L + t0: f * L + t0 + TCM] for f in range(NFT)]

        emit_ln_stats(src, TCM, onescf[:], extra_rs=True)
        @blk.vector
        def _(e, _cse=C["SE"]):
            e.wait_ge(sems["SE"], _cse)
            inc(e.tensor_tensor(rC(C_RP, TCM), rA(A_R, TCM), rB(B_RS, TCM), op=MUL), "VE")
        ve_rp = C["VE"]
        emit_bcast(rC(C_RP, TCM), TCM, ps_bc, ve_rp, base=C_RP)
        emit_bcast(rC(C_M, TCM), TCM, ps_dt, ve_rp, base=C_M)
        te_bc = emit_bcast(rA(A_R, TCM), TCM, ps_y, ve_rp, base=A_R)
        @blk.scalar
        def _(e, te_bc=te_bc):
            e.wait_ge(sems["TE"], te_bc)
            e.activation(rrep_sb[:], ps_y[:, :TCM], AF.Identity)
            inc(e.activation(mrep_sb[:], ps_dt[:, :TCM], AF.Identity), "SE")
        se_rm = C["SE"]
        for f in range(NFT):
            @blk.vector
            def _(e, f=f, te_bc=te_bc, _cse=C["SE"]):
                if f == 0:
                    e.wait_ge(sems["TE"], te_bc)
                    e.wait_ge(sems["SE"], _cse)
                e.tensor_tensor(sq_sb[:, :TCM], src[f], ps_dt[:, :TCM], op=SUB)
                e.tensor_tensor(sq_sb[:, :TCM], sq_sb[:, :TCM], ps_bc[:, :TCM], op=MUL)
                i = e.tensor_scalar(xnbc[:, f * TCM:(f + 1) * TCM], sq_sb[:, :TCM],
                                    rmsw_sb[:, f:f + 1], None, op0=MUL)
                if f == NFT - 1:
                    inc(i, "VE")
        ve_xnb = C["VE"]
        if debug and c == 0:
            @blk.gpsimd
            def _(e):
                e.wait_ge(sems["VE"], ve_xnb)
                e.dma_start(out=dbg["d_xnb"][:], in_=xnbc[:]).then_inc(sems["GD"], 16)
            C["GD"] += 16
        if debug and c == 1:
            @blk.gpsimd
            def _(e):
                e.wait_ge(sems["VE"], ve_xnb)
                e.dma_start(out=dbg["d2_xnb"][:], in_=xnbc[:]).then_inc(sems["GD"], 16)
                e.dma_start(out=dbg["d2_stat"][:], in_=statrC[:]).then_inc(sems["GD"], 16)
            C["GD"] += 32

        ipw_v = [ipw_m[:, 0:1024], ipw_m[:, 1024:2048]]
        mm_done = [0, 0]
        evac_ve = [0, 0]
        evac_se = [0, 0]
        se_silu_prev = 0
        for m in range(32):
            buf = m % 2
            if c == 0:
                @blk.gpsimd
                def _(e, m=m, buf=buf, _mm_done_buf=mm_done[buf]):
                    if m < 2:
                        e.wait_ge(sems["GD"], gd_init)
                    if _mm_done_buf:
                        e.wait_ge(sems["TE"], _mm_done_buf)
                    e.dma_start(out=ipw_v[buf][:], in_=IPW[m]).then_inc(sems["GD"], 16)
                C["GD"] += 16
                gd_t = C["GD"]
                ipq = "GD"
            else:
                @blk.sync
                def _(e, m=m, buf=buf, c=c, _w=(mm_done[buf] or mm_prev[buf])):
                    if c == 1 and m < 2:
                        e.wait_ge(sems["GD"], gd_ipwb)
                    if _w:
                        e.wait_ge(sems["TE"], _w)
                    e.dma_start(out=ipw_v[buf][:], in_=IPWB[m]).then_inc(sems["SD"], 16)
                C["SD"] += 16
                gd_t = C["SD"]
                ipq = "SD"
            pshalf = ps_mm[:, buf * TCM:(buf + 1) * TCM]
            @blk.tensor
            def _(e, m=m, buf=buf, gd_t=gd_t, ipq=ipq, pshalf=pshalf, ve_xnb=ve_xnb, _evac_se_buf=evac_se[buf], _evac_ve_buf=evac_ve[buf]):
                e.wait_ge(sems[ipq], gd_t)
                if m == 0:
                    e.wait_ge(sems["VE"], ve_xnb)
                if _evac_se_buf:
                    e.wait_ge(sems["SE"], _evac_se_buf)
                if _evac_ve_buf:
                    e.wait_ge(sems["VE"], _evac_ve_buf)
                for k in range(8):
                    mm = e.matmul(pshalf, ipw_v[buf][:, k * 128:(k + 1) * 128],
                                  xnbc[:, k * TCM:(k + 1) * TCM], start=(k == 0), stop=(k == 7))
                inc(mm, "TE")
            mm_done[buf] = C["TE"]
            if m < 16:
                dg = m
                @blk.vector
                def _(e, dg=dg, buf=buf, pshalf=pshalf, ssp=se_silu_prev, _mm_done_buf=mm_done[buf]):
                    e.wait_ge(sems["TE"], _mm_done_buf)
                    if ssp:
                        e.wait_ge(sems["SE"], ssp)
                    w3 = convw_sb[:, dg * 4 + 3: dg * 4 + 4]
                    e.tensor_scalar(sq_sb[:, :TCM], pshalf, w3, None, op0=MUL)
                    for kk in range(3):
                        sh = 3 - kk
                        wk = convw_sb[:, dg * 4 + kk: dg * 4 + kk + 1]
                        e.scalar_tensor_tensor(sq_sb[:, 0:sh], xtail[:, dg * 3 + kk: dg * 3 + 3], wk,
                                               sq_sb[:, 0:sh], op0=MUL, op1=ADD)
                        e.scalar_tensor_tensor(sq_sb[:, sh:TCM], pshalf[:, 0:TCM - sh], wk,
                                               sq_sb[:, sh:TCM], op0=MUL, op1=ADD)
                    inc(e.tensor_copy(xtail[:, dg * 3: dg * 3 + 3], pshalf[:, TCM - 3:TCM]), "VE")
                evac_ve[buf] = C["VE"]
                @blk.scalar
                def _(e, dg=dg, buf=buf, _evac_ve_buf=evac_ve[buf]):
                    e.wait_ge(sems["VE"], _evac_ve_buf)
                    inc(e.activation(xib[:, dg * (TCM + 3) + 3: (dg + 1) * (TCM + 3)], sq_sb[:, :TCM],
                                     AF.Silu, bias=convb_sb[:, dg:dg + 1], scale=1.0), "SE")
                evac_se[buf] = C["SE"]
                se_silu_prev = C["SE"]
            else:
                dg = m - 16
                @blk.scalar
                def _(e, dg=dg, buf=buf, pshalf=pshalf, _mm_done_buf=mm_done[buf]):
                    e.wait_ge(sems["TE"], _mm_done_buf)
                    inc(e.activation(zsb[:, dg * TCM:(dg + 1) * TCM], pshalf, AF.Silu), "SE")
                evac_se[buf] = C["SE"]
                evac_ve[buf] = 0
        se_inproj = C["SE"]

        @blk.tensor
        def _(e):
            e.wait_ge(sems["SE"], se_inproj)
            for dg in range(NDG):
                mm = e.matmul(ps_dbcv[:, :], xpw_sb[:, dg * 96:(dg + 1) * 96],
                              xib[:, dg * (TCM + 3) + 3:(dg + 1) * (TCM + 3)],
                              start=(dg == 0), stop=(dg == NDG - 1))
            inc(mm, "TE")
        te_dbc = C["TE"]
        @blk.scalar
        def _(e, te_dbc=te_dbc):
            e.wait_ge(sems["TE"], te_dbc)
            inc(e.activation(dbc_sb[:], ps_dbcv[:, :], AF.Identity), "SE")
        se_dbc = C["SE"]

        se_stage = 0
        for s in range(DS):
            @blk.tensor
            def _(e, s=s, se_stage=se_stage, _cve=C["VE"]):
                if s == 0:
                    e.wait_ge(sems["SE"], se_dbc)
                    e.wait_ge(sems["VE"], _cve)
                else:
                    e.wait_ge(sems["SE"], se_stage)
                e.matmul(ps_dt[0:1, :], indv_sb[:, s:s + 1], dbc_sb[:, :], start=True, stop=True)
                inc(e.matmul(ps_dt[32:33, :], indv_sb[:, 16 + s:17 + s], dbc_sb[:, :], start=True, stop=True), "TE")
            te_ex = C["TE"]
            @blk.scalar
            def _(e, s=s, te_ex=te_ex):
                e.wait_ge(sems["TE"], te_ex)
                e.activation(rA(A_BS, TCM), ps_dt[0:1, :], AF.Identity)
                inc(e.activation(rB(B_CS, TCM), ps_dt[32:33, :], AF.Identity), "SE")
            se_stage = C["SE"]
            @blk.tensor
            def _(e, s=s, se_stage=se_stage):
                e.wait_ge(sems["SE"], se_stage)
                e.matmul(ps_bc[:, :], onesA[A_BS:A_BS + 1, :], rA(A_BS, TCM), start=True, stop=True)
                inc(e.matmul(ps_y[:, :], onesA[B_CS:B_CS + 1, :], rB(B_CS, TCM), start=True, stop=True), "TE")
            te_b = C["TE"]
            @blk.scalar
            def _(e, s=s, te_b=te_b):
                e.wait_ge(sems["TE"], te_b)
                e.activation(Brep[:, s * TCM:(s + 1) * TCM], ps_bc[:, :], AF.Identity)
                inc(e.activation(Crep[:, s * TCM:(s + 1) * TCM], ps_y[:, :], AF.Identity), "SE")
        se_bcrep = C["SE"]

        if c == 0:
            @blk.gpsimd
            def _(e):
                for m in range(32):
                    e.dma_start(out=IPWB[m], in_=IPW[m]).then_inc(sems["GD"], 16)
                for m in range(8):
                    e.dma_start(out=OPWB[m], in_=OPW[m]).then_inc(sems["GD"], 16)
            C["GD"] += 640
        elif c == 1:
            @blk.gpsimd
            def _(e):
                for m in range(8):
                    e.dma_start(out=W2Q[m], in_=W2T[m]).then_inc(sems["GD"], 16)
                for m in range(16):
                    e.dma_start(out=W1B[m], in_=W1T[m]).then_inc(sems["GD"], 16)
            C["GD"] += 384
        elif c == 2:
            @blk.gpsimd
            def _(e):
                for m in range(16):
                    e.dma_start(out=W1B[16 + m], in_=W1T[16 + m]).then_inc(sems["GD"], 16)
            C["GD"] += 256
        elif c == 3:
            @blk.gpsimd
            def _(e):
                for m in range(16):
                    e.dma_start(out=WQKB[m], in_=WQKT[m]).then_inc(sems["GD"], 16)
            C["GD"] += 256
        if c == 0:
            gd_ipwb = C["GD"]
        elif c == 1:
            gd_w2q = C["GD"]
        elif c == 2:
            gd_w1b = C["GD"]
        elif c == 3:
            gd_wqkb = C["GD"]

        ve_u_prev = 0
        for dg in range(NDG):
            @blk.tensor
            def _(e, dg=dg, _cse=C["SE"]):
                if dg == 0:
                    e.wait_ge(sems["SE"], se_bcrep)
                else:
                    e.wait_ge(sems["SE"], _cse)
                inc(e.matmul(ps_dt[:, :], dtw_sb[:, dg * 128:(dg + 1) * 128], dbc_sb[0:64, :],
                             start=True, stop=True), "TE")
            te_dt = C["TE"]
            @blk.scalar
            def _(e, dg=dg, te_dt=te_dt, vup=ve_u_prev):
                e.wait_ge(sems["TE"], te_dt)
                if vup:
                    e.wait_ge(sems["VE"], vup)
                e.activation(delta_sb[:], ps_dt[:, :], AF.Exp, bias=dtb_sb[:, dg:dg + 1], scale=1.0)
                inc(e.activation(delta_sb[:], delta_sb[:], AF.Ln, bias=C_ONE, scale=1.0), "SE")
            se_delta = C["SE"]
            @blk.vector
            def _(e, dg=dg, se_delta=se_delta):
                e.wait_ge(sems["SE"], se_delta)
                inc(e.tensor_tensor(u_sb[:], delta_sb[:],
                                    xib[:, dg * (TCM + 3) + 3:(dg + 1) * (TCM + 3)], op=MUL), "VE")
            ve_u = C["VE"]
            ve_u_prev = ve_u

            for s in range(DS):
                sl = s % 2
                asl = a_sb[:, sl * TCM:(sl + 1) * TCM]
                bsl = b_sb[:, sl * TCM:(sl + 1) * TCM]
                hsl = h_sb[:, sl * TCM:(sl + 1) * TCM]
                hcsl = hc_sb[:, sl * TCM:(sl + 1) * TCM]
                @blk.scalar
                def _(e, dg=dg, s=s, sl=sl, asl=asl, sd=scan_done[sl]):
                    if sd:
                        e.wait_ge(sems["VE"], sd)
                    inc(e.activation(asl, delta_sb[:], AF.Exp,
                                     scale=A_sb[:, dg * DS + s: dg * DS + s + 1]), "SE")
                se_a = C["SE"]
                @blk.vector
                def _(e, dg=dg, s=s, sl=sl, se_a=se_a, asl=asl, bsl=bsl, hsl=hsl, hcsl=hcsl,
                      ve_u=ve_u, yd=y_done[sl]):
                    if s == 0:
                        e.wait_ge(sems["VE"], ve_u)
                    if yd:
                        e.wait_ge(sems["TE"], yd)
                    e.tensor_tensor(bsl, u_sb[:], Brep[:, s * TCM:(s + 1) * TCM], op=MUL)
                    e.wait_ge(sems["SE"], se_a)
                    e.tensor_tensor_scan(hsl, asl, bsl, h0_sb[:, dg * DS + s: dg * DS + s + 1],
                                         op0=MUL, op1=ADD)
                    e.tensor_copy(h0_sb[:, dg * DS + s: dg * DS + s + 1], hsl[:, TCM - 1:TCM])
                    inc(e.tensor_tensor(hcsl, hsl, Crep[:, s * TCM:(s + 1) * TCM], op=MUL), "VE")
                scan_done[sl] = C["VE"]
                @blk.tensor
                def _(e, s=s, sl=sl, hcsl=hcsl, vs=scan_done[sl]):
                    e.wait_ge(sems["VE"], vs)
                    inc(e.matmul(ps_y[:, :], idb_sb[:], hcsl, start=(s == 0), stop=(s == DS - 1)), "TE")
                y_done[sl] = C["TE"]
            te_y = C["TE"]
            @blk.vector
            def _(e, dg=dg, te_y=te_y):
                e.wait_ge(sems["TE"], te_y)
                e.scalar_tensor_tensor(sq_sb[:, :TCM], xib[:, dg * (TCM + 3) + 3:(dg + 1) * (TCM + 3)],
                                       D_sb[:, dg:dg + 1], ps_y[:, :], op0=MUL, op1=ADD)
                inc(e.tensor_tensor(y2b[:, dg * TCM:(dg + 1) * TCM], sq_sb[:, :TCM],
                                    zsb[:, dg * TCM:(dg + 1) * TCM], op=MUL), "VE")
        ve_y2 = C["VE"]
        if debug and c == 0:
            @blk.gpsimd
            def _(e):
                e.wait_ge(sems["VE"], ve_y2)
                e.dma_start(out=dbg["d_y2"][:], in_=y2b[:]).then_inc(sems["GD"], 16)
            C["GD"] += 16
        if debug and c == 1:
            @blk.gpsimd
            def _(e):
                e.wait_ge(sems["VE"], ve_y2)
                e.dma_start(out=dbg["d2_y2"][:], in_=y2b[:]).then_inc(sems["GD"], 16)
                e.dma_start(out=dbg["d2_xib"][:], in_=xib[:]).then_inc(sems["GD"], 16)
                e.dma_start(out=dbg["d2_dbc"][:], in_=dbc_sb[:]).then_inc(sems["GD"], 16)
            C["GD"] += 48

        opw_v = [opw_m[:, 0:2048], opw_m[:, 2048:4096]]
        opmm_done = [0, 0]
        of_done = [0, 0]
        ps_op = [ps_mm[:, 0:TCM], ps_mm[:, TCM:2 * TCM]]
        for m in range(NFT):
            buf = m % 2
            if c == 0:
                @blk.gpsimd
                def _(e, m=m, buf=buf, _opmm_done_buf=opmm_done[buf]):
                    if _opmm_done_buf:
                        e.wait_ge(sems["TE"], _opmm_done_buf)
                    e.dma_start(out=opw_v[buf][:], in_=OPW[m]).then_inc(sems["GD"], 16)
                C["GD"] += 16
                gd_t = C["GD"]
                opq = "GD"
            else:
                @blk.sync
                def _(e, m=m, buf=buf, c=c, _w=(opmm_done[buf] or op_prev[buf])):
                    if c == 1 and m < 2:
                        e.wait_ge(sems["GD"], gd_ipwb)
                    if _w:
                        e.wait_ge(sems["TE"], _w)
                    e.dma_start(out=opw_v[buf][:], in_=OPWB[m]).then_inc(sems["SD"], 16)
                C["SD"] += 16
                gd_t = C["SD"]
                opq = "SD"
            @blk.tensor
            def _(e, m=m, buf=buf, gd_t=gd_t, opq=opq, ve_y2=ve_y2, _of_done_buf=of_done[buf]):
                e.wait_ge(sems[opq], gd_t)
                if m == 0:
                    e.wait_ge(sems["VE"], ve_y2)
                if _of_done_buf:
                    e.wait_ge(sems["VE"], _of_done_buf)
                for k in range(16):
                    mm = e.matmul(ps_op[buf], opw_v[buf][:, k * 128:(k + 1) * 128],
                                  y2b[:, k * TCM:(k + 1) * TCM], start=(k == 0), stop=(k == 15))
                inc(mm, "TE")
            opmm_done[buf] = C["TE"]
            @blk.vector
            def _(e, m=m, buf=buf, c=c, se_rm=se_rm, _opmm_done_buf=opmm_done[buf]):
                e.wait_ge(sems["TE"], _opmm_done_buf)
                if m == 0:
                    e.wait_ge(sems["SE"], se_rm)
                e.tensor_tensor(sq_sb[:, :TCM], xfm[:, m * L + t0: m * L + t0 + TCM], mrep_sb[:], op=SUB)
                e.tensor_tensor(sq_sb[:, :TCM], sq_sb[:, :TCM], rrep_sb[:], op=MUL)
                inc(e.tensor_tensor(of16[:, m * TCM:(m + 1) * TCM], sq_sb[:, :TCM], ps_op[buf], op=ADD), "VE")
            of_done[buf] = C["VE"]

        if debug and c == 1:
            @blk.gpsimd
            def _(e, _cve=C["VE"]):
                e.wait_ge(sems["VE"], _cve)
                e.dma_start(out=dbg["d2_of"][:], in_=of16[:]).then_inc(sems["GD"], 16)
            C["GD"] += 16
        of_t = [of16[:, f * TCM:(f + 1) * TCM] for f in range(NFT)]
        emit_postln_resid(of_t, t0, onescb[:])
        mm_prev = list(mm_done)
        op_prev = list(opmm_done)
    ve_mamba = C["VE"]
    if debug:
        @blk.gpsimd
        def _(e):
            e.wait_ge(sems["VE"], ve_mamba)
            e.dma_start(out=dbg["d_x1"][:], in_=xfm[:]).then_inc(sems["GD"], 16)
        C["GD"] += 16

    es_m.close()

    es_f = contextlib.ExitStack()

    def sbf(name, shape, dt):
        return es_f.enter_context(nc.sbuf_tensor(name, list(shape), dt))

    xnb2 = sbf("xnb2", [128, NFT * TCM], BF16)
    offq = sbf("offq", [128, NFT * TCM], FP16)
    hq_sb = sbf("hq_sb", [128, 32 * TCM], FP8)
    wq_all = sbf("wq_all", [128, 8 * H], FP8)
    b1_sb = sbf("b1_sb", [128, 32], F32)
    b2q_sb = sbf("b2q_sb", [128, NFT], F32)
    b2h_sb = sbf("b2h_sb", [128, NFT], FP16)

    @blk.sync
    def _(e):
        e.wait_ge(sems["VE"], ve_mamba)
        e.wait_ge(sems["GD"], gd_w2q)
        e.dma_start(out=b1_sb[:], in_=B1[:]).then_inc(sems["SD"], 16)
        e.dma_start(out=b2q_sb[:], in_=B2[:]).then_inc(sems["SD"], 16)
        for m in range(8):
            e.dma_start(out=wq_all[:, m * H:(m + 1) * H], in_=W2Q[m]).then_inc(sems["SD"], 16)
    C["SD"] += 160
    sd_f = C["SD"]
    @blk.gpsimd
    def _(e):
        for m in range(8):
            e.dma_start(out=WOUTB[m], in_=WOUTT[m]).then_inc(sems["GD"], 16)
        for m in range(8):
            e.dma_start(out=L1B[m], in_=L1WT[m]).then_inc(sems["GD"], 16)
    C["GD"] += 256
    gd_woutb = C["GD"]
    @blk.scalar
    def _(e):
        e.wait_ge(sems["SD"], sd_f)
        e.activation(b2h_sb[:], b2q_sb[:], AF.Identity)
        inc(e.activation(b2q_sb[:], b2h_sb[:], AF.Identity), "SE")
    se_b2q = C["SE"]

    mmf_prev = [0, 0]
    for c in range(NCM):
        t0 = c * TCM
        ve_xn2 = emit_preln(t0, lambda f: xnb2[:, f * TCM:(f + 1) * TCM])
        mmf_done = [0, 0]
        epi_se = [0, 0]
        for m in range(32):
            buf = m % 2
            @blk.sync
            def _(e, m=m, buf=buf, c=c, _mmf_done_buf=(mmf_done[buf] or mmf_prev[buf])):
                if c == 0 and m < 2:
                    e.wait_ge(sems["VE"], ve_mamba)
                    e.wait_ge(sems["GD"], gd_w1b)
                if _mmf_done_buf:
                    e.wait_ge(sems["TE"], _mmf_done_buf)
                e.dma_start(out=ipw_m[:, buf * 1024:(buf + 1) * 1024], in_=W1B[m]).then_inc(sems["SD"], 16)
            C["SD"] += 16
            gd_t = C["SD"]
            pshalf = ps_mm[:, buf * TCM:(buf + 1) * TCM]
            @blk.tensor
            def _(e, m=m, buf=buf, gd_t=gd_t, pshalf=pshalf, ve_xn2=ve_xn2, _epi_se_buf=epi_se[buf]):
                e.wait_ge(sems["SD"], gd_t)
                if m == 0:
                    e.wait_ge(sems["VE"], ve_xn2)
                if _epi_se_buf:
                    e.wait_ge(sems["SE"], _epi_se_buf)
                for k in range(8):
                    mm = e.matmul(pshalf, ipw_m[:, buf * 1024 + k * 128: buf * 1024 + (k + 1) * 128],
                                  xnb2[:, k * TCM:(k + 1) * TCM], start=(k == 0), stop=(k == 7))
                inc(mm, "TE")
            mmf_done[buf] = C["TE"]
            @blk.scalar
            def _(e, m=m, buf=buf, pshalf=pshalf, _mmf_done_buf=mmf_done[buf]):
                e.wait_ge(sems["TE"], _mmf_done_buf)
                if m == 0:
                    e.wait_ge(sems["SE"], se_b2q)
                inc(e.activation(hq_sb[:, m * TCM:(m + 1) * TCM], pshalf, AF.Relu,
                                 bias=b1_sb[:, m:m + 1], scale=1.0), "SE")
            epi_se[buf] = C["SE"]
        mm2_done = [0, 0]
        epi2_se = [0, 0]
        for m in range(NFT):
            buf = m % 2
            psh2 = ps_o2[:, buf * TCM:(buf + 1) * TCM]
            @blk.tensor
            def _(e, m=m, buf=buf, psh2=psh2, _epi_se_0=epi_se[0], _epi_se_1=epi_se[1], _epi2_se_buf=epi2_se[buf], _sd_f=sd_f):
                if m == 0:
                    e.wait_ge(sems["SD"], _sd_f)
                    e.wait_ge(sems["SE"], _epi_se_0)
                    e.wait_ge(sems["SE"], _epi_se_1)
                if _epi2_se_buf:
                    e.wait_ge(sems["SE"], _epi2_se_buf)
                for k in range(32):
                    mm = e.matmul(psh2, wq_all[:, m * H + k * 128: m * H + (k + 1) * 128],
                                  hq_sb[:, k * TCM:(k + 1) * TCM], start=(k == 0), stop=(k == 31))
                inc(mm, "TE")
            mm2_done[buf] = C["TE"]
            @blk.scalar
            def _(e, m=m, buf=buf, psh2=psh2, _mm2_done_buf=mm2_done[buf]):
                e.wait_ge(sems["TE"], _mm2_done_buf)
                inc(e.activation(offq[:, m * TCM:(m + 1) * TCM], psh2, AF.Identity,
                                 bias=b2q_sb[:, m:m + 1], scale=1.0), "SE")
            epi2_se[buf] = C["SE"]
        of_t = [offq[:, f * TCM:(f + 1) * TCM] for f in range(NFT)]
        emit_postln_resid(of_t, t0, onesch[:])
        mmf_prev = list(mmf_done)
    ve_ffn0 = C["VE"]
    if debug:
        @blk.gpsimd
        def _(e):
            e.wait_ge(sems["VE"], ve_ffn0)
            e.dma_start(out=dbg["d_wq"][:], in_=wq_all[:]).then_inc(sems["GD"], 16)
            e.dma_start(out=dbg["d_hq"][:], in_=hq_sb[:]).then_inc(sems["GD"], 16)
            e.dma_start(out=dbg["d_x2"][:], in_=xfm[:]).then_inc(sems["GD"], 16)
        C["GD"] += 16
    es_f.close()

    es_a = contextlib.ExitStack()

    def sba(name, shape, dt):
        return es_a.enter_context(nc.sbuf_tensor(name, list(shape), dt))

    VT = sba("VT", [128, 16 * 1040], BF16)
    negones = sba("negones", [1, 128], F32)
    bq_sb = sba("bq_sb", [128, NFT], F32)
    bk_sb = sba("bk_sb", [128, NFT], F32)
    bout_sb = sba("bout_sb", [128, NFT], F32)
    bvrow = sba("bvrow", [1, D], BF16)
    Qb = sba("Qb", [128, 8 * L], BF16)
    es_aK = contextlib.ExitStack()
    Kb = es_aK.enter_context(nc.sbuf_tensor("Kb", [128, 8 * L], BF16))
    es_a1 = contextlib.ExitStack()
    wv_sb = es_a1.enter_context(nc.sbuf_tensor("wv_sb", [128, 8 * 1024], BF16))
    xnbq = es_a1.enter_context(nc.sbuf_tensor("xnbq", [128, NFT * TCM], BF16))

    @blk.sync
    def _(e):
        e.wait_ge(sems["VE"], ve_ffn0)
        e.dma_start(out=bq_sb[:], in_=BQ[:]).then_inc(sems["SD"], 16)
        e.dma_start(out=bk_sb[:], in_=BK[:]).then_inc(sems["SD"], 16)
        e.dma_start(out=bout_sb[:], in_=BOUT[:]).then_inc(sems["SD"], 16)
    C["SD"] += 48
    sd_a = C["SD"]
    @blk.gpsimd
    def _(e):
        e.wait_ge(sems["VE"], ve_ffn0)
        e.dma_start(out=bvrow[:], in_=BV[:]).then_inc(sems["GD"], 16)
        for k in range(8):
            e.dma_start(out=wv_sb[:, k * 1024:(k + 1) * 1024], in_=WVR[k]).then_inc(sems["GD"], 16)
    C["GD"] += 144
    gd_wv = C["GD"]
    @blk.vector
    def _(e):
        e.wait_ge(sems["SD"], sd_a)
        e.memset(negones[:], -1.0)
        e.tensor_scalar(bq_sb[:], bq_sb[:], C_QS, None, op0=MUL)
        vt3 = VT[:].rearrange("p (t h c) -> p t h c", h=NH, c=65)
        inc(e.memset(vt3[:, :, :, 64:65], 1.0), "VE")
    ve_abias = C["VE"]

    se_vt_prev = 0
    mmq_done = [0, 0]
    for qc in range(NCM):
        t0 = qc * TCM
        ve_xq = emit_preln(t0, lambda f: xnbq[:, f * TCM:(f + 1) * TCM])
        for tt in range(4):
            t16 = qc * 4 + tt
            @blk.tensor
            def _(e, tt=tt, t16=t16, ve_xq=ve_xq, svp=se_vt_prev):
                if t16 == 0:
                    e.wait_ge(sems["GD"], gd_wv)
                e.wait_ge(sems["VE"], ve_xq)
                if svp:
                    e.wait_ge(sems["SE"], svp)
                for k in range(8):
                    for nh in range(2):
                        e.matmul(ps_mm[:, nh * TCM:(nh + 1) * TCM],
                                 xnbq[:, k * TCM + tt * 128: k * TCM + (tt + 1) * 128],
                                 wv_sb[:, k * 1024 + nh * TCM: k * 1024 + (nh + 1) * TCM],
                                 start=(k == 0), stop=False)
                for nh in range(2):
                    mm = e.matmul(ps_mm[:, nh * TCM:(nh + 1) * TCM], onesb[0:1, :],
                                  bvrow[0:1, nh * TCM:(nh + 1) * TCM], start=False, stop=True)
                inc(mm, "TE")
            te_v = C["TE"]
            @blk.scalar
            def _(e, t16=t16, te_v=te_v):
                e.wait_ge(sems["TE"], te_v)
                if t16 == 0:
                    e.wait_ge(sems["VE"], ve_abias)
                dst = VT[:, t16 * 1040:(t16 + 1) * 1040].rearrange("p (h c) -> p h c", c=65)[:, :, 0:64]
                src_ = ps_mm[:, 0:1024].rearrange("p (h c) -> p h c", c=64)
                inc(e.activation(dst, src_, AF.Identity), "SE")
            se_vt_prev = C["SE"]
        for m in range(16):
            buf = m % 2
            @blk.sync
            def _(e, m=m, buf=buf, qc=qc, svp=se_vt_prev, _mmq_done_buf=mmq_done[buf]):
                if qc == 0 and m < 2:
                    e.wait_ge(sems["SE"], svp)
                    e.wait_ge(sems["GD"], gd_wqkb)
                if _mmq_done_buf:
                    e.wait_ge(sems["TE"], _mmq_done_buf)
                e.dma_start(out=ipw_m[:, buf * 1024:(buf + 1) * 1024], in_=WQKB[m]).then_inc(sems["SD"], 16)
            C["SD"] += 16
            gd_t = C["SD"]
            pshalf = ps_mm[:, buf * TCM:(buf + 1) * TCM]
            @blk.tensor
            def _(e, m=m, buf=buf, gd_t=gd_t, pshalf=pshalf, _cse=C["SE"]):
                e.wait_ge(sems["SD"], gd_t)
                e.wait_ge(sems["SE"], _cse)
                for k in range(8):
                    mm = e.matmul(pshalf, ipw_m[:, buf * 1024 + k * 128: buf * 1024 + (k + 1) * 128],
                                  xnbq[:, k * TCM:(k + 1) * TCM], start=(k == 0), stop=(k == 7))
                inc(mm, "TE")
            mmq_done[buf] = C["TE"]
            @blk.scalar
            def _(e, m=m, qc=qc, buf=buf, pshalf=pshalf, _mmq_done_buf=mmq_done[buf]):
                e.wait_ge(sems["TE"], _mmq_done_buf)
                if m < 8:
                    inc(e.activation(Qb[:, m * L + qc * TCM: m * L + (qc + 1) * TCM], pshalf,
                                     AF.Identity, bias=bq_sb[:, m:m + 1], scale=C_QS), "SE")
                else:
                    inc(e.activation(Kb[:, (m - 8) * L + qc * TCM: (m - 8) * L + (qc + 1) * TCM], pshalf,
                                     AF.Identity, bias=bk_sb[:, m - 8:m - 7], scale=1.0), "SE")
    se_qk = C["SE"]
    es_a1.close()
    es_a2 = contextlib.ExitStack()
    arow = es_a2.enter_context(nc.sbuf_tensor("arow", [96, 1024], F32))
    rdrep = es_a2.enter_context(nc.sbuf_tensor("rdrep", [64, TCM], F32))
    wrow = es_a2.enter_context(nc.sbuf_tensor("wrow", [1, TCM], BF16))
    wsb = es_a2.enter_context(nc.sbuf_tensor("wsb", [128, TCM], BF16))
    tsc = es_a2.enter_context(nc.sbuf_tensor("tsc", [128, TCM], BF16))
    expP = es_a2.enter_context(nc.sbuf_tensor("expP", [128, 16 * TCM], BF16))
    if debug:
        @blk.gpsimd
        def _(e):
            e.wait_ge(sems["SE"], se_qk)
            e.dma_start(out=dbg["d_q"][:], in_=Qb[:]).then_inc(sems["GD"], 16)
            e.dma_start(out=dbg["d_vt"][:], in_=VT[:]).then_inc(sems["GD"], 16)
        C["GD"] += 32

    @blk.vector
    def _(e):
        e.wait_ge(sems["SE"], se_qk)
        e.memset(arow[0:1, 0:TCM], 1.0)
        inc(e.memset(arow[64:65, 0:TCM], 1.0), "VE")

    se_exp_slot = [0, 0]
    ve_norm_par = [0, 0]
    ve_t_last = 0
    ve_sd1 = 0
    prev = None
    it_list = [(h, qv) for h in range(NH) for qv in range(4)]
    for it in range(65):
        cur = it_list[it] if it < 64 else None
        if cur is not None:
            h, qv = cur
            tile = h // 2
            po = (h % 2) * 64
            qoff = tile * L + qv * TCM
        for kc in range(16):
            sl = kc % 2
            if cur is not None:
                @blk.tensor
                def _(e, kc=kc, sl=sl, po=po, tile=tile, qoff=qoff,
                      ses=se_exp_slot[sl], first=(it == 0 and kc < 2)):
                    if first:
                        e.wait_ge(sems["SE"], se_qk)
                    if ses:
                        e.wait_ge(sems["SE"], ses)
                    inc(e.matmul(ps_mm[:, sl * TCM:(sl + 1) * TCM],
                                 Kb[po:po + 64, tile * L + kc * 128: tile * L + (kc + 1) * 128],
                                 Qb[po:po + 64, qoff:qoff + TCM],
                                 start=True, stop=True), "TE")
            if prev is not None:
                ph = prev["h"]
                osl = prev["osl"]
                @blk.tensor
                def _(e, kc=kc, ph=ph, osl=osl, vsq=prev["ve_sq"][kc],
                      vn=(ve_norm_par[osl] if kc == 0 else 0)):
                    e.wait_ge(sems["VE"], vsq)
                    if vn:
                        e.wait_ge(sems["VE"], vn)
                    inc(e.matmul(ps_o2[0:65, osl * TCM:(osl + 1) * TCM],
                                 VT[:, kc * 1040 + ph * 65: kc * 1040 + (ph + 1) * 65],
                                 expP[:, kc * TCM:(kc + 1) * TCM],
                                 start=(kc == 0), stop=(kc == 15)), "TE")
            if cur is not None:
                @blk.scalar
                def _(e, kc=kc, sl=sl, te_w=C["TE"]):
                    e.wait_ge(sems["TE"], te_w)
                    inc(e.activation(expP[:, kc * TCM:(kc + 1) * TCM],
                                     ps_mm[:, sl * TCM:(sl + 1) * TCM],
                                     AF.Exp, bias=C_M16, scale=C_SPARE), "SE")
                se_exp_slot[sl] = C["SE"]
                @blk.tensor
                def _(e, kc=kc, _se=C["SE"], _vs=(ve_sd1 if kc == 0 else 0)):
                    e.wait_ge(sems["SE"], _se)
                    if _vs:
                        e.wait_ge(sems["VE"], _vs)
                    inc(e.matmul(ps_y[0:1, :], onescb[:], expP[:, kc * TCM:(kc + 1) * TCM],
                                 start=(kc == 0), stop=(kc == 15)), "TE")
        if prev is not None:
            @blk.vector
            def _(e, osl=prev["osl"], _te=C["TE"]):
                e.wait_ge(sems["TE"], _te)
                inc(e.tensor_scalar(arow[64:65, 0:TCM], ps_o2[64:65, osl * TCM:(osl + 1) * TCM],
                                    C_ONE[64:65, :], None, op0=MUL), "VE")
            ve_sd2 = C["VE"]
            @blk.scalar
            def _(e, _ve=ve_sd2):
                e.wait_ge(sems["VE"], _ve)
                e.activation(arow[64:65, TCM:2 * TCM], arow[64:65, 0:TCM], AF.Ln)
                inc(e.activation(arow[64:65, TCM:2 * TCM], arow[64:65, TCM:2 * TCM], AF.Exp,
                                 scale=C_NEG1[0:1, :]), "SE")
            se_rd = C["SE"]
        if cur is not None:
            @blk.vector
            def _(e, _te=C["TE"]):
                e.wait_ge(sems["TE"], _te)
                inc(e.tensor_scalar(arow[0:1, 0:TCM], ps_y[0:1, :],
                                    C_ONE[0:1, :], None, op0=MUL), "VE")
            ve_sd1 = C["VE"]
            @blk.scalar
            def _(e, _ve=ve_sd1):
                e.wait_ge(sems["VE"], _ve)
                e.activation(arow[0:1, TCM:2 * TCM], arow[0:1, 0:TCM], AF.Ln)
                inc(e.activation(wrow[0:1, :], arow[0:1, TCM:2 * TCM], AF.Exp,
                                 scale=C_NEG1[0:1, :]), "SE")
            se_w = C["SE"]
        if prev is not None:
            @blk.tensor
            def _(e, _se=se_rd):
                e.wait_ge(sems["SE"], _se)
                inc(e.matmul(ps_dt[0:64, :], onesA[64:65, 0:64], arow[64:65, TCM:2 * TCM],
                             start=True, stop=True), "TE")
            te_rdb = C["TE"]
        if cur is not None:
            @blk.tensor
            def _(e, _se=se_w):
                e.wait_ge(sems["SE"], _se)
                inc(e.matmul(ps_bc[:, :], onesb[0:1, :], wrow[0:1, :],
                             start=True, stop=True), "TE")
            te_wb = C["TE"]
        if prev is not None:
            @blk.vector
            def _(e, _te=te_rdb):
                e.wait_ge(sems["TE"], _te)
                inc(e.tensor_scalar(rdrep[:, :], ps_dt[0:64, :], C_ONE[0:64, :],
                                    None, op0=MUL), "VE")
        if cur is not None:
            @blk.scalar
            def _(e, _te=te_wb, _vt=ve_t_last):
                e.wait_ge(sems["TE"], _te)
                if _vt:
                    e.wait_ge(sems["VE"], _vt)
                inc(e.activation(wsb[:, :], ps_bc[:, :], AF.Identity), "SE")
            se_wc = C["SE"]
        if prev is not None:
            ph = prev["h"]
            osl = prev["osl"]
            ppo = (ph % 2) * 64
            pqoff = (ph // 2) * L + prev["qv"] * TCM
            @blk.vector
            def _(e, osl=osl, ppo=ppo, pqoff=pqoff):
                inc(e.tensor_tensor(Qb[ppo:ppo + 64, pqoff:pqoff + TCM],
                                    ps_o2[0:64, osl * TCM:(osl + 1) * TCM],
                                    rdrep[:, :], op=MUL), "VE")
            ve_norm_par[osl] = C["VE"]
        if cur is not None:
            ve_sq = [0] * 16
            for kc in range(16):
                @blk.vector
                def _(e, kc=kc, _se=(se_wc if kc == 0 else 0)):
                    if _se:
                        e.wait_ge(sems["SE"], _se)
                    e.tensor_tensor(tsc[:, :], expP[:, kc * TCM:(kc + 1) * TCM], wsb[:, :], op=MUL)
                    inc(e.tensor_tensor(expP[:, kc * TCM:(kc + 1) * TCM], tsc[:, :], tsc[:, :],
                                        op=MUL), "VE")
                ve_sq[kc] = C["VE"]
            ve_t_last = ve_sq[15]
            prev = {"h": h, "qv": qv, "osl": it % 2, "ve_sq": ve_sq}
    ve_attn_done = C["VE"]
    if debug:
        @blk.gpsimd
        def _(e):
            e.wait_ge(sems["VE"], ve_attn_done)
            e.dma_start(out=dbg["d_ao"][:], in_=Qb[:]).then_inc(sems["GD"], 16)
            e.dma_start(out=dbg["d_den"][:], in_=arow[:]).then_inc(sems["GD"], 16)
        C["GD"] += 32

    es_a2.close()
    es_aK.close()
    wout_res = sba("wout_res", [128, 8 * 1024], BF16)
    ofa = sba("ofa", [128, NFT * TCM], F32)
    @blk.sync
    def _(e):
        e.wait_ge(sems["VE"], ve_attn_done)
        e.wait_ge(sems["GD"], gd_woutb)
        for m in range(8):
            e.dma_start(out=wout_res[:, m * 1024:(m + 1) * 1024], in_=WOUTB[m]).then_inc(sems["SD"], 16)
    C["SD"] += 128
    gd_wo = C["SD"]
    for c in range(NCM):
        t0 = c * TCM
        for m in range(NFT):
            pshalf = ps_mm[:, (m % 2) * TCM:((m % 2) + 1) * TCM]
            @blk.tensor
            def _(e, m=m, c=c, pshalf=pshalf, _cse=C["SE"]):
                if m == 0 and c == 0:
                    e.wait_ge(sems["SD"], gd_wo)
                    e.wait_ge(sems["VE"], ve_attn_done)
                e.wait_ge(sems["SE"], _cse)
                for k in range(8):
                    mm = e.matmul(pshalf, wout_res[:, m * 1024 + k * 128: m * 1024 + (k + 1) * 128],
                                  Qb[:, k * L + t0: k * L + t0 + TCM], start=(k == 0), stop=(k == 7))
                inc(mm, "TE")
            te_wo = C["TE"]
            @blk.scalar
            def _(e, m=m, te_wo=te_wo, pshalf=pshalf):
                e.wait_ge(sems["TE"], te_wo)
                inc(e.activation(ofa[:, m * TCM:(m + 1) * TCM], pshalf, AF.Identity,
                                 bias=bout_sb[:, m:m + 1], scale=1.0), "SE")
        of_t = [ofa[:, f * TCM:(f + 1) * TCM] for f in range(NFT)]
        if debug and c == 0:
            @blk.gpsimd
            def _(e, _cse=C["SE"]):
                e.wait_ge(sems["SE"], _cse)
                e.dma_start(out=dbg["d_ofa"][:], in_=ofa[:]).then_inc(sems["GD"], 16)
            C["GD"] += 16
        emit_postln_resid(of_t, t0, onescf[:])
    ve_attn = C["VE"]
    if debug:
        @blk.gpsimd
        def _(e):
            e.wait_ge(sems["VE"], ve_attn)
            e.dma_start(out=dbg["d_x3"][:], in_=xfm[:]).then_inc(sems["GD"], 16)
        C["GD"] += 16
    es_a.close()

    es_l = contextlib.ExitStack()

    def sbl(name, shape, dt):
        return es_l.enter_context(nc.sbuf_tensor(name, list(shape), dt))

    xnbL = sbl("xnbL", [128, NFT * L], BF16)
    l1res = sbl("l1res", [128, 8 * 1024], BF16)
    ofl = sbl("ofl", [128, NFT * TCM], F32)

    @blk.sync
    def _(e):
        e.wait_ge(sems["VE"], ve_attn)
        e.wait_ge(sems["GD"], gd_woutb)
        for m in range(8):
            e.dma_start(out=l1res[:, m * 1024:(m + 1) * 1024], in_=L1B[m]).then_inc(sems["SD"], 16)
    C["SD"] += 128
    gd_l1 = C["SD"]
    @blk.scalar
    def _(e):
        e.wait_ge(sems["SD"], gd_l1)
        inc(e.activation(l1res[:], l1res[:], AF.Sign), "SE")
    se_sign = C["SE"]

    for c in range(NCM):
        emit_preln(c * TCM,
                   lambda f, c=c: xnbL[:, f * L + c * TCM: f * L + (c + 1) * TCM])
    ve_xnL = C["VE"]
    for c in range(NCM):
        t0 = c * TCM
        for m in range(NFT):
            pshalf = ps_mm[:, (m % 2) * TCM:((m % 2) + 1) * TCM]
            @blk.tensor
            def _(e, m=m, c=c, pshalf=pshalf, _cse=C["SE"]):
                if m == 0 and c == 0:
                    e.wait_ge(sems["SE"], se_sign)
                    e.wait_ge(sems["VE"], ve_xnL)
                e.wait_ge(sems["SE"], _cse)
                for k in range(8):
                    mm = e.matmul(pshalf, l1res[:, m * 1024 + k * 128: m * 1024 + (k + 1) * 128],
                                  xnbL[:, k * L + t0: k * L + t0 + TCM], start=(k == 0), stop=(k == 7))
                inc(mm, "TE")
            te_l1 = C["TE"]
            @blk.scalar
            def _(e, m=m, te_l1=te_l1, pshalf=pshalf):
                e.wait_ge(sems["TE"], te_l1)
                inc(e.activation(ofl[:, m * TCM:(m + 1) * TCM], pshalf, AF.Identity), "SE")
        of_t = [ofl[:, f * TCM:(f + 1) * TCM] for f in range(NFT)]
        emit_postln_resid(of_t, t0, onescf[:])
    ve_final = C["VE"]
    es_l.close()

    xio_sd = [0, 0]
    for i in range(16):
        bi = i % 2
        for f in range(NFT):
            q = (i * NFT + f) % 7
            @blk.tensor
            def _(e, i=i, f=f, q=q, _trv_se_q=trv_se[q]):
                if i == 0 and f == 0:
                    e.wait_ge(sems["VE"], ve_final)
                if _trv_se_q:
                    e.wait_ge(sems["SE"], _trv_se_q)
                inc(e.transpose(trv[q][:], xfm[:, f * L + i * 128:f * L + (i + 1) * 128], idf_sb[:]), "TE")
            te_n = C["TE"]
            @blk.scalar
            def _(e, i=i, f=f, q=q, bi=bi, te_n=te_n, _xio_sd_bi=xio_sd[bi]):
                e.wait_ge(sems["TE"], te_n)
                if f == 0 and _xio_sd_bi:
                    e.wait_ge(sems["SD"], _xio_sd_bi)
                inc(e.activation(xio_v[bi][:, f * 128:(f + 1) * 128], trv[q][:], AF.Identity), "SE")
            trv_se[q] = C["SE"]
        se_t = C["SE"]
        @blk.sync
        def _(e, i=i, bi=bi, se_t=se_t):
            e.wait_ge(sems["SE"], se_t)
            e.dma_start(out=OUTt[i], in_=xio_v[bi][:]).then_inc(sems["SD"], 16)
        C["SD"] += 16
        xio_sd[0] = xio_sd[1] = C["SD"]
    @blk.sync
    def _(e, _csd=C["SD"]):
        e.wait_ge(sems["SD"], _csd)

    _replay[0] = True
    for _key in ("sync", "gpsimd", "tensor", "vector", "scalar"):
        _fns = _PEND[_key]
        for _i0 in range(0, len(_fns), 16):
            _chunk = _fns[_i0:_i0 + 16]
            getattr(blk_real, _key)(lambda e, _c=_chunk: [f(e) for f in _c])
    _replay[0] = False

    es.close()
    return nc


_CACHE = {}


def _prep(inputs):
    f32 = np.float32

    def c(a):
        return np.ascontiguousarray(np.asarray(a), dtype=f32)

    def pmaj(a, n):
        s = np.asarray(a).shape[1]
        return c(np.asarray(a).reshape(n, 128, s).transpose(1, 0, 2).reshape(128, n * s))

    def vec_pm(a, n):
        return c(np.asarray(a).reshape(n, 128).T)

    ip = c(inputs["L0_attn_in_proj"])
    ipw = c(ip.reshape(8, 128, 32, 128).transpose(2, 1, 0, 3).reshape(32, 128, 1024))
    op = c(inputs["L0_attn_out_proj"])
    opw = c(op.reshape(16, 128, 8, 128).transpose(2, 1, 0, 3).reshape(8, 128, 2048))
    shared = {
        "ipw": ipw, "opw": opw,
        "xpw": pmaj(inputs["L0_attn_x_proj"], NDG),
        "dtw": c(inputs["L0_attn_dt_w"]),
        "convw": pmaj(inputs["L0_attn_conv_w"], NDG),
        "convb": vec_pm(inputs["L0_attn_conv_b"], NDG),
        "dtb": vec_pm(inputs["L0_attn_dt_b"], NDG),
        "alog": pmaj(inputs["L0_attn_A_log"], NDG),
        "dd": vec_pm(inputs["L0_attn_D"], NDG),
        "rmsw": vec_pm(inputs["L0_attn_rms_w"], NFT),
        "idf": np.eye(128, dtype=f32),
        "idb": np.eye(128, dtype=f32).astype(ml_dtypes.bfloat16),
        "indv": np.concatenate([np.eye(96, dtype=f32)[:, 64:80],
                                np.eye(96, dtype=f32)[:, 80:96]], axis=1).astype(ml_dtypes.bfloat16),
    }
    w1 = c(inputs["L0_ffn_w1"]).T
    shared["w1t"] = c(w1.reshape(8, 128, 32, 128).transpose(2, 1, 0, 3).reshape(32, 128, 1024))
    shared["b1"] = vec_pm(inputs["L0_ffn_b1"], 32)
    w2 = c(inputs["L0_ffn_w2"]).T
    shared["w2t"] = c(w2.reshape(32, 128, 8, 128).transpose(2, 1, 0, 3).reshape(8, 128, 4096))
    shared["b2"] = vec_pm(inputs["L0_ffn_b2"], NFT)
    wqkv = c(inputs["L1_attn_W_qkv"])
    wq_r = wqkv.reshape(NH, 3, DH, D).transpose(1, 0, 2, 3).reshape(3 * D, D)
    wqkT = c(wq_r[0:2 * D].T)
    shared["wqkt"] = c(wqkT.reshape(8, 128, 16, 128).transpose(2, 1, 0, 3).reshape(16, 128, 1024))
    shared["wvr"] = c(wq_r[2 * D:3 * D].T.reshape(8, 128, 1024))
    bqkv = c(inputs["L1_attn_b_qkv"]).reshape(NH, 3, DH).transpose(1, 0, 2).reshape(3, D)
    shared["bq"] = vec_pm(bqkv[0], NFT)
    shared["bk"] = vec_pm(bqkv[1], NFT)
    shared["bv"] = c(bqkv[2]).reshape(1, D)
    wout = c(inputs["L1_attn_W_out"]).T
    shared["woutt"] = c(wout.reshape(8, 128, 8, 128).transpose(2, 1, 0, 3).reshape(8, 128, 1024))
    shared["bout"] = vec_pm(inputs["L1_attn_b_out"], NFT)
    l1w = c(inputs["L1_ffn_weight"]).T
    shared["l1wt"] = c(l1w.reshape(8, 128, 8, 128).transpose(2, 1, 0, 3).reshape(8, 128, 1024))
    return shared


def kernel(**inputs):
    shared = _prep(inputs)
    x = np.ascontiguousarray(np.asarray(inputs["L0_attn_x"]), dtype=np.float32)
    if "nc" not in _CACHE:
        _CACHE["nc"] = build(debug=DEBUG)
    nc = _CACHE["nc"]
    in_maps = []
    for b in range(8):
        m = dict(shared)
        m["x"] = np.ascontiguousarray(x[b])
        in_maps.append(m)
    res = run_bass_kernel_spmd(nc, in_maps, core_ids=list(range(8)))
    out = np.stack([res.results[b]["out"] for b in range(8)], axis=0)
    return out
